# version 13
# baseline (speedup 1.0000x reference)
import math
from contextlib import ExitStack
import numpy as np
import concourse.bass as bass
import concourse.mybir as mybir
from concourse.bass_utils import run_bass_kernel_spmd


F32 = mybir.dt.float32
BF16 = mybir.dt.bfloat16
AF = mybir.ActivationFunctionType
ALU = mybir.AluOpType
AX = mybir.AxisListType

ENGINES = ("sync", "tensor", "vector", "scalar", "gpsimd")


class Sem:

    def __init__(self, nc, es, name):
        self.h = es.enter_context(nc.semaphore(name))
        self.name = name


class Emitter:
    def __init__(self, nc, me, eng, sems):
        self.nc = nc
        self.me = me
        self.eng = eng
        self.n = {s.name: 0 for s in sems}

    def cnt(self, sem):
        return self.n[sem.name]

    def bump(self, sem, amt):
        self.n[sem.name] += amt
        return self.n[sem.name]

    def do(self, engine, fn, wait=(), inc=None, inc_amt=1):
        if self.me == engine:
            for s, v in wait:
                if v > 0:
                    self.eng.wait_ge(s.h, v)
            ins = fn(self.eng)
            if inc is not None:
                ins.then_inc(inc.h, inc_amt)
        if inc is not None:
            self.bump(inc, inc_amt)

    def wait(self, engine, sem, value):
        if self.me == engine and value > 0:
            self.eng.wait_ge(sem.h, value)

    def drain(self, engine):
        if self.me == engine:
            self.eng.drain()


def ceil_div(a, b):
    return (a + b - 1) // b


def alibi_slopes(n):
    def pow2(n):
        start = 2 ** (-(2 ** -(math.log2(n) - 3)))
        return [start * start ** i for i in range(n)]
    if math.log2(n).is_integer():
        return pow2(n)
    c = 2 ** math.floor(math.log2(n))
    return pow2(c) + alibi_slopes(2 * c)[0::2][: n - c]


def host_consts(H=16, N=2048, HD=64):
    import ml_dtypes
    slopes = np.asarray(alibi_slopes(H), dtype=np.float64)
    ali = np.zeros((128, H * 40), dtype=np.float32)
    p = np.arange(128)
    off = [0, 4, 12, 24]
    for h in range(H):
        for qc in range(4):
            for kt in range(4 * (qc + 1)):
                ali[:, h * 40 + off[qc] + kt] = (
                    slopes[h] * (kt * 128 + p - qc * 512) / 8.0
                ).astype(np.float32)
    msk = np.zeros((128, 4 * 512), dtype=np.float32)
    f = np.arange(512)
    for j in range(4):
        m = (f[None, :] < (p[:, None] + 128 * j)).astype(np.float32) * (-1e30)
        msk[:, j * 512:(j + 1) * 512] = m
    ident = np.eye(128).astype(ml_dtypes.bfloat16)
    ones = np.ones((128,), dtype=np.float32)
    return dict(c_alibi=ali, c_mask=msk.astype(ml_dtypes.bfloat16),
                c_ident=np.asarray(ident), c_ones=ones)



EPS = 1e-5

ATTN_SEM_NAMES = [
    "s_xin0", "s_xin1", "s_v3", "s_ln", "s_v4", "s_tr", "s_ev",
    "s_w", "s_a2t", "s_a2v",
    "s_sc", "s_msk", "s_exp", "s_avm", "s_rcp", "s_bc", "s_bcc", "s_nrm",
    "s_const", "s_wot", "s_x2v", "s_x2o0", "s_x2o1", "s_bob", "s_bobv", "s_eps",
    "s_xbf", "s_sq", "s_ve", "s_sd"]


def attn_sems(nc, es):
    return {n: Sem(nc, es, n) for n in ATTN_SEM_NAMES}


def emit_attention(E, S, T, P, extra=None):
    emit_a0_consts(E, S, T)
    emit_a1_ln_transpose(E, S, T, P)
    if extra is not None:
        extra(E)
    emit_a2_qkv(E, S, T, P)
    emit_a3_scores(E, S, T, P)
    emit_a4_out(E, S, T, P)


def emit_a0_consts(E, S, T):
    s_const = S["s_const"]
    E.do("sync", lambda e: e.dma_start(T["alibi"][:, :], T["c_alibi"].ap()),
         inc=s_const, inc_amt=16)
    E.do("sync", lambda e: e.dma_start(T["ident"][:, :], T["c_ident"].ap()),
         inc=s_const, inc_amt=16)
    E.do("sync", lambda e: e.dma_start(T["mask"][:, :], T["c_mask"].ap()),
         inc=s_const, inc_amt=16)
    E.do("sync", lambda e: e.dma_start(
        T["ones"][:, :], T["c_ones"].ap().rearrange("(o f) -> o f", o=1)),
        inc=s_const, inc_amt=16)
    E.do("sync", lambda e: e.dma_start(
        T["bo_row"][:, :], T["d_bo"].ap().rearrange("(o f) -> o f", o=1)),
        inc=s_const, inc_amt=16)
    E.do("gpsimd", lambda e: e.dma_start(
        T["W_sb"][:, :].rearrange("p (dc c) -> p dc c", dc=8),
        T["d_Wk"].ap().rearrange("(dc p) c -> p dc c", p=128)),
        inc=S["s_w"], inc_amt=16)


def emit_a1_ln_transpose(E, S, T, P):
    y1 = T["y1_tile"]
    E.do("vector", lambda e: e.memset(T["eps_t"][:, :], EPS), inc=S["s_eps"])
    for q in range(4):
        E.do("gpsimd", lambda e, q=q: e.dma_start(
            T["xbf"][:, q * 4096:(q + 1) * 4096].rearrange(
                "p (st c) -> p st c", st=4),
            T["d_x"].ap()[q * 512:(q + 1) * 512, :].rearrange(
                "(st p) c -> p st c", p=128)),
            wait=([(S["s_xbf"], 16 * q)] if q > 0 else []),
            inc=S["s_xbf"], inc_amt=16)
    for q in range(4):
        qs = slice(q * 4, (q + 1) * 4)
        for st in range(q * 4, q * 4 + 4):
            E.do("vector", lambda e, st=st: e.reduce_sum(
                T["st_m16"][:, st:st + 1],
                T["xbf"][:, st * 1024:(st + 1) * 1024], axis=AX.X),
                wait=([(S["s_xbf"], 16 * (q + 1))] if st % 4 == 0 else []))
            E.do("scalar", lambda e, st=st: e.activation(
                T["sq_scratch"][st % 2][:, :],
                T["xbf"][:, st * 1024:(st + 1) * 1024],
                AF.Square, accum_out=T["st_q16"][:, st:st + 1]),
                wait=([(S["s_xbf"], 16 * (q + 1))] if st % 4 == 0 else []),
                inc=(S["s_sq"] if st % 4 == 3 else None))
            E.drain("scalar")
        E.drain("vector")
        E.do("vector", lambda e, qs=qs: e.tensor_scalar_mul(
            T["st_m16"][:, qs], T["st_m16"][:, qs], 1.0 / 1024))
        E.drain("vector")
        E.do("vector", lambda e, qs=qs: e.tensor_scalar_mul(
            T["st_n16"][:, qs], T["st_m16"][:, qs], -1.0))
        E.drain("vector")
        E.do("vector", lambda e, qs=qs: e.tensor_tensor(
            T["st_v16"][:, qs], T["st_n16"][:, qs], T["st_m16"][:, qs],
            ALU.mult))
        E.drain("vector")
        E.do("vector", lambda e, qs=qs: e.scalar_tensor_tensor(
            T["st_v16"][:, qs], T["st_q16"][:, qs], 1.0 / 1024,
            T["st_v16"][:, qs], ALU.mult, ALU.add),
            wait=[(S["s_sq"], q + 1)], inc=S["s_ve"])
        E.do("scalar", lambda e, qs=qs: e.activation(
            T["st_d16"][:, qs], T["st_v16"][:, qs], AF.Sqrt,
            bias=T["eps_t"][:, :]),
            wait=[(S["s_ve"], q + 1), (S["s_eps"], 1)], inc=S["s_sd"])
        E.do("vector", lambda e, qs=qs: e.reciprocal(
            T["st_r16"][:, qs], T["st_d16"][:, qs]),
            wait=[(S["s_sd"], q + 1)])
        E.drain("vector")
        for st in range(q * 4, q * 4 + 4):
            par = st % 2
            E.do("vector", lambda e, st=st, par=par: e.tensor_scalar(
                y1[par][:, :], T["xbf"][:, st * 1024:(st + 1) * 1024],
                T["st_n16"][:, st:st + 1], T["st_r16"][:, st:st + 1],
                ALU.add, ALU.mult),
                wait=[(S["s_tr"], st - 1)], inc=S["s_v4"])
            for dc in range(8):
                E.do("tensor", lambda e, dc=dc, par=par: e.transpose(
                    P["PT"][par][:, dc * 128:(dc + 1) * 128],
                    y1[par][:, dc * 128:(dc + 1) * 128], T["ident"][:, :]),
                    wait=([(S["s_v4"], st + 1), (S["s_ev"], st - 1),
                           (S["s_const"], 80)] if dc == 0 else []),
                    inc=(S["s_tr"] if dc == 7 else None))
            E.do("vector", lambda e, st=st, par=par: e.tensor_copy(
                T["y1T"][:, :].rearrange("p (dc s) -> p dc s", dc=8)[
                    :, :, st * 128:(st + 1) * 128],
                P["PT"][par][:, :].rearrange("p (dc c) -> p dc c", dc=8)),
                wait=[(S["s_tr"], st + 1)], inc=S["s_ev"])


def _qkv_groups():
    gs = []
    for hp in range(8):
        for c in range(4):
            gs.append(("kt", hp, c))
    for st in range(16):
        for half in range(2):
            gs.append(("vn", st, half))
    for hp in range(8):
        for c in range(4):
            gs.append(("qt", hp, c))
    return gs


def emit_a2_qkv(E, S, T, P):
    W, y1T = T["W_sb"], T["y1T"]
    groups = _qkv_groups()
    for gi, g in enumerate(groups):
        kind = g[0]
        par = gi % 2
        psum = P["P01"][par]
        if kind == "vn" and g[1] == 0 and g[2] == 0:
            E.do("gpsimd", lambda e: e.dma_start(
                T["W_sb"][:, :].rearrange("p (dc c) -> p dc c", dc=8),
                T["d_Wv"].ap().rearrange("(dc p) c -> p dc c", p=128)),
                wait=[(S["s_a2t"], 32)], inc=S["s_w"], inc_amt=16)
        if kind == "qt" and g[1] == 0 and g[2] == 0:
            E.do("gpsimd", lambda e: e.dma_start(
                T["W_sb"][:, :].rearrange("p (dc c) -> p dc c", dc=8),
                T["d_Wq"].ap().rearrange("(dc p) c -> p dc c", p=128)),
                wait=[(S["s_a2t"], 64)], inc=S["s_w"], inc_amt=16)
        wload = 16 if kind == "kt" else (32 if kind == "vn" else 48)
        for dc in range(8):
            if kind in ("kt", "qt"):
                _, hp, c = g
                lhs = W[:, dc * 1024 + hp * 128: dc * 1024 + (hp + 1) * 128]
                rhs = y1T[:, dc * 2048 + c * 512: dc * 2048 + (c + 1) * 512]
            else:
                _, st, half = g
                lhs = y1T[:, dc * 2048 + st * 128: dc * 2048 + st * 128 + 128]
                rhs = W[:, dc * 1024 + half * 512: dc * 1024 + half * 512 + 512]
            waits = []
            if dc == 0:
                waits = [(S["s_w"], wload), (S["s_a2v"], gi - 1)]
                if gi == 0:
                    waits.append((S["s_ev"], 16))
            E.do("tensor", lambda e, o=psum, l=lhs, r=rhs, dc=dc: e.matmul(
                o[:, :], l, r, start=(dc == 0), stop=(dc == 7)),
                wait=waits, inc=(S["s_a2t"] if dc == 7 else None))
        if kind in ("kt", "qt"):
            _, hp, c = g
            dst_t = T["KT"] if kind == "kt" else T["QT"]
            dst = dst_t[:, hp * 2048 + c * 512: hp * 2048 + (c + 1) * 512]
            E.do("vector", lambda e, d=dst, o=psum: e.tensor_copy(d, o[:, :]),
                 wait=[(S["s_a2t"], gi + 1)], inc=S["s_a2v"])
        else:
            _, st, half = g
            base = st * 1040 + half * 520
            dst = T["VN"][:, base: base + 520].rearrange(
                "p (h j) -> p h j", h=8)[:, :, 0:64]
            E.do("vector", lambda e, d=dst, o=psum: e.tensor_copy(
                d, o[:, :].rearrange("p (h j) -> p h j", h=8)),
                wait=[(S["s_a2t"], gi + 1)], inc=S["s_a2v"])
    E.do("vector", lambda e: e.memset(
        T["VN"][:, :].rearrange("p (g j) -> p g j", j=65)[:, :, 64:65], 1.0),
        inc=S["s_a2v"])


def _a3_iters():
    return [(h, qc) for h in range(16) for qc in range(4)]


def emit_a3_scores(E, S, T, P):
    iters = _a3_iters()
    KT, QT, VN = T["KT"], T["QT"], T["VN"]
    tiles = []
    sc_count, msk_count, exp_count, avm_count = {}, {}, {}, {}
    n_sc = n_msk = n_exp = n_avm = 0
    for i, (h, qc) in enumerate(iters):
        for j in range(4 * (qc + 1)):
            t = (i, j)
            tiles.append(t)
            n_sc += 1
            sc_count[t] = n_sc
            if j >= 4 * qc:
                n_msk += 1
            msk_count[t] = n_msk
            n_exp += 1
            exp_count[t] = n_exp
            n_avm += 1
            avm_count[t] = n_avm
    tile_idx = {t: n for n, t in enumerate(tiles)}

    def prev_tile(t, back):
        n = tile_idx[t] - back
        return tiles[n] if n >= 0 else None

    def attnv(i, h, qc, j, K):
        t = (i, j)
        waits = [(S["s_exp"], exp_count[t])]
        if j == 0:
            waits.append((S["s_nrm"], i - 1))
        E.do("tensor", lambda e, t=t, j=j, i=i, h=h: e.matmul(
            P["PC"][i % 2][:, :],
            VN[:, j * 1040 + h * 65: j * 1040 + (h + 1) * 65],
            T["expb"][tile_idx[t] % 3][:, :],
            start=(j == 0), stop=(j == K - 1)),
            wait=waits, inc=S["s_avm"])

    for i, (h, qc) in enumerate(iters):
        po = 64 * (h % 2)
        hp = h // 2
        K = 4 * (qc + 1)
        for j in range(K):
            t = (i, j)
            pt = prev_tile(t, 3)
            waits = [(S["s_exp"], exp_count[pt] if pt else 0)]
            if i == 0 and j == 0:
                waits.append((S["s_a2v"], 97))
                waits.append((S["s_const"], 80))
            lhs = KT[po:po + 64, hp * 2048 + j * 128: hp * 2048 + (j + 1) * 128]
            rhs = QT[po:po + 64, hp * 2048 + qc * 512: hp * 2048 + (qc + 1) * 512]
            E.do("tensor", lambda e, l=lhs, r=rhs, t=t: e.matmul(
                P["P01"][tile_idx[t] % 3][:, :], l, r, start=True, stop=True),
                wait=waits, inc=S["s_sc"])
            if j == 1 and i >= 1:
                E.do("tensor", lambda e, i=i: e.matmul(
                    P["PB"][0][:, :], T["ones"][0:1, 0:64],
                    T["rd"][(i - 1) % 2][:, :], start=True, stop=True),
                    wait=[(S["s_rcp"], i), (S["s_bcc"], i - 1)], inc=S["s_bc"])
            if j >= 2:
                attnv(i, h, qc, j - 2, K)
        attnv(i, h, qc, K - 2, K)
        attnv(i, h, qc, K - 1, K)

        if i >= 1:
            _normalize(E, S, T, P, i - 1)
        for j in range(4 * qc, K):
            jj = j - 4 * qc
            E.do("vector", lambda e, j=j, jj=jj, i=i, t=(i, j): e.tensor_tensor(
                P["P01"][tile_idx[t] % 3][:, :], P["P01"][tile_idx[t] % 3][:, :],
                T["mask"][:, jj * 512:(jj + 1) * 512], ALU.add),
                wait=[(S["s_sc"], sc_count[(i, j)])], inc=S["s_msk"])
        E.do("vector", lambda e, i=i: e.reciprocal(
            T["rd"][i % 2][:, :], P["PC"][i % 2][64:65, :]),
            wait=[(S["s_avm"], avm_count[(i, K - 1)]), (S["s_bc"], i - 1)],
            inc=S["s_rcp"])

        if i >= 1:
            E.do("scalar", lambda e, i=i: e.activation(
                T["bc_sb"][(i - 1) % 2][:, :], P["PB"][0][:, :],
                AF.Identity),
                wait=[(S["s_bc"], i), (S["s_nrm"], i - 2)], inc=S["s_bcc"])
        for j in range(K):
            t = (i, j)
            idx = h * 40 + [0, 4, 12, 24][qc] + j
            diag = j >= 4 * qc
            w = [(S["s_msk"], msk_count[t])] if diag else [(S["s_sc"], sc_count[t])]
            pt = prev_tile(t, 3)
            if pt:
                w.append((S["s_avm"], avm_count[pt]))
            E.do("scalar", lambda e, j=j, idx=idx, t=t: e.activation(
                T["expb"][tile_idx[t] % 3][:, :], P["P01"][tile_idx[t] % 3][:, :],
                AF.Exp, bias=T["alibi"][:, idx:idx + 1], scale=0.125),
                wait=w, inc=S["s_exp"])
    i_last = len(iters) - 1
    E.do("tensor", lambda e: e.matmul(
        P["PB"][0][:, :], T["ones"][0:1, 0:64],
        T["rd"][i_last % 2][:, :], start=True, stop=True),
        wait=[(S["s_rcp"], i_last + 1), (S["s_bcc"], i_last)], inc=S["s_bc"])
    E.do("scalar", lambda e: e.activation(
        T["bc_sb"][i_last % 2][:, :], P["PB"][0][:, :], AF.Identity),
        wait=[(S["s_bc"], i_last + 1), (S["s_nrm"], i_last - 1)], inc=S["s_bcc"])
    _normalize(E, S, T, P, i_last)


def _normalize(E, S, T, P, i):
    h, qc = _a3_iters()[i]
    po = 64 * (h % 2)
    hp = h // 2
    dst = T["ctxT"][po:po + 64, hp * 2048 + qc * 512: hp * 2048 + (qc + 1) * 512]
    E.do("vector", lambda e, d=dst, i=i: e.tensor_tensor(
        d, P["PC"][i % 2][0:64, :], T["bc_sb"][i % 2][:, :], ALU.mult),
        wait=[(S["s_bcc"], i + 1)], inc=S["s_nrm"])


def emit_a4_out(E, S, T, P):
    n_iters = len(_a3_iters())
    exp_total = E.cnt(S["s_exp"])
    msk_total = E.cnt(S["s_msk"])
    E.do("gpsimd", lambda e: e.dma_start(
        T["W_sb"][:, :].rearrange("p (dc c) -> p dc c", dc=8),
        T["d_Wo"].ap().rearrange("(dc p) c -> p dc c", p=128)),
        wait=[(S["s_a2t"], 96)], inc=S["s_w"], inc_amt=16)
    for dq in range(2):
        E.do("tensor", lambda e, dq=dq: e.matmul(
            P["P01"][dq][:, :], T["ones"][0:1, 0:128],
            T["bo_row"][0:1, dq * 512:(dq + 1) * 512], start=True, stop=True),
            wait=([(S["s_exp"], exp_total), (S["s_msk"], msk_total),
                   (S["s_const"], 80)] if dq == 0 else []),
            inc=S["s_bob"])
    E.do("vector", lambda e: e.tensor_copy(
        T["bo_bc"][:, 0:512], P["P01"][0][:, :]), wait=[(S["s_bob"], 2)])
    E.do("vector", lambda e: e.tensor_copy(
        T["bo_bc"][:, 512:1024], P["P01"][1][:, :]), inc=S["s_bobv"])

    x_t, x2b = T["x_tile"], T["x2_buf"]
    for st in range(16):
        par = st % 2
        xin_waits = [(S["s_x2v"], 2 * (st - 1))]
        if st < 2:
            xin_waits.append((S["s_nrm"], n_iters))
        E.do("sync", lambda e, st=st, par=par: e.dma_start(
            x_t[par][:, :], T["d_x"].ap()[st * 128:(st + 1) * 128, :]),
            wait=xin_waits, inc=S["s_xin%d" % par], inc_amt=16)
        for dq in range(2):
            gidx = st * 2 + dq
            for dc in range(8):
                waits = []
                if dc == 0:
                    waits.append((S["s_x2v"], gidx - 1))
                    if gidx == 0:
                        waits.extend([(S["s_nrm"], n_iters), (S["s_w"], 64),
                                      (S["s_bobv"], 1)])
                E.do("tensor", lambda e, st=st, dq=dq, dc=dc, pp=gidx % 2: e.matmul(
                    P["P01"][pp][:, :],
                    T["ctxT"][:, dc * 2048 + st * 128: dc * 2048 + st * 128 + 128],
                    T["W_sb"][:, dc * 1024 + dq * 512: dc * 1024 + dq * 512 + 512],
                    start=(dc == 0), stop=(dc == 7)),
                    wait=waits, inc=(S["s_wot"] if dc == 7 else None))
            E.do("vector", lambda e, dq=dq, pp=gidx % 2, xp=par: e.tensor_tensor(
                x2b[xp][:, dq * 512:(dq + 1) * 512], P["P01"][pp][:, :],
                x_t[xp][:, dq * 512:(dq + 1) * 512], ALU.add),
                wait=[(S["s_wot"], gidx + 1),
                      (S["s_xin%d" % par], 16 * (st // 2 + 1)),
                      (S["s_x2o%d" % par], 16 * (st // 2))])
            E.drain("vector")
            E.do("vector", lambda e, dq=dq, xp=par: e.tensor_tensor(
                x2b[xp][:, dq * 512:(dq + 1) * 512],
                x2b[xp][:, dq * 512:(dq + 1) * 512],
                T["bo_bc"][:, dq * 512:(dq + 1) * 512], ALU.add),
                inc=S["s_x2v"])
        E.do("sync", lambda e, st=st, par=par: e.dma_start(
            T["d_x2"].ap()[st * 128:(st + 1) * 128, :], x2b[par][:, :]),
            wait=[(S["s_x2v"], 2 * (st + 1))], inc=S["s_x2o%d" % par], inc_amt=16)
    E.wait("sync", S["s_x2o0"], 16 * 8)
    E.wait("sync", S["s_x2o1"], 16 * 8)


def make_attn_tensors(nc, es, T):
    def sb(name, shape, dt):
        return es.enter_context(nc.sbuf_tensor(name, shape, dt))
    T["KT"] = sb("KT", [128, 8 * 2048], BF16)
    T["VN"] = sb("VN", [128, 16 * 1040], BF16)
    T["QT"] = sb("QT", [128, 8 * 2048], BF16)
    T["ctxT"] = sb("ctxT", [128, 8 * 2048], BF16)
    T["y1T"] = sb("y1T", [128, 8 * 2048], BF16)
    T["W_sb"] = sb("W_sb", [128, 8 * 1024], BF16)
    T["alibi"] = sb("alibi", [128, 640], F32)
    T["mask"] = sb("mask", [128, 2048], BF16)
    T["ident"] = sb("ident", [128, 128], BF16)
    T["ones"] = sb("ones", [1, 128], F32)
    T["bo_row"] = sb("bo_row", [1, 1024], F32)
    T["bo_bc"] = sb("bo_bc", [128, 1024], F32)
    T["x_tile"] = [sb("x_tile0", [128, 1024], F32), sb("x_tile1", [128, 1024], F32)]
    T["xbf"] = nc.alloc_sbuf_tensor_at(
        "xbf", [128, 16 * 1024], BF16, offset=nc.lookup_mloc(T["ctxT"]).addr)
    T["sq_scratch"] = [
        nc.alloc_sbuf_tensor_at(
            "sq_scratch0", [128, 1024], F32, offset=nc.lookup_mloc(T["QT"]).addr),
        nc.alloc_sbuf_tensor_at(
            "sq_scratch1", [128, 1024], F32,
            offset=nc.lookup_mloc(T["QT"]).addr + 4096),
    ]
    addr = nc.lookup_mloc(T["y1T"]).addr
    T["x2_buf"] = [
        nc.alloc_sbuf_tensor_at("x2b0", [128, 1024], F32, offset=addr),
        nc.alloc_sbuf_tensor_at("x2b1", [128, 1024], F32, offset=addr + 4096),
    ]
    for n in ("st_m16", "st_n16", "st_q16", "st_v16", "st_d16", "st_r16"):
        T[n] = sb(n, [128, 16], F32)
    T["eps_t"] = sb("eps_t", [128, 1], F32)
    kaddr = nc.lookup_mloc(T["KT"]).addr
    T["y1_tile"] = [
        nc.alloc_sbuf_tensor_at("y1_0", [128, 1024], BF16, offset=kaddr),
        nc.alloc_sbuf_tensor_at("y1_1", [128, 1024], BF16, offset=kaddr + 2048),
    ]
    T["expb"] = [sb("expb%d" % k, [128, 512], BF16) for k in range(3)]
    xa0 = nc.lookup_mloc(T["x_tile"][0]).addr
    xa1 = nc.lookup_mloc(T["x_tile"][1]).addr
    T["rd"] = [nc.alloc_sbuf_tensor_at("rd0", [1, 512], F32, offset=xa0),
               nc.alloc_sbuf_tensor_at("rd1", [1, 512], F32, offset=xa0 + 2048)]
    T["bc_sb"] = [nc.alloc_sbuf_tensor_at("bc0", [64, 512], F32, offset=xa1),
                  nc.alloc_sbuf_tensor_at("bc1", [64, 512], F32, offset=xa1 + 2048)]
    return T


def make_attn_psum(nc, es):
    P = {}
    P["PT"] = [es.enter_context(nc.psum_tensor("PT%d" % k, [128, 1024], BF16))
               for k in range(2)]
    P["P01"] = [es.enter_context(nc.psum_tensor("P01_%d" % k, [128, 512], F32))
                for k in range(3)]
    P["PC"] = [es.enter_context(nc.psum_tensor("PC%d" % k, [65, 512], F32))
               for k in range(2)]
    P["PB"] = [es.enter_context(nc.psum_tensor("PB0", [64, 512], F32))]
    return P



EPS = 1e-5

FFN_SEM_NAMES = [
    "f_xin0", "f_xin1", "f_v4", "f_tr", "f_xnt",
    "f_cst", "f_gt", "f_gv", "f_wn",
    "f_at", "f_as", "f_glat", "f_glav", "f_abo",
    "f_ew0", "f_ew1", "f_hmm", "f_gel", "f_eomm", "f_eov",
    "f_e2t", "f_e2v",
    "f_r0", "f_r1", "f_r2", "f_trf", "f_trfv", "f_rot", "f_rotv",
    "f_glmt", "f_glmv", "f_gate", "f_gbt", "f_gbv", "f_mix",
    "f_ot", "f_ov", "f_out0", "f_out1", "f_abi", "f_cstg", "f_ab2",
    "f_xbf", "f_sq", "f_ve", "f_sd",
    "f_xnw0", "f_xnw1", "f_zm", "f_gth", "f_xev", "f_xtr",
    "f_tiv", "f_sct", "f_mld", "f_gbb", "f_xnv",
    "f_tiv2", "f_cmp", "f_inv", "f_ivv", "f_ti3",
    "f_ixe", "f_wgc"]


def ffn_sems(nc, es):
    return {n: Sem(nc, es, n) for n in FFN_SEM_NAMES}


def mm_di(E, fn, wait=(), inc=None):
    E.do("tensor", fn, wait=wait)
    E.drain("tensor")
    E.do("tensor", lambda e: e.nop(), inc=inc)


def emit_ffn(E, S, T, P):
    emit_f0_consts(E, S, T)
    emit_f1_ln2_xnt(E, S, T, P)
    emit_gbc_build(E, S, T, P)
    emit_f2_gating(E, S, T, P)
    emit_xn_spill(E, S, T)
    emit_slot_chain(E, S, T)
    emit_f3_abel(E, S, T, P)
    emit_f4_moe_sparse(E, S, T, P)
    emit_f5_rot_mix_out(E, S, T, P)

CAP = 768
NCH = CAP // 128


def emit_gbc_build(E, S, T, P):
    for i, (row, dst) in enumerate([("gamrow", "gamma_bc"), ("betrow", "beta_bc")]):
        for c in range(2):
            mm_di(E, lambda e, row=row, c=c: e.matmul(
                P["PS2"][:, :], T["ones_f"][0:1, :],
                T[row][0:1, c * 512:(c + 1) * 512], start=True, stop=True),
                wait=[(S["f_cst"], 16 * N_CST), (S["f_gbb"], 2 * i + c)],
                inc=S["f_tiv"])
            E.do("vector", lambda e, dst=dst, c=c, i=i: e.tensor_copy(
                T[dst][:, c * 512:(c + 1) * 512], P["PS2"][:, :]),
                wait=[(S["f_tiv"], 2 * i + c + 1)], inc=S["f_gbb"])


def emit_slot_chain(E, S, T):
    B1, B3 = T["ib1"], T["ib3"]
    E.do("vector", lambda e: e.tensor_single_scalar(
        B1[:, :], T["wT"][0:8, :], 0.0, ALU.is_gt),
        wait=[(S["f_wn"], 32), (S["f_e2v"], 16)])
    E.drain("vector")
    E.do("vector", lambda e: e.tensor_tensor_scan(
        B3[:, :], B1[:, :], B1[:, :], 0.0, ALU.add, ALU.bypass))
    E.drain("vector")
    E.do("vector", lambda e: e.tensor_tensor(
        T["ib2b"][:, :], B1[:, :], B3[:, :], ALU.mult))
    E.drain("vector")
    E.do("vector", lambda e: e.tensor_scalar_add(
        T["ib2b"][:, :], T["ib2b"][:, :], -1.0), inc=S["f_xnv"])


def emit_slot_transposes(E, S, T, P):
    for st in range(16):
        w = []
        if st == 0:
            w = [(S["f_xnv"], 18), (S["f_e2v"], 16)]
        E.do("tensor", lambda e, st=st: e.matmul(
            P["PS2"][:, st * 8:(st + 1) * 8],
            T["ib2b"][0:8, st * 128:(st + 1) * 128], T["id8"][0:8, :],
            start=True, stop=True), wait=w)
    E.drain("tensor")
    E.do("tensor", lambda e: e.nop(), inc=S["f_tiv"])
    E.do("vector", lambda e: e.tensor_copy(
        T["slotsT"][:, :], P["PS2"][:, 0:128]),
        wait=[(S["f_tiv"], 5)], inc=S["f_tiv2"])


def _inv_compares(E, S, T, e_):
    for tb in range(16):
        gi = e_ * 16 + tb
        w = []
        if gi >= 2:
            w.append((S["f_inv"], 2 * (gi - 1)))
        if gi == 0:
            w += [(S["f_tiv2"], 1), (S["f_abi"], 32), (S["f_gt"], 4)]
        E.do("vector", lambda e, e_=e_, tb=tb, gi=gi: e.tensor_scalar(
            T["MT"][gi % 2][:, :], T["siota"][:, :],
            T["slotsT"][:, tb * 8 + e_: tb * 8 + e_ + 1], None, ALU.is_equal),
            wait=w, inc=S["f_cmp"])


def _inv_mms(E, S, T, P, e_):
    for tb in range(16):
        gi = e_ * 16 + tb
        for sl in range(2):
            w = []
            if sl == 0:
                w.append((S["f_cmp"], gi + 1))
            if tb == 0:
                w.append((S["f_ivv"], 2 * e_))
                if e_ == 0 and sl == 0:
                    w.append((S["f_glav"], 4))
            if tb == 15:
                mm_di(E, lambda e, gi=gi, tb=tb, sl=sl: e.matmul(
                    (P["PS1"][0:2, 0:512] if sl == 0 else P["PS2"][0:2, 0:256]),
                    T["c_hilo_sb"][:, tb * 2:tb * 2 + 2],
                    T["MT"][gi % 2][:, sl * 512: 512 + sl * 256],
                    start=(tb == 0), stop=(tb == 15)),
                    wait=w, inc=S["f_inv"])
            else:
                E.do("tensor", lambda e, gi=gi, tb=tb, sl=sl: e.matmul(
                    (P["PS1"][0:2, 0:512] if sl == 0 else P["PS2"][0:2, 0:256]),
                    T["c_hilo_sb"][:, tb * 2:tb * 2 + 2],
                    T["MT"][gi % 2][:, sl * 512: 512 + sl * 256],
                    start=(tb == 0), stop=(tb == 15)),
                    wait=w, inc=S["f_inv"])


def _inv_finish(E, S, T, P, e_):
    E.do("vector", lambda e: e.tensor_copy(
        T["vrow"][0:2, 0:512], P["PS1"][0:2, 0:512]),
        wait=[(S["f_inv"], 32 * (e_ + 1)), (S["f_ti3"], NCH * e_)],
        inc=S["f_ivv"])
    E.do("vector", lambda e: e.tensor_copy(
        T["vrow"][0:2, 512:768], P["PS2"][0:2, 0:256]), inc=S["f_ivv"])
    for ci in range(NCH):
        n = e_ * NCH + ci
        c0 = 300 + 2 * (ci % 2)
        w = []
        if ci == 0:
            w.append((S["f_ivv"], 2 * (e_ + 1)))
        if n >= 2:
            w.append((S["f_ixe"], 2 * n - 2))
        mm_di(E, lambda e, ci=ci, c0=c0: e.matmul(
            P["PS2"][:, c0:c0 + 2], T["vrow"][0:2, ci * 128:(ci + 1) * 128],
            T["id8"][0:2, 0:2], start=True, stop=True),
            wait=w, inc=S["f_ti3"])
        E.do("vector", lambda e, c0=c0: e.tensor_copy(
            T["pse"][:, :], P["PS2"][:, c0:c0 + 2]),
            wait=[(S["f_ti3"], n + 1)], inc=S["f_ixe"])
        E.drain("vector")
        E.do("vector", lambda e: e.tensor_single_scalar(
            T["iocc"][:, :], T["pse"][:, 0:1], 0.5, ALU.is_gt))
        E.do("vector", lambda e: e.scalar_tensor_tensor(
            T["itmp"][:, :], T["pse"][:, 0:1], 16.0, T["pse"][:, 1:2],
            ALU.mult, ALU.add))
        E.drain("vector")
        E.do("vector", lambda e: e.tensor_tensor(
            T["itmp"][:, :], T["itmp"][:, :], T["c_spad_sb"][:, 0:1],
            ALU.subtract))
        E.drain("vector")
        E.do("vector", lambda e: e.tensor_tensor(
            T["itmp"][:, :], T["itmp"][:, :], T["iocc"][:, :], ALU.mult))
        E.drain("vector")
        E.do("vector", lambda e: e.tensor_tensor(
            T["itmp"][:, :], T["itmp"][:, :], T["c_spad_sb"][:, 1:2], ALU.add))
        E.drain("vector")
        E.do("vector", lambda e, ci=ci, e_=e_: e.tensor_copy(
            T["idxU"][:, ci * 8 + e_: ci * 8 + e_ + 1], T["itmp"][:, :]),
            inc=S["f_ixe"])


def _gathers(E, S, T, e_):
    from concourse.bass import IndirectOffsetOnAxis
    for g in range(NCH):
        w = [(S["f_ixe"], 2 * (NCH * e_ + g + 1))]
        if g == 0:
            w += [(S["f_xtr"], NCH * e_), (S["f_wgc"], NCH * e_)]
            if e_ == 0:
                w += [(S["f_xnw0"], 16 * 8), (S["f_xnw1"], 16 * 8),
                      (S["f_zm"], 16 * 34), (S["f_at"], 32)]
        E.do("gpsimd", lambda e, e_=e_, g=g: e.indirect_dma_start(
            T["gath"][:, g * 1032:(g + 1) * 1032], None,
            T["d_xn"].ap(),
            IndirectOffsetOnAxis(ap=T["idxU"][:, g * 8 + e_: g * 8 + e_ + 1],
                                 axis=0)),
            wait=w, inc=S["f_gth"], inc_amt=16)


def _xposes(E, S, T, P, e_):
    for g in range(NCH):
        par = g % 2
        for dc in range(8):
            w = []
            if dc == 0:
                w = [(S["f_gth"], 16 * NCH * (e_ + 1)),
                     (S["f_xev"], NCH * e_ + g - 1)]
                if e_ == 0 and g == 0:
                    w.append((S["f_xnt"], 16))
            E.do("tensor", lambda e, g=g, dc=dc, par=par: e.transpose(
                P["PTf"][par][:, dc * 128:(dc + 1) * 128],
                T["gath"][:, g * 1032 + dc * 128: g * 1032 + (dc + 1) * 128],
                T["ident"][:, :]),
                wait=w, inc=(S["f_xtr"] if dc == 7 else None))
        E.do("vector", lambda e, g=g, par=par: e.tensor_copy(
            T["xncT"][:, :].rearrange("p (dc c) -> p dc c", dc=8)[
                :, :, g * 128:(g + 1) * 128],
            P["PTf"][par][:, :].rearrange("p (dc c) -> p dc c", dc=8)),
            wait=[(S["f_xtr"], NCH * e_ + g + 1)], inc=S["f_xev"])
    for g in range(NCH):
        E.do("vector", lambda e, g=g, e_=e_: e.tensor_copy(
            T["wgc"][:, g * 8 + e_: g * 8 + e_ + 1],
            T["gath"][:, g * 1032 + 1024 + e_: g * 1032 + 1024 + e_ + 1]),
            wait=([(S["f_gth"], 16 * NCH * (e_ + 1))] if g == 0 else []),
            inc=S["f_wgc"])


def emit_xn_spill(E, S, T):
    E.do("vector", lambda e: e.memset(T["zt"][:, :], 0.0), inc=S["f_xnv"])
    for z in range(17):
        E.do("sync", lambda e, z=z: e.dma_start(
            T["d_moe"].ap()[z * 128:(z + 1) * 128, :], T["zt"][:, 0:1024]),
            wait=([(S["f_xnv"], 1)] if z == 0 else []), inc=S["f_zm"], inc_amt=16)
    E.do("sync", lambda e: e.dma_start(
        T["d_xn"].ap()[2048:2176, :], T["zt"][:, :]), inc=S["f_zm"], inc_amt=16)
    for st in range(16):
        E.do("sync", lambda e, st=st: e.dma_start(
            T["d_xn"].ap()[st * 128:(st + 1) * 128, 1024:1032],
            T["wNb"][:, st * 8:(st + 1) * 8]),
            wait=[(S["f_wn"], 2 * st + 2)], inc=S["f_zm"], inc_amt=16)
    for st0 in range(0, 16, 2):
        for q in range(2):
            st = st0 + q
            E.do("vector", lambda e, st=st, q=q: e.tensor_tensor(
                T["xnn"][q][:, :], T["y2_sb"][:, st * 1024:(st + 1) * 1024],
                T["gamma_bc"][:, :], ALU.mult),
                wait=[(S["f_v4"], st + 1), (S["f_gbb"], 4),
                      (S["f_xnw%d" % q], 16 * (st // 2))])
        E.drain("vector")
        for q in range(2):
            st = st0 + q
            E.do("vector", lambda e, st=st, q=q: e.tensor_tensor(
                T["xnn"][q][:, :], T["xnn"][q][:, :], T["beta_bc"][:, :],
                ALU.add), inc=S["f_xnv"])
        E.drain("vector")
        for q in range(2):
            st = st0 + q
            E.do("sync", lambda e, st=st, q=q: e.dma_start(
                T["d_xn"].ap()[st * 128:(st + 1) * 128, 0:1024], T["xnn"][q][:, :]),
                wait=[(S["f_xnv"], st + 2)], inc=S["f_xnw%d" % q], inc_amt=16)



N_CST = 14
N_CSTG = 4


def emit_f0_consts(E, S, T):
    for q in range(4):
        E.do("gpsimd", lambda e, q=q: e.dma_start(
            T["x2bf"][:, q * 4096:(q + 1) * 4096].rearrange(
                "p (st c) -> p st c", st=4),
            T["d_x2"].ap()[q * 512:(q + 1) * 512, :].rearrange(
                "(st p) c -> p st c", p=128)),
            wait=([(S["f_xbf"], 16 * q)] if q > 0 else []),
            inc=S["f_xbf"], inc_amt=16)
    s = S["f_cst"]
    E.do("sync", lambda e: e.dma_start(T["ident"][:, :], T["c_ident"].ap()),
         inc=s, inc_amt=16)
    E.do("sync", lambda e: e.dma_start(T["id8"][:, :], T["c_id8"].ap()),
         inc=s, inc_amt=16)
    E.do("sync", lambda e: e.dma_start(T["c_hilo_sb"][:, :], T["d_c_hilo"].ap()),
         inc=s, inc_amt=16)
    E.do("sync", lambda e: e.dma_start(T["c_spad_sb"][:, :], T["d_c_spad"].ap()),
         inc=s, inc_amt=16)
    E.do("sync", lambda e: e.dma_start(T["ones_bf"][:, :], T["c_ones_bf"].ap(
        ).rearrange("(o f) -> o f", o=1)), inc=s, inc_amt=16)
    E.do("sync", lambda e: e.dma_start(T["ones_f"][:, :], T["c_ones_f"].ap(
        ).rearrange("(o f) -> o f", o=1)), inc=s, inc_amt=16)
    E.do("sync", lambda e: e.dma_start(
        T["gamrow"][:, :], T["d_ln_gamma"].ap().rearrange("(o f) -> o f", o=1)),
        inc=s, inc_amt=16)
    E.do("sync", lambda e: e.dma_start(
        T["betrow"][:, :], T["d_ln_beta"].ap().rearrange("(o f) -> o f", o=1)),
        inc=s, inc_amt=16)
    E.do("sync", lambda e: e.dma_start(
        T["gam"][:, :], T["d_ln_gamma_t"].ap()), inc=s, inc_amt=16)
    E.do("sync", lambda e: e.dma_start(
        T["bet"][:, :], T["d_ln_beta_t"].ap()), inc=s, inc_amt=16)
    E.do("sync", lambda e: e.dma_start(
        T["ab_b"][:, :], T["d_abel_b_t"].ap()), inc=s, inc_amt=16)
    E.do("sync", lambda e: e.dma_start(
        T["eb1_sb"][:, :], T["d_eb1_t"].ap()), inc=s, inc_amt=16)
    E.do("sync", lambda e: e.dma_start(
        T["gb_sb"][:, :], T["d_gate_b"].ap().rearrange("(p o) -> p o", o=1)),
        inc=s, inc_amt=16)
    E.do("sync", lambda e: e.dma_start(
        T["gcb_sb"][:, :], T["d_gc_b"].ap().rearrange("(p o) -> p o", o=1)),
        inc=s, inc_amt=16)
    sg = S["f_cstg"]
    E.do("gpsimd", lambda e: e.dma_start(
        T["eb2_sb"][:, :], T["d_eb2"].ap()), inc=sg, inc_amt=16)
    E.do("gpsimd", lambda e: e.dma_start(
        T["gw_sb"][:, :], T["d_gate_w_t"].ap()), inc=sg, inc_amt=16)
    E.do("gpsimd", lambda e: e.dma_start(
        T["gc1_sb"][:, :], T["d_gc1_t"].ap()), inc=sg, inc_amt=16)
    E.do("gpsimd", lambda e: e.dma_start(
        T["gc2_sb"][:, :], T["d_gc2_t"].ap()), inc=sg, inc_amt=16)
    E.do("gpsimd", lambda e: e.dma_start(
        T["aw_sb"][:, :].rearrange("p (dc c) -> p dc c", dc=8),
        T["d_abel_w"].ap().rearrange("(dc p) c -> p dc c", p=128)),
        inc=S["f_abi"], inc_amt=16)
    _ew_load(E, S, T, 0)


def _ew_load(E, S, T, idx, wait=()):
    e, half = idx // 2, idx % 2
    buf = idx % 2
    sem = S["f_ew%d" % buf]
    E.do("gpsimd", lambda en: en.dma_start(
        T["ew1h"][buf][:, :].rearrange("p (dc c) -> p dc c", dc=8),
        T["d_ew1"].ap()[e, :, half * 1024:(half + 1) * 1024].rearrange(
            "(dc p) c -> p dc c", p=128)),
        wait=wait, inc=sem, inc_amt=16)
    E.do("gpsimd", lambda en: en.dma_start(
        T["ew2h"][buf][:, :].rearrange("p (hc c) -> p hc c", hc=8),
        T["d_ew2"].ap()[e, half * 1024:(half + 1) * 1024, :].rearrange(
            "(hc p) c -> p hc c", p=128)),
        inc=sem, inc_amt=16)


def emit_f1_ln2_xnt(E, S, T, P):
    E.do("vector", lambda e: e.memset(T["eps_t"][:, :], EPS), inc=S["f_ve"])
    for q in range(4):
        qs = slice(q * 4, (q + 1) * 4)
        for st in range(q * 4, q * 4 + 4):
            E.do("vector", lambda e, st=st: e.reduce_sum(
                T["fs_m16"][:, st:st + 1],
                T["x2bf"][:, st * 1024:(st + 1) * 1024], axis=AX.X),
                wait=([(S["f_xbf"], 16 * (q + 1))] if st % 4 == 0 else []))
            E.do("scalar", lambda e, st=st: e.activation(
                T["f_sqs" if st % 2 == 0 else "f_sqs2"][:, :],
                T["x2bf"][:, st * 1024:(st + 1) * 1024],
                AF.Square, accum_out=T["fs_q16"][:, st:st + 1]),
                wait=([(S["f_xbf"], 16 * (q + 1))] if st % 4 == 0 else []),
                inc=(S["f_sq"] if st % 4 == 3 else None))
            E.drain("scalar")
        E.drain("vector")
        E.do("vector", lambda e, qs=qs: e.tensor_scalar_mul(
            T["fs_m16"][:, qs], T["fs_m16"][:, qs], 1.0 / 1024))
        E.drain("vector")
        E.do("vector", lambda e, qs=qs: e.tensor_scalar_mul(
            T["fs_n16"][:, qs], T["fs_m16"][:, qs], -1.0))
        E.drain("vector")
        E.do("vector", lambda e, qs=qs: e.tensor_tensor(
            T["fs_v16"][:, qs], T["fs_n16"][:, qs], T["fs_m16"][:, qs],
            ALU.mult))
        E.drain("vector")
        E.do("vector", lambda e, qs=qs: e.scalar_tensor_tensor(
            T["fs_v16"][:, qs], T["fs_q16"][:, qs], 1.0 / 1024,
            T["fs_v16"][:, qs], ALU.mult, ALU.add),
            wait=[(S["f_sq"], q + 1)], inc=S["f_ve"])
        E.do("scalar", lambda e, qs=qs: e.activation(
            T["fs_d16"][:, qs], T["fs_v16"][:, qs], AF.Sqrt,
            bias=T["eps_t"][:, :]),
            wait=[(S["f_ve"], q + 2)], inc=S["f_sd"])
        E.do("vector", lambda e, qs=qs: e.reciprocal(
            T["fs_r16"][:, qs], T["fs_d16"][:, qs]),
            wait=[(S["f_sd"], q + 1)])
        E.drain("vector")
        for st in range(q * 4, q * 4 + 4):
            E.do("vector", lambda e, st=st: e.tensor_scalar(
                T["y2_sb"][:, st * 1024:(st + 1) * 1024],
                T["x2bf"][:, st * 1024:(st + 1) * 1024],
                T["fs_n16"][:, st:st + 1], T["fs_r16"][:, st:st + 1],
                ALU.add, ALU.mult), inc=S["f_v4"])
    for st in range(16):
        par = st % 2
        for dc in range(8):
            E.do("tensor", lambda e, dc=dc, st=st, par=par: e.transpose(
                P["PTf"][par][:, dc * 128:(dc + 1) * 128],
                T["y2_sb"][:, st * 1024 + dc * 128: st * 1024 + (dc + 1) * 128],
                T["ident"][:, :]),
                wait=([(S["f_v4"], st + 1), (S["f_xnt"], st - 1),
                       (S["f_cst"], 16 * N_CST)] if dc == 0 else []),
                inc=(S["f_tr"] if dc == 7 else None))
        for dc in range(8):
            E.do("scalar", lambda e, dc=dc, st=st, par=par: e.activation(
                T["xnT"][:, dc * 2048 + st * 128: dc * 2048 + st * 128 + 128],
                P["PTf"][par][:, dc * 128:(dc + 1) * 128], AF.Identity,
                bias=T["bet"][:, dc:dc + 1], scale=T["gam"][:, dc:dc + 1]),
                wait=([(S["f_tr"], st + 1)] if dc == 0 else []),
                inc=(S["f_xnt"] if dc == 7 else None))


def emit_f2_gating(E, S, T, P):
    PS1, PS2 = P["PS1"], P["PS2"]
    for c in range(4):
        for dc in range(8):
            fn = lambda e, c=c, dc=dc: e.matmul(
                PS1[0:8, :], T["gw_sb"][:, dc * 8:(dc + 1) * 8],
                T["xnT"][:, dc * 2048 + c * 512: dc * 2048 + (c + 1) * 512],
                start=(dc == 0), stop=(dc == 7))
            gwt = ([(S["f_xnt"], 16), (S["f_gv"], c),
                    (S["f_cstg"], 16 * N_CSTG)] if dc == 0 else [])
            if dc == 7:
                mm_di(E, fn, wait=gwt, inc=S["f_gt"])
            else:
                E.do("tensor", fn, wait=gwt)
        E.do("vector", lambda e, c=c: e.tensor_scalar(
            T["glog"][:, c * 512:(c + 1) * 512], PS1[0:8, :],
            T["gb_sb"][0:8, :], None, ALU.add),
            wait=[(S["f_gt"], c + 1)], inc=S["f_gv"])
    for st in range(16):
        E.do("tensor", lambda e, st=st: e.matmul(
            PS2[:, st * 8:(st + 1) * 8],
            T["glog"][0:8, st * 128:(st + 1) * 128],
            T["id8"][0:8, :], start=True, stop=True),
            wait=([(S["f_gv"], 4), (S["f_gbb"], 4)] if st == 0 else []))
    E.drain("tensor")
    E.do("tensor", lambda e: e.nop(), inc=S["f_gt"])
    for st in range(16):
        E.do("vector", lambda e, st=st: e.reduce_max(
            T["g_m1"][:, :], PS2[:, st * 8:(st + 1) * 8], axis=AX.X),
            wait=([(S["f_gt"], 5)] if st == 0 else [(S["f_gate"], st)]))
        E.drain("vector")
        E.do("vector", lambda e: e.tensor_scalar_mul(
            T["g_nm1"][:, :], T["g_m1"][:, :], -1.0))
        E.drain("vector")
        E.do("vector", lambda e, st=st: e.tensor_single_scalar(
            T["g_msk"][:, :], PS2[:, st * 8:(st + 1) * 8], T["g_m1"][:, :],
            ALU.is_equal))
        E.drain("vector")
        E.do("vector", lambda e, st=st: e.scalar_tensor_tensor(
            T["g_lm"][:, :], T["g_msk"][:, :], -1e30,
            PS2[:, st * 8:(st + 1) * 8], ALU.mult, ALU.add))
        E.drain("vector")
        E.do("vector", lambda e: e.reduce_max(
            T["g_m2"][:, :], T["g_lm"][:, :], axis=AX.X))
        E.drain("vector")
        E.do("vector", lambda e: e.tensor_scalar(
            T["g_d0"][:, :], T["g_m2"][:, :], T["g_nm1"][:, :], None, ALU.add),
            inc=S["f_wn"])
        E.do("scalar", lambda e, st=st: e.activation(
            T["g_e2"][:, :], T["g_d0"][:, :], AF.Exp),
            wait=[(S["f_wn"], 2 * st + 1)])
        E.drain("scalar")
        E.do("scalar", lambda e: e.add(T["g_dn"][:, :], T["g_e2"][:, :], 1.0))
        E.drain("scalar")
        E.do("scalar", lambda e, st=st: e.activation(
            T["g_e8"][:, :], PS2[:, st * 8:(st + 1) * 8], AF.Exp,
            bias=T["g_nm1"][:, :]),
            inc=S["f_gate"])
        E.do("vector", lambda e, st=st: e.reciprocal(
            T["g_rdn"][:, :], T["g_dn"][:, :]),
            wait=[(S["f_gate"], st + 1)])
        E.drain("vector")
        E.do("vector", lambda e, st=st: e.tensor_single_scalar(
            T["g_mge"][:, :], PS2[:, st * 8:(st + 1) * 8], T["g_m2"][:, :],
            ALU.is_ge))
        E.drain("vector")
        E.do("vector", lambda e: e.tensor_tensor(
            T["g_t8"][:, :], T["g_e8"][:, :], T["g_mge"][:, :], ALU.mult))
        E.drain("vector")
        E.do("vector", lambda e, st=st: e.tensor_scalar(
            T["wN"][:, st * 8:(st + 1) * 8], T["g_t8"][:, :], T["g_rdn"][:, :],
            None, ALU.mult))
        E.drain("vector")
        E.do("vector", lambda e, st=st: e.tensor_copy(
            T["wNb"][:, st * 8:(st + 1) * 8], T["wN"][:, st * 8:(st + 1) * 8]),
            inc=S["f_wn"])
    E.do("vector", lambda e: e.memset(T["wT"][:, :], 0.0),
         wait=[(S["f_sq"], 4)])
    E.drain("vector")
    for st in range(16):
        waits = [(S["f_wn"], 2 * st + 2), (S["f_e2v"], st)]
        if st == 0:
            waits += [(S["f_wn"], 32), (S["f_gate"], 16)]
        mm_di(E, lambda e, st=st: e.matmul(
            PS2[0:8, 0:128], T["wNb"][:, st * 8:(st + 1) * 8], T["ident"][:, :],
            start=True, stop=True),
            wait=waits, inc=S["f_e2t"])
        E.do("vector", lambda e, st=st: e.tensor_copy(
            T["wT"][0:8, st * 128:(st + 1) * 128], PS2[0:8, 0:128]),
            wait=[(S["f_e2t"], st + 1)], inc=S["f_e2v"])


def emit_f3_abel(E, S, T, P):
    dcp_order = [4, 5, 6, 7, 0, 1, 2, 3]
    groups = [(dcp, c) for dcp in dcp_order for c in range(4)]
    for gidx, (dcp, c) in enumerate(groups):
        par = gidx % 2
        if gidx == 16:
            emit_slot_transposes(E, S, T, P)
        for dc in range(8):
            waits = []
            if dc == 0:
                waits = [(S["f_as"], gidx - 1)]
                if gidx == 0:
                    waits += [(S["f_abi"], 16), (S["f_xnt"], 16)]
        
            E.do("tensor", lambda e, dcp=dcp, c=c, dc=dc, par=par: e.matmul(
                P["PEO"][par][:, :],
                T["aw_sb"][:, dc * 1024 + dcp * 128: dc * 1024 + (dcp + 1) * 128],
                T["xnT"][:, dc * 2048 + c * 512: dc * 2048 + (c + 1) * 512],
                start=(dc == 0), stop=(dc == 7)),
                wait=waits, inc=(S["f_at"] if dc == 7 else None))
        gw = [(S["f_at"], gidx + 1)]
        if gidx == 8:
            gw.append((S["f_xnv"], 18))
        if gidx == 16:
            gw.append((S["f_tiv"], 5))
        E.do("scalar", lambda e, dcp=dcp, c=c, par=par, gidx=gidx: e.activation(
            T["abelT"][:, dcp * 2048 + c * 512: dcp * 2048 + (c + 1) * 512],
            P["PEO"][par][:, :], AF.Gelu, bias=T["ab_b"][:, dcp:dcp + 1]),
            wait=gw, inc=S["f_as"])
    for c in range(4):
        for dcp in range(8):
            waits = []
            if dcp == 0:
                waits = [(S["f_glav"], c)]
                if c == 0:
                    waits += [(S["f_as"], 32), (S["f_cstg"], 16 * N_CSTG)]
            fn = lambda e, c=c, dcp=dcp: e.matmul(
                P["PS1"][0:1, :], T["gc1_sb"][:, dcp:dcp + 1],
                T["abelT"][:, dcp * 2048 + c * 512: dcp * 2048 + (c + 1) * 512],
                start=(dcp == 0), stop=(dcp == 7))
            if dcp == 7:
                mm_di(E, fn, wait=waits, inc=S["f_glat"])
            else:
                E.do("tensor", fn, wait=waits)
        E.do("vector", lambda e, c=c: e.tensor_scalar(
            T["gla"][0:1, c * 512:(c + 1) * 512], P["PS1"][0:1, :],
            T["gcb_sb"][0:1, :], None, ALU.add),
            wait=[(S["f_glat"], c + 1)], inc=S["f_glav"])
    E.do("sync", lambda e: e.dma_start(T["d_abel"].ap(), T["abelT"][:, :]),
         wait=[(S["f_glav"], 4)], inc=S["f_abo"], inc_amt=16)


def emit_f4_moe_sparse(E, S, T, P):
    from concourse.bass import IndirectOffsetOnAxis
    gel_count, hmm_count, eomm_count = {}, {}, {}
    n_gel = n_hmm = n_eomm = 0
    for idx in range(16):
        for hc in range(8):
            for sc in range(2):
                n_hmm += 1
                hmm_count[(idx, hc, sc)] = n_hmm
                n_gel += 1
                gel_count[(idx, hc, sc)] = n_gel
        for k in range(NCH):
            for dq in range(2):
                n_eomm += 1
                eomm_count[(idx, k, dq)] = n_eomm
    h_groups = [(idx, hc, sc) for idx in range(16) for hc in range(8)
                for sc in range(2)]
    h_gnum = {g: n for n, g in enumerate(h_groups)}
    eo_groups = [(idx, k, dq) for idx in range(16) for k in range(NCH)
                 for dq in range(2)]
    eo_gnum = {g: n for n, g in enumerate(eo_groups)}

    E.do("gpsimd", lambda e: e.dma_start(T["siota"][:, :], T["d_c_siota"].ap()),
         wait=[(S["f_at"], 32)], inc=S["f_abi"], inc_amt=16)
    _inv_compares(E, S, T, 0)
    _inv_mms(E, S, T, P, 0)
    _inv_finish(E, S, T, P, 0)
    _gathers(E, S, T, 0)

    for e_ in range(8):
        if e_ + 1 <= 7:
            _inv_compares(E, S, T, e_ + 1)
            _inv_mms(E, S, T, P, e_ + 1)
            _inv_finish(E, S, T, P, e_ + 1)
            _gathers(E, S, T, e_ + 1)
        _xposes(E, S, T, P, e_)
        for half in range(2):
            idx = e_ * 2 + half
            buf = idx % 2
            if idx + 1 < 16:
                w = []
                if idx >= 1:
                    w = [(S["f_eomm"], eomm_count[(idx - 1, NCH - 1, 1)])]
                _ew_load(E, S, T, idx + 1, wait=w)
            for hc in range(8):
                for sc in range(2):
                    g = (idx, hc, sc)
                    gn = h_gnum[g]
                    nw = 512 if sc == 0 else 256
                    for dc in range(8):
                        waits = []
                        if dc == 0:
                            pg = h_groups[gn - 2] if gn >= 2 else None
                            if pg:
                                waits.append((S["f_gel"], gel_count[pg]))
                            if hc == 0 and sc == 0:
                                waits.append((S["f_ew%d" % buf],
                                              32 * (idx // 2 + 1)))
                                waits.append((S["f_xev"], NCH * (e_ + 1)))
                        E.do("tensor", lambda e, buf=buf, hc=hc, sc=sc, dc=dc,
                             nw=nw, par=gn % 2: e.matmul(
                            P["PH"][par][:, 0:nw],
                            T["ew1h"][buf][:, dc * 1024 + hc * 128:
                                           dc * 1024 + (hc + 1) * 128],
                            T["xncT"][:, dc * CAP + sc * 512:
                                      dc * CAP + sc * 512 + nw],
                            start=(dc == 0), stop=(dc == 7)),
                            wait=waits, inc=(S["f_hmm"] if dc == 7 else None))
                    gw = [(S["f_hmm"], hmm_count[g])]
                    if idx == 0 and hc == 0 and sc == 0:
                        gw.append((S["f_abo"], 16))
                    pgn = gn - 16
                    if pgn >= 0:
                        pidx = h_groups[pgn][0]
                        gw.append((S["f_eomm"], eomm_count[(pidx, NCH - 1, 1)]))
                    E.do("scalar", lambda e, hc=hc, sc=sc, idx=idx, nw=nw,
                         par=gn % 2: e.activation(
                        T["hcT"][:, hc * CAP + sc * 512: hc * CAP + sc * 512 + nw],
                        P["PH"][par][:, 0:nw], AF.Gelu,
                        bias=T["eb1_sb"][:, idx * 8 + hc: idx * 8 + hc + 1]),
                        wait=gw, inc=S["f_gel"])
            for k in range(NCH):
                for dq in range(2):
                    g = (idx, k, dq)
                    gn = eo_gnum[g]
                    for hc in range(8):
                        waits = [(S["f_gel"],
                                  gel_count[(idx, hc, 0 if k < 4 else 1)])]
                        if hc == 0:
                            pg = eo_groups[gn - 2] if gn >= 2 else None
                            if pg:
                                waits.append((S["f_eov"],
                                              eomm_count[pg]))
                        E.do("tensor", lambda e, buf=buf, k=k, dq=dq, hc=hc,
                             par=gn % 2: e.matmul(
                            P["PEO"][par][:, :],
                            T["hcT"][:, hc * CAP + k * 128:
                                     hc * CAP + k * 128 + 128],
                            T["ew2h"][buf][:, hc * 1024 + dq * 512:
                                           hc * 1024 + dq * 512 + 512],
                            start=(hc == 0), stop=(hc == 7)),
                            wait=waits, inc=(S["f_eomm"] if hc == 7 else None))
                    ew = [(S["f_eomm"], eomm_count[g]),
                          (S["f_sct"], 16 * NCH * idx),
                          (S["f_wgc"], NCH * e_ + k + 1)]
                    E.do("vector", lambda e, e_=e_, k=k, dq=dq, par=gn % 2:
                         e.tensor_scalar(
                        T["eoc"][:, k * 1024 + dq * 512: k * 1024 + dq * 512 + 512],
                        P["PEO"][par][:, :],
                        T["wgc"][:, k * 8 + e_: k * 8 + e_ + 1],
                        None, ALU.mult),
                        wait=ew, inc=S["f_eov"])
            for k in range(NCH):
                waits = [(S["f_eov"], eomm_count[(idx, k, 1)]),
                         (S["f_sct"], 16 * (NCH * idx + k))]
                E.do("gpsimd", lambda e, e_=e_, k=k: e.indirect_dma_start(
                    T["d_moe"].ap(),
                    IndirectOffsetOnAxis(ap=T["idxU"][:, k * 8 + e_: k * 8 + e_ + 1],
                                         axis=0),
                    T["eoc"][:, k * 1024:(k + 1) * 1024], None,
                    compute_op=ALU.add),
                    wait=waits, inc=S["f_sct"], inc_amt=16)
    for q in range(4):
        w = [(S["f_mld"], 16 * q)]
        if q == 0:
            w = [(S["f_sct"], 16 * NCH * 16)]
        E.do("sync", lambda e, q=q: e.dma_start(
            T["macc"][:, q * 4096:(q + 1) * 4096].rearrange(
                "p (st c) -> p st c", st=4),
            T["d_moe"].ap()[q * 512:(q + 1) * 512, :].rearrange(
                "(st p) c -> p st c", p=128)),
            wait=w, inc=S["f_mld"], inc_amt=16)
    e2t_base = E.cnt(S["f_e2t"])
    for stt in range(16):
        for dq in range(2):
            gn = stt * 2 + dq
            waits = [(S["f_e2v"], 16 + gn - 1)]
            if gn == 0:
                waits += [(S["f_eov"], E.cnt(S["f_eov"])), (S["f_e2v"], 16),
                          (S["f_cstg"], 16 * N_CSTG)]
            mm_di(E, lambda e, stt=stt, dq=dq, par=gn % 2: e.matmul(
                P["PEO"][par][:, :], T["wT"][0:8, stt * 128:(stt + 1) * 128],
                T["eb2_sb"][0:8, dq * 512:(dq + 1) * 512],
                start=True, stop=True),
                wait=waits, inc=S["f_e2t"])
            E.do("vector", lambda e, stt=stt, dq=dq, par=gn % 2: e.tensor_tensor(
                T["macc"][:, stt * 1024 + dq * 512: stt * 1024 + dq * 512 + 512],
                T["macc"][:, stt * 1024 + dq * 512: stt * 1024 + dq * 512 + 512],
                P["PEO"][par][:, :], ALU.add),
                wait=[(S["f_e2t"], e2t_base + gn + 1),
                      (S["f_mld"], 16 * (stt // 4 + 1))],
                inc=S["f_e2v"])


def emit_f5_rot_mix_out(E, S, T, P):
    hmm_total = E.cnt(S["f_hmm"])
    eomm_total = E.cnt(S["f_eomm"])
    e2v_total = E.cnt(S["f_e2v"])

    def rload(r, dstslice, sem, wait):
        E.do("gpsimd", lambda e: e.dma_start(
            dstslice, T["d_Rall"].ap()[r]),
            wait=wait, inc=sem, inc_amt=16)

    R0s = T["xnT"][:, 0:8192]
    R1s = T["xnT"][:, 8192:16384]
    ccw = (T["sem_cc"], 1)
    wgc_total = E.cnt(S["f_wgc"])
    rload(0, R0s, S["f_r0"], [(S["f_hmm"], hmm_total), (S["f_wgc"], wgc_total),
                              ccw])
    rload(1, R1s, S["f_r1"], [(S["f_hmm"], hmm_total), (S["f_wgc"], wgc_total),
                              ccw])
    for st in range(16):
        par = st % 2
        for dc in range(8):
            waits = []
            if dc == 0:
                waits = [(S["f_trfv"], st - 1),
                         (S["f_e2v"], 16 + 2 * (st + 1))]
            E.do("tensor", lambda e, st=st, dc=dc, par=par: e.transpose(
                P["PTf"][par][:, dc * 128:(dc + 1) * 128],
                T["macc"][:, st * 1024 + dc * 128: st * 1024 + (dc + 1) * 128],
                T["ident"][:, :]),
                wait=waits, inc=(S["f_trf"] if dc == 7 else None))
        E.do("vector", lambda e, st=st, par=par: e.tensor_copy(
            T["moeT"][:, :].rearrange("p (dc s) -> p dc s", dc=8)[
                :, :, st * 128:(st + 1) * 128],
            P["PTf"][par][:, :].rearrange("p (dc c) -> p dc c", dc=8)),
            wait=[(S["f_trf"], st + 1)], inc=S["f_trfv"])
    rsrc = [T["moeT"], T["rbufA"], T["rbufB"]]
    rdst = [T["rbufA"], T["rbufB"], T["rbufA"]]
    rsem = [S["f_r0"], S["f_r1"], S["f_r2"]]
    gidx = 0
    for r in range(3):
        if r == 1:
            rload(2, R0s, S["f_r2"], [(S["f_rot"], 32)])
        rR = R0s if r in (0, 2) else R1s
        for dcO in range(8):
            for c in range(4):
                par = gidx % 2
                for dc in range(8):
                    waits = []
                    if dc == 0:
                        waits = [(S["f_rotv"], gidx - 1), (rsem[r], 16)]
                        if r == 0 and dcO == 0 and c == 0:
                            waits.append((S["f_trfv"], 16))
                    E.do("tensor", lambda e, rR=rR, r=r, dcO=dcO, c=c, dc=dc,
                         par=par: e.matmul(
                        P["PEO"][par][:, :],
                        rR[:, dc * 1024 + dcO * 128: dc * 1024 + (dcO + 1) * 128],
                        rsrc[r][:, dc * 2048 + c * 512: dc * 2048 + (c + 1) * 512],
                        start=(dc == 0), stop=(dc == 7)),
                        wait=waits, inc=(S["f_rot"] if dc == 7 else None))
                E.do("vector", lambda e, r=r, dcO=dcO, c=c, par=par, gidx=gidx:
                     e.tensor_copy(
                    rdst[r][:, dcO * 2048 + c * 512: dcO * 2048 + (c + 1) * 512],
                    P["PEO"][par][:, :]),
                    wait=[(S["f_rot"], gidx + 1)], inc=S["f_rotv"])
                gidx += 1
    E.do("sync", lambda e: e.dma_start(
        T["abelT2"][:, :], T["d_abel"].ap()),
        wait=[(S["f_eomm"], eomm_total)], inc=S["f_ab2"], inc_amt=16)
    for c in range(4):
        for dcp in range(8):
            waits = []
            if dcp == 0:
                waits = [(S["f_glmv"], c)]
                if c == 0:
                    waits += [(S["f_rotv"], 96), (S["f_cstg"], 16 * N_CSTG)]
            fn = lambda e, c=c, dcp=dcp: e.matmul(
                P["PS1"][0:1, :], T["gc2_sb"][:, dcp:dcp + 1],
                T["rbufA"][:, dcp * 2048 + c * 512: dcp * 2048 + (c + 1) * 512],
                start=(dcp == 0), stop=(dcp == 7))
            if dcp == 7:
                mm_di(E, fn, wait=waits, inc=S["f_glmt"])
            else:
                E.do("tensor", fn, wait=waits)
        E.do("vector", lambda e, c=c: e.tensor_copy(
            T["glm"][0:1, c * 512:(c + 1) * 512], P["PS1"][0:1, :]),
            wait=[(S["f_glmt"], c + 1)], inc=S["f_glmv"])
    E.do("vector", lambda e: e.tensor_tensor(
        T["glm"][0:1, :], T["glm"][0:1, :], T["gla"][0:1, :], ALU.add),
        wait=[(S["f_glmv"], 4), (S["f_glav"], 4)], inc=S["f_gbv"])
    E.do("scalar", lambda e: e.activation(
        T["g_row"][0:1, :], T["glm"][0:1, :], AF.Sigmoid),
        wait=[(S["f_gbv"], 1)], inc=S["f_gate"])
    for c in range(4):
        waits = [(S["f_gate"], 17), (S["f_gbv"], c + 1)]
        if c == 0:
            waits.append((S["f_e2v"], e2v_total))
        mm_di(E, lambda e, c=c: e.matmul(
            P["PS2"][:, :], T["ones_bf"][0:1, :],
            T["g_row"][0:1, c * 512:(c + 1) * 512], start=True, stop=True),
            wait=waits, inc=S["f_gbt"])
        E.do("vector", lambda e, c=c: e.tensor_copy(
            T["gbc"][:, c * 512:(c + 1) * 512], P["PS2"][:, :]),
            wait=[(S["f_gbt"], c + 1)], inc=S["f_gbv"])
    for dc in range(8):
        E.do("vector", lambda e, dc=dc: e.tensor_tensor(
            T["mixtmp"][:, :], T["rbufA"][:, dc * 2048:(dc + 1) * 2048],
            T["abelT2"][:, dc * 2048:(dc + 1) * 2048], ALU.subtract),
            wait=([(S["f_gbv"], 5), (S["f_ab2"], 16)] if dc == 0 else []))
        E.drain("vector")
        E.do("vector", lambda e, dc=dc: e.tensor_tensor(
            T["mixtmp"][:, :], T["mixtmp"][:, :], T["gbc"][:, :], ALU.mult))
        E.drain("vector")
        E.do("vector", lambda e, dc=dc: e.tensor_tensor(
            T["rbufA"][:, dc * 2048:(dc + 1) * 2048],
            T["abelT2"][:, dc * 2048:(dc + 1) * 2048], T["mixtmp"][:, :],
            ALU.add), inc=S["f_mix"])
        E.drain("vector")
    x_t = T["x2s"]
    for st in range(8):
        xw = [(S["f_xin0"], 16 * st)]
        if st == 0:
            xw = [(S["f_rot"], 96), (S["f_at"], 32)]
        E.do("sync", lambda e, st=st: e.dma_start(
            x_t[st][:, :], T["d_x2"].ap()[st * 128:(st + 1) * 128, :]),
            wait=xw, inc=S["f_xin0"], inc_amt=16)
    for st in range(16):
        par = st % 2
        for dc in range(8):
            waits = []
            if dc == 0:
                waits = [(S["f_ov"], st - 1)]
                if st < 2:
                    waits.append((S["f_mix"], 8))
            E.do("tensor", lambda e, st=st, dc=dc, par=par: e.transpose(
                P["PTf"][par][:, dc * 128:(dc + 1) * 128],
                T["rbufA"][:, dc * 2048 + st * 128: dc * 2048 + st * 128 + 128],
                T["ident"][:, :]),
                wait=waits, inc=(S["f_ot"] if dc == 7 else None))
        E.do("vector", lambda e, st=st, par=par: e.tensor_tensor(
            T["outb"][par][:, :], x_t[st % 8][:, :],
            T["y2_sb"][:, st * 1024:(st + 1) * 1024], ALU.add),
            wait=[(S["f_ot"], st + 1), (S["f_xin0"], 16 * (st + 1)),
                  (S["f_out%d" % par], 16 * (st // 2))])
        E.drain("vector")
        E.do("vector", lambda e, st=st, par=par: e.tensor_tensor(
            T["outb"][par][:, :], T["outb"][par][:, :], P["PTf"][par][:, :],
            ALU.add), inc=S["f_ov"])
        E.do("sync", lambda e, st=st, par=par: e.dma_start(
            T["d_out"].ap()[st * 128:(st + 1) * 128, :], T["outb"][par][:, :]),
            wait=[(S["f_ov"], st + 1)], inc=S["f_out%d" % par], inc_amt=16)
        if st + 8 < 16:
            E.do("sync", lambda e, st=st: e.dma_start(
                x_t[st % 8][:, :],
                T["d_x2"].ap()[(st + 8) * 128:(st + 9) * 128, :]),
                wait=[(S["f_xin0"], 16 * (st + 8))], inc=S["f_xin0"],
                inc_amt=16)
    E.wait("sync", S["f_out0"], 16 * 8)
    E.wait("sync", S["f_out1"], 16 * 8)


def make_ffn_tensors(nc, es, T):
    def sb(name, shape, dt):
        return es.enter_context(nc.sbuf_tensor(name, shape, dt))

    def at(name, shape, dt, base, off=0):
        return nc.alloc_sbuf_tensor_at(
            name, shape, dt, offset=nc.lookup_mloc(base).addr + off)
    T["y2_sb"] = sb("y2_sb", [128, 16 * 1024], BF16)
    T["xnT"] = sb("xnT", [128, 8 * 2048], BF16)
    T["macc"] = sb("macc", [128, 16 * 1024], BF16)
    T["ew1h"] = [sb("ew1h%d" % k, [128, 8 * 1024], BF16) for k in range(2)]
    T["ew2h"] = [sb("ew2h%d" % k, [128, 8 * 1024], BF16) for k in range(2)]
    T["x2_tile"] = [sb("x2t%d" % k, [128, 1024], F32) for k in range(2)]
    T["outb"] = [sb("outb%d" % k, [128, 1024], F32) for k in range(2)]
    T["gbc"] = sb("gbc", [128, 2048], BF16)
    T["gamma_bc"] = sb("gamma_bc", [128, 1024], BF16)
    T["beta_bc"] = sb("beta_bc", [128, 1024], BF16)
    T["idxU"] = sb("idxU", [128, NCH * 8], mybir.dt.uint32)
    T["abelT"] = at("abelT", [128, 8 * 2048], BF16, T["macc"])
    T["moeT"] = at("moeT", [128, 8 * 2048], BF16, T["ew1h"][0])
    T["rbufB"] = T["moeT"]
    T["rbufA"] = at("rbufA", [128, 8 * 2048], BF16, T["macc"])
    T["abelT2"] = at("abelT2", [128, 8 * 2048], BF16, T["ew2h"][0])
    T["aw_sb"] = at("aw_sb", [128, 8 * 1024], BF16, T["x2_tile"][0])
    T["f_sqs"] = at("f_sqs", [128, 1024], F32, T["gbc"])
    T["f_sqs2"] = at("f_sqs2", [128, 1024], F32, T["ew1h"][1])
    T["x2bf"] = at("x2bf", [128, 16 * 1024], BF16, T["macc"])
    T["hcT"] = at("hcT", [128, 8 * 1024], BF16, T["macc"])
    T["eoc"] = at("eoc", [128, 8 * 1024], BF16, T["macc"], off=16384)
    T["glog"] = at("glog", [8, 2048], F32, T["macc"])
    T["ib2b"] = at("ib2b", [8, 2048], F32, T["macc"])
    T["ib1"] = at("ib1", [8, 2048], F32, T["macc"], off=8192)
    T["ib3"] = at("ib3", [8, 2048], F32, T["macc"], off=24576)
    T["gla"] = at("gla", [1, 2048], BF16, T["gamma_bc"])
    for n in ("fs_m16", "fs_n16", "fs_q16", "fs_v16", "fs_d16", "fs_r16"):
        T[n] = sb(n, [128, 16], F32)
    T["eps_t"] = sb("f_eps_t", [128, 1], F32)
    T["gam"] = sb("gam", [128, 8], F32)
    T["bet"] = sb("bet", [128, 8], F32)
    T["ab_b"] = sb("ab_b", [128, 8], F32)
    T["eb1_sb"] = sb("eb1_sb", [128, 128], F32)
    T["eb2_sb"] = sb("eb2_sb", [8, 1024], BF16)
    T["gw_sb"] = sb("gw_sb", [128, 64], BF16)
    T["gb_sb"] = sb("gb_sb", [8, 1], F32)
    T["gc1_sb"] = sb("gc1_sb", [128, 8], BF16)
    T["gc2_sb"] = sb("gc2_sb", [128, 8], BF16)
    T["gcb_sb"] = sb("gcb_sb", [1, 1], F32)
    T["ident"] = sb("f_ident", [128, 128], BF16)
    T["id8"] = sb("id8", [8, 8], F32)
    T["ones_bf"] = sb("ones_bf", [1, 128], BF16)
    T["ones_f"] = sb("ones_f", [1, 128], F32)
    T["wN"] = sb("wN", [128, 16 * 8], F32)
    T["wNb"] = sb("wNb", [128, 16 * 8], BF16)
    for n in ("g_m1", "g_nm1", "g_m2", "g_d0", "g_e2", "g_dn", "g_rdn"):
        T[n] = sb(n, [128, 1], F32)
    for n in ("g_msk", "g_lm", "g_e8", "g_mge", "g_t8"):
        T[n] = sb(n, [128, 8], F32)
    T["gath"] = at("gath", [128, NCH * 1032], BF16, T["xnT"])
    T["xncT"] = at("xncT", [128, 8 * CAP], BF16, T["xnT"], off=12416)
    T["MT"] = [at("mtb%d" % b, [128, CAP], BF16, T["xnT"], off=24704 + 1536 * b)
               for b in range(2)]
    T["xnn"] = [sb("xnn%d" % k, [128, 1024], BF16) for k in range(2)]
    T["zt"] = sb("zt", [128, 1032], BF16)
    T["mixtmp"] = at("mixtmp", [128, 2048], BF16, T["x2_tile"][0])
    T["glm"] = at("glm", [1, 2048], BF16, T["outb"][0])
    T["g_row"] = at("g_row", [1, 2048], BF16, T["outb"][1])
    T["gamrow"] = sb("gamrow", [1, 1024], F32)
    T["betrow"] = sb("betrow", [1, 1024], F32)
    T["wT"] = at("wT", [16, 2048], BF16, T["gbc"])
    T["slotsT"] = sb("slotsT", [128, 128], F32)
    T["vrow"] = at("vrow", [2, CAP], F32, T["x2_tile"][1], off=512)
    T["siota"] = at("siota", [128, CAP], F32, T["outb"][1])
    T["pse"] = sb("pse", [128, 2], F32)
    T["iocc"] = sb("iocc", [128, 1], F32)
    T["itmp"] = sb("itmp", [128, 1], F32)
    T["wgc"] = sb("wgc", [128, NCH * 8], F32)
    T["c_hilo_sb"] = sb("c_hilo_sb", [128, 32], BF16)
    T["x2s"] = [at("x2s%d" % k, [128, 1024], F32, T["ew1h"][0], off=4096 * k)
                for k in range(8)]
    T["c_spad_sb"] = sb("c_spad_sb", [128, 2], F32)
    return T


def make_ffn_psum(nc, es):
    P = {}
    P["PTf"] = [es.enter_context(nc.psum_tensor("PTf%d" % k, [128, 1024], BF16))
                for k in range(2)]
    P["PEO"] = [es.enter_context(nc.psum_tensor("PEOf%d" % k, [128, 512], F32))
                for k in range(2)]
    P["PH"] = [es.enter_context(nc.psum_tensor("PHf%d" % k, [128, 512], F32))
               for k in range(2)]
    P["PS1"] = es.enter_context(nc.psum_tensor("PS1f", [128, 512], F32))
    P["PS2"] = es.enter_context(nc.psum_tensor("PS2f", [128, 512], F32))
    return P



FACT = [1.0]
for _k in range(1, 13):
    FACT.append(FACT[-1] * _k)
C = [1.0 / f for f in FACT]

EXPM_SEM_NAMES = ["b_ld", "b_mt", "b_ev", "b_bl", "b_ro", "b_cc"]


def expm_sems(nc, es):
    return {n: Sem(nc, es, n) for n in EXPM_SEM_NAMES}


def emit_expm(E, S, T, P):
    E.do("sync", lambda e: e.dma_start(T["Irow"][:, :], T["c_irow"].ap()),
         inc=S["b_ro"], inc_amt=16)
    _emit_gen(E, S, T, P, 0)
    E.wait("sync", S["b_ro"], 16 + 16)


def emit_expm_allgather(E, S, T):
    E.do("gpsimd", lambda e: e.collective_compute(
        "AllGather", ALU.bypass, [list(range(8))],
        [T["d_Rpart"].ap().opt()], [T["d_Rall"].ap().opt()]),
        wait=[(S["b_ro"], 32)], inc=S["b_cc"])


def _build(E, S, T, dst, terms, gate_waits=(), inc=True, accumulate=False):
    E.drain("vector")
    first = not accumulate
    for k, (coef, mat) in enumerate(terms):
        w = list(gate_waits) if k == 0 else []
        src = T["Irow"] if mat is None else T[mat]
        if first:
            E.do("vector", lambda e, d=dst, s=src, c=coef: e.tensor_scalar_mul(
                d[:, :], s[:, :], c), wait=w)
        else:
            E.do("vector", lambda e, d=dst, s=src, c=coef: e.scalar_tensor_tensor(
                d[:, :], s[:, :], c, d[:, :], ALU.mult, ALU.add))
        E.drain("vector")
        first = False
    if inc:
        E.do("vector", lambda e: e.engine_nop(), inc=S["b_bl"])


def _product(E, S, T, P, g, pi, lhs_name, rhs_name, dst_name, add_name=None,
             lhs_waits=(), rhs_ev=None, out_dma=None):
    lhs, rhs, dst = T[lhs_name], T[rhs_name], T[dst_name]
    base_g = g * 80
    for rm in range(8):
        for q in range(2):
            gi = pi * 16 + rm * 2 + q
            par = gi % 2
            for dc in range(8):
                waits = []
                if dc == 0:
                    waits = [(S["b_ev"], base_g + gi - 1)]
                    if rm == 0 and q == 0:
                        waits += list(lhs_waits)
                        if rhs_ev is not None:
                            waits.append((S["b_ev"], rhs_ev))
                elif pi == 0 and rm == 0 and q == 0:
                    waits = [(S["b_bl"], dc + 1)]
                E.do("tensor", lambda e, rm=rm, q=q, dc=dc, par=par: e.matmul(
                    P["PB"][par][:, :],
                    lhs[:, dc * 1024 + rm * 128: dc * 1024 + (rm + 1) * 128],
                    rhs[:, dc * 1024 + q * 512: dc * 1024 + q * 512 + 512],
                    start=(dc == 0), stop=(dc == 7)),
                    wait=waits, inc=(S["b_mt"] if dc == 7 else None))
            dsl = dst[:, rm * 1024 + q * 512: rm * 1024 + q * 512 + 512]
            evw = [(S["b_mt"], base_g + gi + 1)]
            if pi == 0 and rm == 0 and q == 0:
                evw.append((S["b_ro"], 16 + 16 * g))
            if add_name is None:
                E.do("vector", lambda e, d=dsl, par=par: e.tensor_copy(
                    d, P["PB"][par][:, :]), wait=evw, inc=S["b_ev"])
            else:
                asl = T[add_name][:, rm * 1024 + q * 512: rm * 1024 + q * 512 + 512]
                E.do("vector", lambda e, d=dsl, a=asl, par=par: e.tensor_tensor(
                    d, P["PB"][par][:, :], a, ALU.add), wait=evw, inc=S["b_ev"])
    if out_dma is not None:
        E.do("sync", lambda e: e.dma_start(T[out_dma].ap(), dst[:, :]),
             wait=[(S["b_ev"], base_g + (pi + 1) * 16)], inc=S["b_ro"], inc_amt=16)


def _emit_gen(E, S, T, P, g):
    lw = []
    if g > 0:
        lw = [(S["b_mt"], g * 80), (S["b_bl"], g * 3)]
    for dc in range(8):
        w = list(lw) if dc == 0 else [(S["b_ld"], 16 * dc)]
        E.do("gpsimd", lambda e, dc=dc: e.dma_start(
            T["Gb"][:, dc * 1024:(dc + 1) * 1024],
            T["d_mygen"].ap()[dc * 128:(dc + 1) * 128, :]),
            wait=w, inc=S["b_ld"], inc_amt=16)
        E.do("vector", lambda e, dc=dc: e.tensor_scalar_mul(
            T["GTb"][:, dc * 1024:(dc + 1) * 1024],
            T["Gb"][:, dc * 1024:(dc + 1) * 1024], -1.0),
            wait=[(S["b_ld"], 16 * (dc + 1))], inc=S["b_bl"])
    ldw = [(S["b_ld"], 16 * 8)]
    base_ev = g * 80
    _product(E, S, T, P, g, 0, "GTb", "Gb", "G2b", lhs_waits=[(S["b_bl"], 1)])
    _product(E, S, T, P, g, 1, "GTb", "G2b", "G3b", rhs_ev=base_ev + 16)
    bw = [(S["b_ro"], 16)]
    if g > 0:
        bw.append((S["b_mt"], g * 80 - 16))
    _build(E, S, T, T["bufA"],
           [(C[8], None), (-C[9], "Gb"), (C[10], "G2b"), (-C[11], "G3b")],
           gate_waits=bw, inc=False)
    _build(E, S, T, T["bufB"],
           [(C[4], None), (-C[5], "Gb"), (C[6], "G2b"), (-C[7], "G3b")])
    _product(E, S, T, P, g, 2, "G2b", "G2b", "G4b", rhs_ev=base_ev + 16)
    _build(E, S, T, T["bufA"], [(C[12], "G4b")], accumulate=True)
    _product(E, S, T, P, g, 3, "G4b", "bufA", "bufD", add_name="bufB",
             lhs_waits=[(S["b_bl"], g * 3 + 2 + 8)], rhs_ev=base_ev + 48)
    _build(E, S, T, T["bufB"],
           [(C[0], None), (C[1], "Gb"), (C[2], "G2b"), (C[3], "G3b")])
    _product(E, S, T, P, g, 4, "bufD", "G4b", "Rout", add_name="bufB",
             lhs_waits=[(S["b_bl"], g * 3 + 3 + 8), (S["b_ev"], base_ev + 64)],
             out_dma="d_Rpart")


def make_expm_tensors(nc, es, T):
    def sb(name, shape, dt):
        return es.enter_context(nc.sbuf_tensor(name, shape, dt))
    for n in ("Gb", "GTb", "G2b", "G3b", "G4b", "bufA", "bufB", "bufD", "Irow"):
        T[n] = sb(n, [128, 8 * 1024], BF16)
    T["Rout"] = nc.alloc_sbuf_tensor_at(
        "Rout", [128, 8 * 1024], BF16, offset=nc.lookup_mloc(T["G2b"]).addr)
    return T


def make_expm_psum(nc, es):
    return {"PB": [es.enter_context(nc.psum_tensor("PBe%d" % k, [128, 512], F32))
                   for k in range(2)]}



def build_program():
    nc = bass.Bass("TRN2", target_bir_lowering=False)
    T = {}
    T["d_x"] = nc.dram_tensor("x", [2048, 1024], F32, kind="ExternalInput")
    for w in ("Wq", "Wk", "Wv", "Wo"):
        T["d_" + w] = nc.dram_tensor(w, [1024, 1024], F32, kind="ExternalInput")
    T["d_bo"] = nc.dram_tensor("bo", [1024], F32, kind="ExternalInput")
    T["d_ln_gamma_t"] = nc.dram_tensor("ln_gamma_t", [128, 8], F32, kind="ExternalInput")
    T["d_ln_beta_t"] = nc.dram_tensor("ln_beta_t", [128, 8], F32, kind="ExternalInput")
    T["d_ln_gamma"] = nc.dram_tensor("ln_gamma", [1024], F32, kind="ExternalInput")
    T["d_ln_beta"] = nc.dram_tensor("ln_beta", [1024], F32, kind="ExternalInput")
    T["d_abel_w"] = nc.dram_tensor("abel_w", [1024, 1024], F32, kind="ExternalInput")
    T["d_abel_b_t"] = nc.dram_tensor("abel_b_t", [128, 8], F32, kind="ExternalInput")
    T["d_gate_w_t"] = nc.dram_tensor("gate_w_t", [128, 64], F32, kind="ExternalInput")
    T["d_gate_b"] = nc.dram_tensor("moe_gate_b", [8], F32, kind="ExternalInput")
    T["d_ew1"] = nc.dram_tensor("ew1", [8, 1024, 2048], F32, kind="ExternalInput")
    T["d_eb1_t"] = nc.dram_tensor("eb1_t", [128, 128], F32, kind="ExternalInput")
    T["d_ew2"] = nc.dram_tensor("ew2", [8, 2048, 1024], F32, kind="ExternalInput")
    T["d_eb2"] = nc.dram_tensor("eb2", [8, 1024], F32, kind="ExternalInput")
    T["d_gc1_t"] = nc.dram_tensor("gc1_t", [128, 8], F32, kind="ExternalInput")
    T["d_gc2_t"] = nc.dram_tensor("gc2_t", [128, 8], F32, kind="ExternalInput")
    T["d_gc_b"] = nc.dram_tensor("gc_b", [1], F32, kind="ExternalInput")
    T["d_mygen"] = nc.dram_tensor("mygen", [1024, 1024], F32, kind="ExternalInput")
    T["c_alibi"] = nc.dram_tensor("c_alibi", [128, 640], F32, kind="ExternalInput")
    T["c_mask"] = nc.dram_tensor("c_mask", [128, 2048], BF16, kind="ExternalInput")
    T["c_ident"] = nc.dram_tensor("c_ident", [128, 128], BF16, kind="ExternalInput")
    T["c_ones"] = nc.dram_tensor("c_ones", [128], F32, kind="ExternalInput")
    T["c_id8"] = nc.dram_tensor("c_id8", [8, 8], F32, kind="ExternalInput")
    T["c_ones_bf"] = nc.dram_tensor("c_ones_bf", [128], BF16, kind="ExternalInput")
    T["c_irow"] = nc.dram_tensor("c_irow", [128, 8192], BF16, kind="ExternalInput")
    T["d_c_hilo"] = nc.dram_tensor("c_hilo", [128, 32], BF16, kind="ExternalInput")
    T["d_c_spad"] = nc.dram_tensor("c_spad", [128, 2], F32, kind="ExternalInput")
    T["d_c_siota"] = nc.dram_tensor("c_siota", [128, 768], F32, kind="ExternalInput")
    T["c_ones_f"] = T["c_ones"]
    T["d_x2"] = nc.dram_tensor("x2_scr", [2048, 1024], F32, kind="Internal")
    T["d_xn"] = nc.dram_tensor("xn_scr", [2176, 1032], BF16, kind="Internal")
    T["d_moe"] = nc.dram_tensor("moe_scr", [2176, 1024], BF16, kind="Internal")
    T["d_abel"] = nc.dram_tensor("abel_scr", [128, 16384], BF16, kind="Internal")
    T["d_Rpart"] = nc.dram_tensor("Rpart_scr", [128, 8192], BF16, kind="Internal")
    T["d_Rall"] = nc.dram_tensor("Rall_scr", [8, 128, 8192], BF16, kind="Internal")
    T["d_out"] = nc.dram_tensor("out", [2048, 1024], F32, kind="ExternalOutput")

    sem_stack = ExitStack()
    SB = expm_sems(nc, sem_stack)
    SA = attn_sems(nc, sem_stack)
    SF = ffn_sems(nc, sem_stack)
    all_sems = list(SB.values()) + list(SA.values()) + list(SF.values())

    es0 = ExitStack()
    make_expm_tensors(nc, es0, T)
    P0 = make_expm_psum(nc, es0)
    with nc.Block() as block:
        for name in ENGINES:
            def body(eng, name=name):
                E = Emitter(nc, name, eng, all_sems)
                emit_expm(E, SB, T, P0)
            getattr(block, name)(body)
    es0.close()

    T["sem_cc"] = SB["b_cc"]
    es1 = ExitStack()
    make_attn_tensors(nc, es1, T)
    P1 = make_attn_psum(nc, es1)
    with nc.Block() as block:
        for name in ENGINES:
            def body(eng, name=name):
                E = Emitter(nc, name, eng, all_sems)
                emit_attention(E, SA, T, P1,
                               extra=lambda EE: emit_expm_allgather(EE, SB, T))
            getattr(block, name)(body)
    es1.close()

    es2 = ExitStack()
    make_ffn_tensors(nc, es2, T)
    P2 = make_ffn_psum(nc, es2)
    with nc.Block() as block:
        for name in ENGINES:
            def body(eng, name=name):
                E = Emitter(nc, name, eng, all_sems)
                emit_ffn(E, SF, T, P2)
            getattr(block, name)(body)
    es2.close()
    sem_stack.close()
    return nc


def _host_irow():
    import ml_dtypes
    ir = np.zeros((128, 8192), dtype=np.float32)
    p = np.arange(128)
    for dc in range(8):
        ir[p, dc * 1024 + dc * 128 + p] = 1.0
    return np.asarray(ir.astype(ml_dtypes.bfloat16))


def _host_hilo():
    import ml_dtypes
    p = np.arange(128)
    hilo = np.zeros((128, 32), dtype=np.float32)
    for tb in range(16):
        hilo[:, 2 * tb] = tb * 8 + (p >> 4) + 1
        hilo[:, 2 * tb + 1] = p & 15
    return np.asarray(hilo.astype(ml_dtypes.bfloat16))


_PROGRAM_CACHE = {}


def kernel(**inputs):
    import ml_dtypes
    x_full = np.ascontiguousarray(np.asarray(inputs["L0_attn_x"], dtype=np.float32))
    gens = np.ascontiguousarray(np.asarray(inputs["L0_ffn_gens"], dtype=np.float32))
    common = dict(
        Wq=np.ascontiguousarray(inputs["L0_attn_Wq"], dtype=np.float32),
        Wk=np.ascontiguousarray(inputs["L0_attn_Wk"], dtype=np.float32),
        Wv=np.ascontiguousarray(inputs["L0_attn_Wv"], dtype=np.float32),
        Wo=np.ascontiguousarray(inputs["L0_attn_Wo"], dtype=np.float32),
        bo=np.ascontiguousarray(inputs["L0_attn_bo"], dtype=np.float32),
        abel_w=np.ascontiguousarray(inputs["L0_ffn_abel_w"], dtype=np.float32),
        moe_gate_b=np.ascontiguousarray(inputs["L0_ffn_moe_gate_b"], dtype=np.float32),
        ew1=np.ascontiguousarray(inputs["L0_ffn_ew1"], dtype=np.float32),
        ew2=np.ascontiguousarray(inputs["L0_ffn_ew2"], dtype=np.float32),
        eb2=np.ascontiguousarray(inputs["L0_ffn_eb2"], dtype=np.float32),
        gc_b=np.ascontiguousarray(inputs["L0_ffn_gc_b"], dtype=np.float32),

        ln_gamma=np.ascontiguousarray(inputs["L0_ffn_ln_gamma"], dtype=np.float32),
        ln_beta=np.ascontiguousarray(inputs["L0_ffn_ln_beta"], dtype=np.float32),
        ln_gamma_t=np.ascontiguousarray(
            np.asarray(inputs["L0_ffn_ln_gamma"], dtype=np.float32).reshape(8, 128).T),
        ln_beta_t=np.ascontiguousarray(
            np.asarray(inputs["L0_ffn_ln_beta"], dtype=np.float32).reshape(8, 128).T),
        abel_b_t=np.ascontiguousarray(
            np.asarray(inputs["L0_ffn_abel_b"], dtype=np.float32).reshape(8, 128).T),
        gate_w_t=np.ascontiguousarray(
            np.asarray(inputs["L0_ffn_moe_gate_w"], dtype=np.float32)
            .reshape(8, 128, 8).transpose(1, 0, 2).reshape(128, 64)),
        eb1_t=np.ascontiguousarray(
            np.asarray(inputs["L0_ffn_eb1"], dtype=np.float32)
            .reshape(8, 2, 8, 128).transpose(3, 0, 1, 2).reshape(128, 128)),
        gc1_t=np.ascontiguousarray(
            np.asarray(inputs["L0_ffn_gc_w"], dtype=np.float32)[:1024, 0]
            .reshape(8, 128).T),
        gc2_t=np.ascontiguousarray(
            np.asarray(inputs["L0_ffn_gc_w"], dtype=np.float32)[1024:, 0]
            .reshape(8, 128).T),
        c_irow=_host_irow(),
        c_hilo=_host_hilo(),
        c_spad=np.stack([2064.0 + np.arange(128), 2048.0 + np.arange(128)],
                        axis=1).astype(np.float32),
        c_siota=np.broadcast_to(
            np.arange(768, dtype=np.float32), (128, 768)).copy(),
        c_id8=np.eye(8, dtype=np.float32),
        c_ones_bf=np.ones(128, dtype=np.float32).astype(ml_dtypes.bfloat16),
        **host_consts(),
    )
    in_maps = [dict(common, x=np.ascontiguousarray(x_full[b]),
                    mygen=np.ascontiguousarray(gens[b % 3]))
               for b in range(8)]
    if "nc" not in _PROGRAM_CACHE:
        _PROGRAM_CACHE["nc"] = build_program()
    nc = _PROGRAM_CACHE["nc"]
    res = run_bass_kernel_spmd(nc, in_maps, core_ids=list(range(8)))
    return np.stack([np.asarray(res.results[b]["out"]) for b in range(8)], axis=0)


# version 14
# speedup vs baseline: 1.0028x; 1.0028x over previous
import math
from contextlib import ExitStack
import numpy as np
import concourse.bass as bass
import concourse.mybir as mybir
from concourse.bass_utils import run_bass_kernel_spmd


F32 = mybir.dt.float32
BF16 = mybir.dt.bfloat16
AF = mybir.ActivationFunctionType
ALU = mybir.AluOpType
AX = mybir.AxisListType

ENGINES = ("sync", "tensor", "vector", "scalar", "gpsimd")


class Sem:

    def __init__(self, nc, es, name):
        self.h = es.enter_context(nc.semaphore(name))
        self.name = name


class Emitter:
    def __init__(self, nc, me, eng, sems):
        self.nc = nc
        self.me = me
        self.eng = eng
        self.n = {s.name: 0 for s in sems}

    def cnt(self, sem):
        return self.n[sem.name]

    def bump(self, sem, amt):
        self.n[sem.name] += amt
        return self.n[sem.name]

    def do(self, engine, fn, wait=(), inc=None, inc_amt=1):
        if self.me == engine:
            for s, v in wait:
                if v > 0:
                    self.eng.wait_ge(s.h, v)
            ins = fn(self.eng)
            if inc is not None:
                ins.then_inc(inc.h, inc_amt)
        if inc is not None:
            self.bump(inc, inc_amt)

    def wait(self, engine, sem, value):
        if self.me == engine and value > 0:
            self.eng.wait_ge(sem.h, value)

    def drain(self, engine):
        if self.me == engine:
            self.eng.drain()


def ceil_div(a, b):
    return (a + b - 1) // b


def alibi_slopes(n):
    def pow2(n):
        start = 2 ** (-(2 ** -(math.log2(n) - 3)))
        return [start * start ** i for i in range(n)]
    if math.log2(n).is_integer():
        return pow2(n)
    c = 2 ** math.floor(math.log2(n))
    return pow2(c) + alibi_slopes(2 * c)[0::2][: n - c]


def host_consts(H=16, N=2048, HD=64):
    import ml_dtypes
    slopes = np.asarray(alibi_slopes(H), dtype=np.float64)
    ali = np.zeros((128, H * 40), dtype=np.float32)
    p = np.arange(128)
    off = [0, 4, 12, 24]
    for h in range(H):
        for qc in range(4):
            for kt in range(4 * (qc + 1)):
                ali[:, h * 40 + off[qc] + kt] = (
                    slopes[h] * (kt * 128 + p - qc * 512) / 8.0
                ).astype(np.float32)
    msk = np.zeros((128, 4 * 512), dtype=np.float32)
    f = np.arange(512)
    for j in range(4):
        m = (f[None, :] < (p[:, None] + 128 * j)).astype(np.float32) * (-1e30)
        msk[:, j * 512:(j + 1) * 512] = m
    ident = np.eye(128).astype(ml_dtypes.bfloat16)
    ones = np.ones((128,), dtype=np.float32)
    return dict(c_alibi=ali, c_mask=msk.astype(ml_dtypes.bfloat16),
                c_ident=np.asarray(ident), c_ones=ones)



EPS = 1e-5

ATTN_SEM_NAMES = [
    "s_xin0", "s_xin1", "s_v3", "s_ln", "s_v4", "s_tr", "s_ev",
    "s_w", "s_a2t", "s_a2v",
    "s_sc", "s_msk", "s_exp", "s_avm", "s_rcp", "s_bc", "s_bcc", "s_nrm",
    "s_const", "s_wot", "s_x2v", "s_x2o0", "s_x2o1", "s_bob", "s_bobv", "s_eps",
    "s_xbf", "s_sq", "s_ve", "s_sd"]


def attn_sems(nc, es):
    return {n: Sem(nc, es, n) for n in ATTN_SEM_NAMES}


def emit_attention(E, S, T, P, extra=None):
    emit_a0_consts(E, S, T)
    emit_a1_ln_transpose(E, S, T, P)
    if extra is not None:
        extra(E)
    emit_a2_qkv(E, S, T, P)
    emit_a3_scores(E, S, T, P)
    emit_a4_out(E, S, T, P)


def emit_a0_consts(E, S, T):
    for q in range(4):
        E.do("gpsimd", lambda e, q=q: e.dma_start(
            T["xbf"][:, q * 4096:(q + 1) * 4096].rearrange(
                "p (st c) -> p st c", st=4),
            T["d_x"].ap()[q * 512:(q + 1) * 512, :].rearrange(
                "(st p) c -> p st c", p=128)),
            wait=([(S["s_xbf"], 16 * q)] if q > 0 else []),
            inc=S["s_xbf"], inc_amt=16)
    s_const = S["s_const"]
    E.do("sync", lambda e: e.dma_start(T["alibi"][:, :], T["c_alibi"].ap()),
         inc=s_const, inc_amt=16)
    E.do("sync", lambda e: e.dma_start(T["ident"][:, :], T["c_ident"].ap()),
         inc=s_const, inc_amt=16)
    E.do("sync", lambda e: e.dma_start(T["mask"][:, :], T["c_mask"].ap()),
         inc=s_const, inc_amt=16)
    E.do("sync", lambda e: e.dma_start(
        T["ones"][:, :], T["c_ones"].ap().rearrange("(o f) -> o f", o=1)),
        inc=s_const, inc_amt=16)
    E.do("sync", lambda e: e.dma_start(
        T["bo_row"][:, :], T["d_bo"].ap().rearrange("(o f) -> o f", o=1)),
        inc=s_const, inc_amt=16)
    E.do("gpsimd", lambda e: e.dma_start(
        T["W_sb"][:, :].rearrange("p (dc c) -> p dc c", dc=8),
        T["d_Wk"].ap().rearrange("(dc p) c -> p dc c", p=128)),
        inc=S["s_w"], inc_amt=16)


def emit_a1_ln_transpose(E, S, T, P):
    y1 = T["y1_tile"]
    E.do("vector", lambda e: e.memset(T["eps_t"][:, :], EPS), inc=S["s_eps"])
    for q in range(4):
        qs = slice(q * 4, (q + 1) * 4)
        for st in range(q * 4, q * 4 + 4):
            E.do("vector", lambda e, st=st: e.reduce_sum(
                T["st_m16"][:, st:st + 1],
                T["xbf"][:, st * 1024:(st + 1) * 1024], axis=AX.X),
                wait=([(S["s_xbf"], 16 * (q + 1))] if st % 4 == 0 else []))
            E.do("scalar", lambda e, st=st: e.activation(
                T["sq_scratch"][st % 2][:, :],
                T["xbf"][:, st * 1024:(st + 1) * 1024],
                AF.Square, accum_out=T["st_q16"][:, st:st + 1]),
                wait=([(S["s_xbf"], 16 * (q + 1))] if st % 4 == 0 else []),
                inc=(S["s_sq"] if st % 4 == 3 else None))
            E.drain("scalar")
        E.drain("vector")
        E.do("vector", lambda e, qs=qs: e.tensor_scalar_mul(
            T["st_m16"][:, qs], T["st_m16"][:, qs], 1.0 / 1024))
        E.drain("vector")
        E.do("vector", lambda e, qs=qs: e.tensor_scalar_mul(
            T["st_n16"][:, qs], T["st_m16"][:, qs], -1.0))
        E.drain("vector")
        E.do("vector", lambda e, qs=qs: e.tensor_tensor(
            T["st_v16"][:, qs], T["st_n16"][:, qs], T["st_m16"][:, qs],
            ALU.mult))
        E.drain("vector")
        E.do("vector", lambda e, qs=qs: e.scalar_tensor_tensor(
            T["st_v16"][:, qs], T["st_q16"][:, qs], 1.0 / 1024,
            T["st_v16"][:, qs], ALU.mult, ALU.add),
            wait=[(S["s_sq"], q + 1)], inc=S["s_ve"])
        E.do("scalar", lambda e, qs=qs: e.activation(
            T["st_d16"][:, qs], T["st_v16"][:, qs], AF.Sqrt,
            bias=T["eps_t"][:, :]),
            wait=[(S["s_ve"], q + 1), (S["s_eps"], 1)], inc=S["s_sd"])
        E.do("vector", lambda e, qs=qs: e.reciprocal(
            T["st_r16"][:, qs], T["st_d16"][:, qs]),
            wait=[(S["s_sd"], q + 1)])
        E.drain("vector")
        for st in range(q * 4, q * 4 + 4):
            par = st % 2
            E.do("vector", lambda e, st=st, par=par: e.tensor_scalar(
                y1[par][:, :], T["xbf"][:, st * 1024:(st + 1) * 1024],
                T["st_n16"][:, st:st + 1], T["st_r16"][:, st:st + 1],
                ALU.add, ALU.mult),
                wait=[(S["s_tr"], st - 1)], inc=S["s_v4"])
            for dc in range(8):
                E.do("tensor", lambda e, dc=dc, par=par: e.transpose(
                    P["PT"][par][:, dc * 128:(dc + 1) * 128],
                    y1[par][:, dc * 128:(dc + 1) * 128], T["ident"][:, :]),
                    wait=([(S["s_v4"], st + 1), (S["s_ev"], st - 1),
                           (S["s_const"], 80)] if dc == 0 else []),
                    inc=(S["s_tr"] if dc == 7 else None))
            E.do("vector", lambda e, st=st, par=par: e.tensor_copy(
                T["y1T"][:, :].rearrange("p (dc s) -> p dc s", dc=8)[
                    :, :, st * 128:(st + 1) * 128],
                P["PT"][par][:, :].rearrange("p (dc c) -> p dc c", dc=8)),
                wait=[(S["s_tr"], st + 1)], inc=S["s_ev"])


def _qkv_groups():
    gs = []
    for hp in range(8):
        for c in range(4):
            gs.append(("kt", hp, c))
    for st in range(16):
        for half in range(2):
            gs.append(("vn", st, half))
    for hp in range(8):
        for c in range(4):
            gs.append(("qt", hp, c))
    return gs


def emit_a2_qkv(E, S, T, P):
    W, y1T = T["W_sb"], T["y1T"]
    groups = _qkv_groups()
    for gi, g in enumerate(groups):
        kind = g[0]
        par = gi % 2
        psum = P["P01"][par]
        if kind == "vn" and g[1] == 0 and g[2] == 0:
            E.do("gpsimd", lambda e: e.dma_start(
                T["W_sb"][:, :].rearrange("p (dc c) -> p dc c", dc=8),
                T["d_Wv"].ap().rearrange("(dc p) c -> p dc c", p=128)),
                wait=[(S["s_a2t"], 32)], inc=S["s_w"], inc_amt=16)
        if kind == "qt" and g[1] == 0 and g[2] == 0:
            E.do("gpsimd", lambda e: e.dma_start(
                T["W_sb"][:, :].rearrange("p (dc c) -> p dc c", dc=8),
                T["d_Wq"].ap().rearrange("(dc p) c -> p dc c", p=128)),
                wait=[(S["s_a2t"], 64)], inc=S["s_w"], inc_amt=16)
        wload = 16 if kind == "kt" else (32 if kind == "vn" else 48)
        for dc in range(8):
            if kind in ("kt", "qt"):
                _, hp, c = g
                lhs = W[:, dc * 1024 + hp * 128: dc * 1024 + (hp + 1) * 128]
                rhs = y1T[:, dc * 2048 + c * 512: dc * 2048 + (c + 1) * 512]
            else:
                _, st, half = g
                lhs = y1T[:, dc * 2048 + st * 128: dc * 2048 + st * 128 + 128]
                rhs = W[:, dc * 1024 + half * 512: dc * 1024 + half * 512 + 512]
            waits = []
            if dc == 0:
                waits = [(S["s_w"], wload), (S["s_a2v"], gi - 1)]
                if gi == 0:
                    waits.append((S["s_ev"], 16))
            E.do("tensor", lambda e, o=psum, l=lhs, r=rhs, dc=dc: e.matmul(
                o[:, :], l, r, start=(dc == 0), stop=(dc == 7)),
                wait=waits, inc=(S["s_a2t"] if dc == 7 else None))
        if kind in ("kt", "qt"):
            _, hp, c = g
            dst_t = T["KT"] if kind == "kt" else T["QT"]
            dst = dst_t[:, hp * 2048 + c * 512: hp * 2048 + (c + 1) * 512]
            E.do("vector", lambda e, d=dst, o=psum: e.tensor_copy(d, o[:, :]),
                 wait=[(S["s_a2t"], gi + 1)], inc=S["s_a2v"])
        else:
            _, st, half = g
            base = st * 1040 + half * 520
            dst = T["VN"][:, base: base + 520].rearrange(
                "p (h j) -> p h j", h=8)[:, :, 0:64]
            E.do("vector", lambda e, d=dst, o=psum: e.tensor_copy(
                d, o[:, :].rearrange("p (h j) -> p h j", h=8)),
                wait=[(S["s_a2t"], gi + 1)], inc=S["s_a2v"])
    E.do("vector", lambda e: e.memset(
        T["VN"][:, :].rearrange("p (g j) -> p g j", j=65)[:, :, 64:65], 1.0),
        inc=S["s_a2v"])


def _a3_iters():
    return [(h, qc) for h in range(16) for qc in range(4)]


def emit_a3_scores(E, S, T, P):
    iters = _a3_iters()
    KT, QT, VN = T["KT"], T["QT"], T["VN"]
    tiles = []
    sc_count, msk_count, exp_count, avm_count = {}, {}, {}, {}
    n_sc = n_msk = n_exp = n_avm = 0
    for i, (h, qc) in enumerate(iters):
        for j in range(4 * (qc + 1)):
            t = (i, j)
            tiles.append(t)
            n_sc += 1
            sc_count[t] = n_sc
            if j >= 4 * qc:
                n_msk += 1
            msk_count[t] = n_msk
            n_exp += 1
            exp_count[t] = n_exp
            n_avm += 1
            avm_count[t] = n_avm
    tile_idx = {t: n for n, t in enumerate(tiles)}

    def prev_tile(t, back):
        n = tile_idx[t] - back
        return tiles[n] if n >= 0 else None

    def attnv(i, h, qc, j, K):
        t = (i, j)
        waits = [(S["s_exp"], exp_count[t])]
        if j == 0:
            waits.append((S["s_nrm"], i - 1))
        E.do("tensor", lambda e, t=t, j=j, i=i, h=h: e.matmul(
            P["PC"][i % 2][:, :],
            VN[:, j * 1040 + h * 65: j * 1040 + (h + 1) * 65],
            T["expb"][tile_idx[t] % 3][:, :],
            start=(j == 0), stop=(j == K - 1)),
            wait=waits, inc=S["s_avm"])

    for i, (h, qc) in enumerate(iters):
        po = 64 * (h % 2)
        hp = h // 2
        K = 4 * (qc + 1)
        for j in range(K):
            t = (i, j)
            pt = prev_tile(t, 3)
            waits = [(S["s_exp"], exp_count[pt] if pt else 0)]
            if i == 0 and j == 0:
                waits.append((S["s_a2v"], 97))
                waits.append((S["s_const"], 80))
            lhs = KT[po:po + 64, hp * 2048 + j * 128: hp * 2048 + (j + 1) * 128]
            rhs = QT[po:po + 64, hp * 2048 + qc * 512: hp * 2048 + (qc + 1) * 512]
            E.do("tensor", lambda e, l=lhs, r=rhs, t=t: e.matmul(
                P["P01"][tile_idx[t] % 3][:, :], l, r, start=True, stop=True),
                wait=waits, inc=S["s_sc"])
            if j == 1 and i >= 1:
                E.do("tensor", lambda e, i=i: e.matmul(
                    P["PB"][0][:, :], T["ones"][0:1, 0:64],
                    T["rd"][(i - 1) % 2][:, :], start=True, stop=True),
                    wait=[(S["s_rcp"], i), (S["s_bcc"], i - 1)], inc=S["s_bc"])
            if j >= 2:
                attnv(i, h, qc, j - 2, K)
        attnv(i, h, qc, K - 2, K)
        attnv(i, h, qc, K - 1, K)

        if i >= 1:
            _normalize(E, S, T, P, i - 1)
        for j in range(4 * qc, K):
            jj = j - 4 * qc
            E.do("vector", lambda e, j=j, jj=jj, i=i, t=(i, j): e.tensor_tensor(
                P["P01"][tile_idx[t] % 3][:, :], P["P01"][tile_idx[t] % 3][:, :],
                T["mask"][:, jj * 512:(jj + 1) * 512], ALU.add),
                wait=[(S["s_sc"], sc_count[(i, j)])], inc=S["s_msk"])
        E.do("vector", lambda e, i=i: e.reciprocal(
            T["rd"][i % 2][:, :], P["PC"][i % 2][64:65, :]),
            wait=[(S["s_avm"], avm_count[(i, K - 1)]), (S["s_bc"], i - 1)],
            inc=S["s_rcp"])

        if i >= 1:
            E.do("scalar", lambda e, i=i: e.activation(
                T["bc_sb"][(i - 1) % 2][:, :], P["PB"][0][:, :],
                AF.Identity),
                wait=[(S["s_bc"], i), (S["s_nrm"], i - 2)], inc=S["s_bcc"])
        for j in range(K):
            t = (i, j)
            idx = h * 40 + [0, 4, 12, 24][qc] + j
            diag = j >= 4 * qc
            w = [(S["s_msk"], msk_count[t])] if diag else [(S["s_sc"], sc_count[t])]
            pt = prev_tile(t, 3)
            if pt:
                w.append((S["s_avm"], avm_count[pt]))
            E.do("scalar", lambda e, j=j, idx=idx, t=t: e.activation(
                T["expb"][tile_idx[t] % 3][:, :], P["P01"][tile_idx[t] % 3][:, :],
                AF.Exp, bias=T["alibi"][:, idx:idx + 1], scale=0.125),
                wait=w, inc=S["s_exp"])
    i_last = len(iters) - 1
    E.do("tensor", lambda e: e.matmul(
        P["PB"][0][:, :], T["ones"][0:1, 0:64],
        T["rd"][i_last % 2][:, :], start=True, stop=True),
        wait=[(S["s_rcp"], i_last + 1), (S["s_bcc"], i_last)], inc=S["s_bc"])
    E.do("scalar", lambda e: e.activation(
        T["bc_sb"][i_last % 2][:, :], P["PB"][0][:, :], AF.Identity),
        wait=[(S["s_bc"], i_last + 1), (S["s_nrm"], i_last - 1)], inc=S["s_bcc"])
    _normalize(E, S, T, P, i_last)


def _normalize(E, S, T, P, i):
    h, qc = _a3_iters()[i]
    po = 64 * (h % 2)
    hp = h // 2
    dst = T["ctxT"][po:po + 64, hp * 2048 + qc * 512: hp * 2048 + (qc + 1) * 512]
    E.do("vector", lambda e, d=dst, i=i: e.tensor_tensor(
        d, P["PC"][i % 2][0:64, :], T["bc_sb"][i % 2][:, :], ALU.mult),
        wait=[(S["s_bcc"], i + 1)], inc=S["s_nrm"])


def emit_a4_out(E, S, T, P):
    n_iters = len(_a3_iters())
    exp_total = E.cnt(S["s_exp"])
    msk_total = E.cnt(S["s_msk"])
    E.do("gpsimd", lambda e: e.dma_start(
        T["W_sb"][:, :].rearrange("p (dc c) -> p dc c", dc=8),
        T["d_Wo"].ap().rearrange("(dc p) c -> p dc c", p=128)),
        wait=[(S["s_a2t"], 96)], inc=S["s_w"], inc_amt=16)
    for dq in range(2):
        E.do("tensor", lambda e, dq=dq: e.matmul(
            P["P01"][dq][:, :], T["ones"][0:1, 0:128],
            T["bo_row"][0:1, dq * 512:(dq + 1) * 512], start=True, stop=True),
            wait=([(S["s_exp"], exp_total), (S["s_msk"], msk_total),
                   (S["s_const"], 80)] if dq == 0 else []),
            inc=S["s_bob"])
    E.do("vector", lambda e: e.tensor_copy(
        T["bo_bc"][:, 0:512], P["P01"][0][:, :]), wait=[(S["s_bob"], 2)])
    E.do("vector", lambda e: e.tensor_copy(
        T["bo_bc"][:, 512:1024], P["P01"][1][:, :]), inc=S["s_bobv"])

    x_t, x2b = T["x_tile"], T["x2_buf"]
    for st in range(16):
        par = st % 2
        xin_waits = [(S["s_x2v"], 2 * (st - 1))]
        if st < 2:
            xin_waits.append((S["s_nrm"], n_iters))
        E.do("sync", lambda e, st=st, par=par: e.dma_start(
            x_t[par][:, :], T["d_x"].ap()[st * 128:(st + 1) * 128, :]),
            wait=xin_waits, inc=S["s_xin%d" % par], inc_amt=16)
        for dq in range(2):
            gidx = st * 2 + dq
            for dc in range(8):
                waits = []
                if dc == 0:
                    waits.append((S["s_x2v"], gidx - 1))
                    if gidx == 0:
                        waits.extend([(S["s_nrm"], n_iters), (S["s_w"], 64),
                                      (S["s_bobv"], 1)])
                E.do("tensor", lambda e, st=st, dq=dq, dc=dc, pp=gidx % 2: e.matmul(
                    P["P01"][pp][:, :],
                    T["ctxT"][:, dc * 2048 + st * 128: dc * 2048 + st * 128 + 128],
                    T["W_sb"][:, dc * 1024 + dq * 512: dc * 1024 + dq * 512 + 512],
                    start=(dc == 0), stop=(dc == 7)),
                    wait=waits, inc=(S["s_wot"] if dc == 7 else None))
            E.do("vector", lambda e, dq=dq, pp=gidx % 2, xp=par: e.tensor_tensor(
                x2b[xp][:, dq * 512:(dq + 1) * 512], P["P01"][pp][:, :],
                x_t[xp][:, dq * 512:(dq + 1) * 512], ALU.add),
                wait=[(S["s_wot"], gidx + 1),
                      (S["s_xin%d" % par], 16 * (st // 2 + 1)),
                      (S["s_x2o%d" % par], 16 * (st // 2))])
            E.drain("vector")
            E.do("vector", lambda e, dq=dq, xp=par: e.tensor_tensor(
                x2b[xp][:, dq * 512:(dq + 1) * 512],
                x2b[xp][:, dq * 512:(dq + 1) * 512],
                T["bo_bc"][:, dq * 512:(dq + 1) * 512], ALU.add),
                inc=S["s_x2v"])
        E.do("sync", lambda e, st=st, par=par: e.dma_start(
            T["d_x2"].ap()[st * 128:(st + 1) * 128, :], x2b[par][:, :]),
            wait=[(S["s_x2v"], 2 * (st + 1))], inc=S["s_x2o%d" % par], inc_amt=16)
    E.wait("sync", S["s_x2o0"], 16 * 8)
    E.wait("sync", S["s_x2o1"], 16 * 8)


def make_attn_tensors(nc, es, T):
    def sb(name, shape, dt):
        return es.enter_context(nc.sbuf_tensor(name, shape, dt))
    T["KT"] = sb("KT", [128, 8 * 2048], BF16)
    T["VN"] = sb("VN", [128, 16 * 1040], BF16)
    T["QT"] = sb("QT", [128, 8 * 2048], BF16)
    T["ctxT"] = sb("ctxT", [128, 8 * 2048], BF16)
    T["y1T"] = sb("y1T", [128, 8 * 2048], BF16)
    T["W_sb"] = sb("W_sb", [128, 8 * 1024], BF16)
    T["alibi"] = sb("alibi", [128, 640], F32)
    T["mask"] = sb("mask", [128, 2048], BF16)
    T["ident"] = sb("ident", [128, 128], BF16)
    T["ones"] = sb("ones", [1, 128], F32)
    T["bo_row"] = sb("bo_row", [1, 1024], F32)
    T["bo_bc"] = sb("bo_bc", [128, 1024], F32)
    T["x_tile"] = [sb("x_tile0", [128, 1024], F32), sb("x_tile1", [128, 1024], F32)]
    T["xbf"] = nc.alloc_sbuf_tensor_at(
        "xbf", [128, 16 * 1024], BF16, offset=nc.lookup_mloc(T["ctxT"]).addr)
    T["sq_scratch"] = [
        nc.alloc_sbuf_tensor_at(
            "sq_scratch0", [128, 1024], F32, offset=nc.lookup_mloc(T["QT"]).addr),
        nc.alloc_sbuf_tensor_at(
            "sq_scratch1", [128, 1024], F32,
            offset=nc.lookup_mloc(T["QT"]).addr + 4096),
    ]
    addr = nc.lookup_mloc(T["y1T"]).addr
    T["x2_buf"] = [
        nc.alloc_sbuf_tensor_at("x2b0", [128, 1024], F32, offset=addr),
        nc.alloc_sbuf_tensor_at("x2b1", [128, 1024], F32, offset=addr + 4096),
    ]
    for n in ("st_m16", "st_n16", "st_q16", "st_v16", "st_d16", "st_r16"):
        T[n] = sb(n, [128, 16], F32)
    T["eps_t"] = sb("eps_t", [128, 1], F32)
    kaddr = nc.lookup_mloc(T["KT"]).addr
    T["y1_tile"] = [
        nc.alloc_sbuf_tensor_at("y1_0", [128, 1024], BF16, offset=kaddr),
        nc.alloc_sbuf_tensor_at("y1_1", [128, 1024], BF16, offset=kaddr + 2048),
    ]
    T["expb"] = [sb("expb%d" % k, [128, 512], BF16) for k in range(3)]
    xa0 = nc.lookup_mloc(T["x_tile"][0]).addr
    xa1 = nc.lookup_mloc(T["x_tile"][1]).addr
    T["rd"] = [nc.alloc_sbuf_tensor_at("rd0", [1, 512], F32, offset=xa0),
               nc.alloc_sbuf_tensor_at("rd1", [1, 512], F32, offset=xa0 + 2048)]
    T["bc_sb"] = [nc.alloc_sbuf_tensor_at("bc0", [64, 512], F32, offset=xa1),
                  nc.alloc_sbuf_tensor_at("bc1", [64, 512], F32, offset=xa1 + 2048)]
    return T


def make_attn_psum(nc, es):
    P = {}
    P["PT"] = [es.enter_context(nc.psum_tensor("PT%d" % k, [128, 1024], BF16))
               for k in range(2)]
    P["P01"] = [es.enter_context(nc.psum_tensor("P01_%d" % k, [128, 512], F32))
                for k in range(3)]
    P["PC"] = [es.enter_context(nc.psum_tensor("PC%d" % k, [65, 512], F32))
               for k in range(2)]
    P["PB"] = [es.enter_context(nc.psum_tensor("PB0", [64, 512], F32))]
    return P



EPS = 1e-5

FFN_SEM_NAMES = [
    "f_xin0", "f_xin1", "f_v4", "f_tr", "f_xnt",
    "f_cst", "f_gt", "f_gv", "f_wn",
    "f_at", "f_as", "f_glat", "f_glav", "f_abo",
    "f_ew0", "f_ew1", "f_hmm", "f_gel", "f_eomm", "f_eov",
    "f_e2t", "f_e2v",
    "f_r0", "f_r1", "f_r2", "f_trf", "f_trfv", "f_rot", "f_rotv",
    "f_glmt", "f_glmv", "f_gate", "f_gbt", "f_gbv", "f_mix",
    "f_ot", "f_ov", "f_out0", "f_out1", "f_abi", "f_cstg", "f_ab2",
    "f_xbf", "f_sq", "f_ve", "f_sd",
    "f_xnw0", "f_xnw1", "f_zm", "f_gth", "f_xev", "f_xtr",
    "f_tiv", "f_sct", "f_mld", "f_gbb", "f_xnv",
    "f_tiv2", "f_cmp", "f_inv", "f_ivv", "f_ti3",
    "f_ixe", "f_wgc"]


def ffn_sems(nc, es):
    return {n: Sem(nc, es, n) for n in FFN_SEM_NAMES}


def mm_di(E, fn, wait=(), inc=None):
    E.do("tensor", fn, wait=wait)
    E.drain("tensor")
    E.do("tensor", lambda e: e.nop(), inc=inc)


def emit_ffn(E, S, T, P):
    emit_f0_consts(E, S, T)
    emit_f1_ln2_xnt(E, S, T, P)
    emit_gbc_build(E, S, T, P)
    emit_f2_gating(E, S, T, P)
    emit_xn_spill(E, S, T)
    emit_slot_chain(E, S, T)
    emit_f3_abel(E, S, T, P)
    emit_f4_moe_sparse(E, S, T, P)
    emit_f5_rot_mix_out(E, S, T, P)

CAP = 768
NCH = CAP // 128


def emit_gbc_build(E, S, T, P):
    for i, (row, dst) in enumerate([("gamrow", "gamma_bc"), ("betrow", "beta_bc")]):
        for c in range(2):
            mm_di(E, lambda e, row=row, c=c: e.matmul(
                P["PS2"][:, :], T["ones_f"][0:1, :],
                T[row][0:1, c * 512:(c + 1) * 512], start=True, stop=True),
                wait=[(S["f_cst"], 16 * N_CST), (S["f_gbb"], 2 * i + c)],
                inc=S["f_tiv"])
            E.do("vector", lambda e, dst=dst, c=c, i=i: e.tensor_copy(
                T[dst][:, c * 512:(c + 1) * 512], P["PS2"][:, :]),
                wait=[(S["f_tiv"], 2 * i + c + 1)], inc=S["f_gbb"])


def emit_slot_chain(E, S, T):
    B1, B3 = T["ib1"], T["ib3"]
    E.do("vector", lambda e: e.tensor_single_scalar(
        B1[:, :], T["wT"][0:8, :], 0.0, ALU.is_gt),
        wait=[(S["f_wn"], 32), (S["f_e2v"], 16)])
    E.drain("vector")
    E.do("vector", lambda e: e.tensor_tensor_scan(
        B3[:, :], B1[:, :], B1[:, :], 0.0, ALU.add, ALU.bypass))
    E.drain("vector")
    E.do("vector", lambda e: e.tensor_tensor(
        T["ib2b"][:, :], B1[:, :], B3[:, :], ALU.mult))
    E.drain("vector")
    E.do("vector", lambda e: e.tensor_scalar_add(
        T["ib2b"][:, :], T["ib2b"][:, :], -1.0), inc=S["f_xnv"])


def emit_slot_transposes(E, S, T, P):
    for st in range(16):
        w = []
        if st == 0:
            w = [(S["f_xnv"], 18), (S["f_e2v"], 16)]
        E.do("tensor", lambda e, st=st: e.matmul(
            P["PS2"][:, st * 8:(st + 1) * 8],
            T["ib2b"][0:8, st * 128:(st + 1) * 128], T["id8"][0:8, :],
            start=True, stop=True), wait=w)
    E.drain("tensor")
    E.do("tensor", lambda e: e.nop(), inc=S["f_tiv"])
    E.do("vector", lambda e: e.tensor_copy(
        T["slotsT"][:, :], P["PS2"][:, 0:128]),
        wait=[(S["f_tiv"], 5)], inc=S["f_tiv2"])


def _inv_compares(E, S, T, e_):
    for tb in range(16):
        gi = e_ * 16 + tb
        w = []
        if gi >= 2:
            w.append((S["f_inv"], 2 * (gi - 1)))
        if gi == 0:
            w += [(S["f_tiv2"], 1), (S["f_abi"], 32), (S["f_gt"], 4)]
        E.do("vector", lambda e, e_=e_, tb=tb, gi=gi: e.tensor_scalar(
            T["MT"][gi % 2][:, :], T["siota"][:, :],
            T["slotsT"][:, tb * 8 + e_: tb * 8 + e_ + 1], None, ALU.is_equal),
            wait=w, inc=S["f_cmp"])


def _inv_mms(E, S, T, P, e_):
    for tb in range(16):
        gi = e_ * 16 + tb
        for sl in range(2):
            w = []
            if sl == 0:
                w.append((S["f_cmp"], gi + 1))
            if tb == 0:
                w.append((S["f_ivv"], 2 * e_))
                if e_ == 0 and sl == 0:
                    w.append((S["f_glav"], 4))
            if tb == 15:
                mm_di(E, lambda e, gi=gi, tb=tb, sl=sl: e.matmul(
                    (P["PS1"][0:2, 0:512] if sl == 0 else P["PS2"][0:2, 0:256]),
                    T["c_hilo_sb"][:, tb * 2:tb * 2 + 2],
                    T["MT"][gi % 2][:, sl * 512: 512 + sl * 256],
                    start=(tb == 0), stop=(tb == 15)),
                    wait=w, inc=S["f_inv"])
            else:
                E.do("tensor", lambda e, gi=gi, tb=tb, sl=sl: e.matmul(
                    (P["PS1"][0:2, 0:512] if sl == 0 else P["PS2"][0:2, 0:256]),
                    T["c_hilo_sb"][:, tb * 2:tb * 2 + 2],
                    T["MT"][gi % 2][:, sl * 512: 512 + sl * 256],
                    start=(tb == 0), stop=(tb == 15)),
                    wait=w, inc=S["f_inv"])


def _inv_finish(E, S, T, P, e_):
    E.do("vector", lambda e: e.tensor_copy(
        T["vrow"][0:2, 0:512], P["PS1"][0:2, 0:512]),
        wait=[(S["f_inv"], 32 * (e_ + 1)), (S["f_ti3"], NCH * e_)],
        inc=S["f_ivv"])
    E.do("vector", lambda e: e.tensor_copy(
        T["vrow"][0:2, 512:768], P["PS2"][0:2, 0:256]), inc=S["f_ivv"])
    for ci in range(NCH):
        n = e_ * NCH + ci
        c0 = 300 + 2 * (ci % 2)
        w = []
        if ci == 0:
            w.append((S["f_ivv"], 2 * (e_ + 1)))
        if n >= 2:
            w.append((S["f_ixe"], 2 * n - 2))
        mm_di(E, lambda e, ci=ci, c0=c0: e.matmul(
            P["PS2"][:, c0:c0 + 2], T["vrow"][0:2, ci * 128:(ci + 1) * 128],
            T["id8"][0:2, 0:2], start=True, stop=True),
            wait=w, inc=S["f_ti3"])
        E.do("vector", lambda e, c0=c0: e.tensor_copy(
            T["pse"][:, :], P["PS2"][:, c0:c0 + 2]),
            wait=[(S["f_ti3"], n + 1)], inc=S["f_ixe"])
        E.drain("vector")
        E.do("vector", lambda e: e.tensor_single_scalar(
            T["iocc"][:, :], T["pse"][:, 0:1], 0.5, ALU.is_gt))
        E.do("vector", lambda e: e.scalar_tensor_tensor(
            T["itmp"][:, :], T["pse"][:, 0:1], 16.0, T["pse"][:, 1:2],
            ALU.mult, ALU.add))
        E.drain("vector")
        E.do("vector", lambda e: e.tensor_tensor(
            T["itmp"][:, :], T["itmp"][:, :], T["c_spad_sb"][:, 0:1],
            ALU.subtract))
        E.drain("vector")
        E.do("vector", lambda e: e.tensor_tensor(
            T["itmp"][:, :], T["itmp"][:, :], T["iocc"][:, :], ALU.mult))
        E.drain("vector")
        E.do("vector", lambda e: e.tensor_tensor(
            T["itmp"][:, :], T["itmp"][:, :], T["c_spad_sb"][:, 1:2], ALU.add))
        E.drain("vector")
        E.do("vector", lambda e, ci=ci, e_=e_: e.tensor_copy(
            T["idxU"][:, ci * 8 + e_: ci * 8 + e_ + 1], T["itmp"][:, :]),
            inc=S["f_ixe"])


def _gathers(E, S, T, e_):
    from concourse.bass import IndirectOffsetOnAxis
    for g in range(NCH):
        w = [(S["f_ixe"], 2 * (NCH * e_ + g + 1))]
        if g == 0:
            w += [(S["f_xtr"], NCH * e_), (S["f_wgc"], NCH * e_)]
            if e_ == 0:
                w += [(S["f_xnw0"], 16 * 8), (S["f_xnw1"], 16 * 8),
                      (S["f_zm"], 16 * 34), (S["f_at"], 32)]
        E.do("gpsimd", lambda e, e_=e_, g=g: e.indirect_dma_start(
            T["gath"][:, g * 1032:(g + 1) * 1032], None,
            T["d_xn"].ap(),
            IndirectOffsetOnAxis(ap=T["idxU"][:, g * 8 + e_: g * 8 + e_ + 1],
                                 axis=0)),
            wait=w, inc=S["f_gth"], inc_amt=16)


def _xposes(E, S, T, P, e_):
    for g in range(NCH):
        par = g % 2
        for dc in range(8):
            w = []
            if dc == 0:
                w = [(S["f_gth"], 16 * NCH * (e_ + 1)),
                     (S["f_xev"], NCH * e_ + g - 1)]
                if e_ == 0 and g == 0:
                    w.append((S["f_xnt"], 16))
            E.do("tensor", lambda e, g=g, dc=dc, par=par: e.transpose(
                P["PTf"][par][:, dc * 128:(dc + 1) * 128],
                T["gath"][:, g * 1032 + dc * 128: g * 1032 + (dc + 1) * 128],
                T["ident"][:, :]),
                wait=w, inc=(S["f_xtr"] if dc == 7 else None))
        E.do("vector", lambda e, g=g, par=par: e.tensor_copy(
            T["xncT"][:, :].rearrange("p (dc c) -> p dc c", dc=8)[
                :, :, g * 128:(g + 1) * 128],
            P["PTf"][par][:, :].rearrange("p (dc c) -> p dc c", dc=8)),
            wait=[(S["f_xtr"], NCH * e_ + g + 1)], inc=S["f_xev"])
    for g in range(NCH):
        E.do("vector", lambda e, g=g, e_=e_: e.tensor_copy(
            T["wgc"][:, g * 8 + e_: g * 8 + e_ + 1],
            T["gath"][:, g * 1032 + 1024 + e_: g * 1032 + 1024 + e_ + 1]),
            wait=([(S["f_gth"], 16 * NCH * (e_ + 1))] if g == 0 else []),
            inc=S["f_wgc"])


def emit_xn_spill(E, S, T):
    E.do("vector", lambda e: e.memset(T["zt"][:, :], 0.0), inc=S["f_xnv"])
    for z in range(17):
        E.do("sync", lambda e, z=z: e.dma_start(
            T["d_moe"].ap()[z * 128:(z + 1) * 128, :], T["zt"][:, 0:1024]),
            wait=([(S["f_xnv"], 1)] if z == 0 else []), inc=S["f_zm"], inc_amt=16)
    E.do("sync", lambda e: e.dma_start(
        T["d_xn"].ap()[2048:2176, :], T["zt"][:, :]), inc=S["f_zm"], inc_amt=16)
    for st in range(16):
        E.do("sync", lambda e, st=st: e.dma_start(
            T["d_xn"].ap()[st * 128:(st + 1) * 128, 1024:1032],
            T["wNb"][:, st * 8:(st + 1) * 8]),
            wait=[(S["f_wn"], 2 * st + 2)], inc=S["f_zm"], inc_amt=16)
    for st0 in range(0, 16, 2):
        for q in range(2):
            st = st0 + q
            E.do("vector", lambda e, st=st, q=q: e.tensor_tensor(
                T["xnn"][q][:, :], T["y2_sb"][:, st * 1024:(st + 1) * 1024],
                T["gamma_bc"][:, :], ALU.mult),
                wait=[(S["f_v4"], st + 1), (S["f_gbb"], 4),
                      (S["f_xnw%d" % q], 16 * (st // 2))])
        E.drain("vector")
        for q in range(2):
            st = st0 + q
            E.do("vector", lambda e, st=st, q=q: e.tensor_tensor(
                T["xnn"][q][:, :], T["xnn"][q][:, :], T["beta_bc"][:, :],
                ALU.add), inc=S["f_xnv"])
        E.drain("vector")
        for q in range(2):
            st = st0 + q
            E.do("sync", lambda e, st=st, q=q: e.dma_start(
                T["d_xn"].ap()[st * 128:(st + 1) * 128, 0:1024], T["xnn"][q][:, :]),
                wait=[(S["f_xnv"], st + 2)], inc=S["f_xnw%d" % q], inc_amt=16)



N_CST = 14
N_CSTG = 4


def emit_f0_consts(E, S, T):
    for q in range(4):
        E.do("gpsimd", lambda e, q=q: e.dma_start(
            T["x2bf"][:, q * 4096:(q + 1) * 4096].rearrange(
                "p (st c) -> p st c", st=4),
            T["d_x2"].ap()[q * 512:(q + 1) * 512, :].rearrange(
                "(st p) c -> p st c", p=128)),
            wait=([(S["f_xbf"], 16 * q)] if q > 0 else []),
            inc=S["f_xbf"], inc_amt=16)
    s = S["f_cst"]
    E.do("sync", lambda e: e.dma_start(T["ident"][:, :], T["c_ident"].ap()),
         inc=s, inc_amt=16)
    E.do("sync", lambda e: e.dma_start(T["id8"][:, :], T["c_id8"].ap()),
         inc=s, inc_amt=16)
    E.do("sync", lambda e: e.dma_start(T["c_hilo_sb"][:, :], T["d_c_hilo"].ap()),
         inc=s, inc_amt=16)
    E.do("sync", lambda e: e.dma_start(T["c_spad_sb"][:, :], T["d_c_spad"].ap()),
         inc=s, inc_amt=16)
    E.do("sync", lambda e: e.dma_start(T["ones_bf"][:, :], T["c_ones_bf"].ap(
        ).rearrange("(o f) -> o f", o=1)), inc=s, inc_amt=16)
    E.do("sync", lambda e: e.dma_start(T["ones_f"][:, :], T["c_ones_f"].ap(
        ).rearrange("(o f) -> o f", o=1)), inc=s, inc_amt=16)
    E.do("sync", lambda e: e.dma_start(
        T["gamrow"][:, :], T["d_ln_gamma"].ap().rearrange("(o f) -> o f", o=1)),
        inc=s, inc_amt=16)
    E.do("sync", lambda e: e.dma_start(
        T["betrow"][:, :], T["d_ln_beta"].ap().rearrange("(o f) -> o f", o=1)),
        inc=s, inc_amt=16)
    E.do("sync", lambda e: e.dma_start(
        T["gam"][:, :], T["d_ln_gamma_t"].ap()), inc=s, inc_amt=16)
    E.do("sync", lambda e: e.dma_start(
        T["bet"][:, :], T["d_ln_beta_t"].ap()), inc=s, inc_amt=16)
    E.do("sync", lambda e: e.dma_start(
        T["ab_b"][:, :], T["d_abel_b_t"].ap()), inc=s, inc_amt=16)
    E.do("sync", lambda e: e.dma_start(
        T["eb1_sb"][:, :], T["d_eb1_t"].ap()), inc=s, inc_amt=16)
    E.do("sync", lambda e: e.dma_start(
        T["gb_sb"][:, :], T["d_gate_b"].ap().rearrange("(p o) -> p o", o=1)),
        inc=s, inc_amt=16)
    E.do("sync", lambda e: e.dma_start(
        T["gcb_sb"][:, :], T["d_gc_b"].ap().rearrange("(p o) -> p o", o=1)),
        inc=s, inc_amt=16)
    sg = S["f_cstg"]
    E.do("gpsimd", lambda e: e.dma_start(
        T["eb2_sb"][:, :], T["d_eb2"].ap()), inc=sg, inc_amt=16)
    E.do("gpsimd", lambda e: e.dma_start(
        T["gw_sb"][:, :], T["d_gate_w_t"].ap()), inc=sg, inc_amt=16)
    E.do("gpsimd", lambda e: e.dma_start(
        T["gc1_sb"][:, :], T["d_gc1_t"].ap()), inc=sg, inc_amt=16)
    E.do("gpsimd", lambda e: e.dma_start(
        T["gc2_sb"][:, :], T["d_gc2_t"].ap()), inc=sg, inc_amt=16)
    E.do("gpsimd", lambda e: e.dma_start(
        T["aw_sb"][:, :].rearrange("p (dc c) -> p dc c", dc=8),
        T["d_abel_w"].ap().rearrange("(dc p) c -> p dc c", p=128)),
        inc=S["f_abi"], inc_amt=16)
    _ew_load(E, S, T, 0)


def _ew_load(E, S, T, idx, wait=()):
    e, half = idx // 2, idx % 2
    buf = idx % 2
    sem = S["f_ew%d" % buf]
    E.do("gpsimd", lambda en: en.dma_start(
        T["ew1h"][buf][:, :].rearrange("p (dc c) -> p dc c", dc=8),
        T["d_ew1"].ap()[e, :, half * 1024:(half + 1) * 1024].rearrange(
            "(dc p) c -> p dc c", p=128)),
        wait=wait, inc=sem, inc_amt=16)
    E.do("gpsimd", lambda en: en.dma_start(
        T["ew2h"][buf][:, :].rearrange("p (hc c) -> p hc c", hc=8),
        T["d_ew2"].ap()[e, half * 1024:(half + 1) * 1024, :].rearrange(
            "(hc p) c -> p hc c", p=128)),
        inc=sem, inc_amt=16)


def emit_f1_ln2_xnt(E, S, T, P):
    E.do("vector", lambda e: e.memset(T["eps_t"][:, :], EPS), inc=S["f_ve"])
    for q in range(4):
        qs = slice(q * 4, (q + 1) * 4)
        for st in range(q * 4, q * 4 + 4):
            E.do("vector", lambda e, st=st: e.reduce_sum(
                T["fs_m16"][:, st:st + 1],
                T["x2bf"][:, st * 1024:(st + 1) * 1024], axis=AX.X),
                wait=([(S["f_xbf"], 16 * (q + 1))] if st % 4 == 0 else []))
            E.do("scalar", lambda e, st=st: e.activation(
                T["f_sqs" if st % 2 == 0 else "f_sqs2"][:, :],
                T["x2bf"][:, st * 1024:(st + 1) * 1024],
                AF.Square, accum_out=T["fs_q16"][:, st:st + 1]),
                wait=([(S["f_xbf"], 16 * (q + 1))] if st % 4 == 0 else []),
                inc=(S["f_sq"] if st % 4 == 3 else None))
            E.drain("scalar")
        E.drain("vector")
        E.do("vector", lambda e, qs=qs: e.tensor_scalar_mul(
            T["fs_m16"][:, qs], T["fs_m16"][:, qs], 1.0 / 1024))
        E.drain("vector")
        E.do("vector", lambda e, qs=qs: e.tensor_scalar_mul(
            T["fs_n16"][:, qs], T["fs_m16"][:, qs], -1.0))
        E.drain("vector")
        E.do("vector", lambda e, qs=qs: e.tensor_tensor(
            T["fs_v16"][:, qs], T["fs_n16"][:, qs], T["fs_m16"][:, qs],
            ALU.mult))
        E.drain("vector")
        E.do("vector", lambda e, qs=qs: e.scalar_tensor_tensor(
            T["fs_v16"][:, qs], T["fs_q16"][:, qs], 1.0 / 1024,
            T["fs_v16"][:, qs], ALU.mult, ALU.add),
            wait=[(S["f_sq"], q + 1)], inc=S["f_ve"])
        E.do("scalar", lambda e, qs=qs: e.activation(
            T["fs_d16"][:, qs], T["fs_v16"][:, qs], AF.Sqrt,
            bias=T["eps_t"][:, :]),
            wait=[(S["f_ve"], q + 2)], inc=S["f_sd"])
        E.do("vector", lambda e, qs=qs: e.reciprocal(
            T["fs_r16"][:, qs], T["fs_d16"][:, qs]),
            wait=[(S["f_sd"], q + 1)])
        E.drain("vector")
        for st in range(q * 4, q * 4 + 4):
            E.do("vector", lambda e, st=st: e.tensor_scalar(
                T["y2_sb"][:, st * 1024:(st + 1) * 1024],
                T["x2bf"][:, st * 1024:(st + 1) * 1024],
                T["fs_n16"][:, st:st + 1], T["fs_r16"][:, st:st + 1],
                ALU.add, ALU.mult), inc=S["f_v4"])
    for st in range(16):
        par = st % 2
        for dc in range(8):
            E.do("tensor", lambda e, dc=dc, st=st, par=par: e.transpose(
                P["PTf"][par][:, dc * 128:(dc + 1) * 128],
                T["y2_sb"][:, st * 1024 + dc * 128: st * 1024 + (dc + 1) * 128],
                T["ident"][:, :]),
                wait=([(S["f_v4"], st + 1), (S["f_xnt"], st - 1),
                       (S["f_cst"], 16 * N_CST)] if dc == 0 else []),
                inc=(S["f_tr"] if dc == 7 else None))
        for dc in range(8):
            E.do("scalar", lambda e, dc=dc, st=st, par=par: e.activation(
                T["xnT"][:, dc * 2048 + st * 128: dc * 2048 + st * 128 + 128],
                P["PTf"][par][:, dc * 128:(dc + 1) * 128], AF.Identity,
                bias=T["bet"][:, dc:dc + 1], scale=T["gam"][:, dc:dc + 1]),
                wait=([(S["f_tr"], st + 1)] if dc == 0 else []),
                inc=(S["f_xnt"] if dc == 7 else None))


def emit_f2_gating(E, S, T, P):
    PS1, PS2 = P["PS1"], P["PS2"]
    for c in range(4):
        for dc in range(8):
            fn = lambda e, c=c, dc=dc: e.matmul(
                PS1[0:8, :], T["gw_sb"][:, dc * 8:(dc + 1) * 8],
                T["xnT"][:, dc * 2048 + c * 512: dc * 2048 + (c + 1) * 512],
                start=(dc == 0), stop=(dc == 7))
            gwt = ([(S["f_xnt"], 16), (S["f_gv"], c),
                    (S["f_cstg"], 16 * N_CSTG)] if dc == 0 else [])
            if dc == 7:
                mm_di(E, fn, wait=gwt, inc=S["f_gt"])
            else:
                E.do("tensor", fn, wait=gwt)
        E.do("vector", lambda e, c=c: e.tensor_scalar(
            T["glog"][:, c * 512:(c + 1) * 512], PS1[0:8, :],
            T["gb_sb"][0:8, :], None, ALU.add),
            wait=[(S["f_gt"], c + 1)], inc=S["f_gv"])
    for st in range(16):
        E.do("tensor", lambda e, st=st: e.matmul(
            PS2[:, st * 8:(st + 1) * 8],
            T["glog"][0:8, st * 128:(st + 1) * 128],
            T["id8"][0:8, :], start=True, stop=True),
            wait=([(S["f_gv"], 4), (S["f_gbb"], 4)] if st == 0 else []))
    E.drain("tensor")
    E.do("tensor", lambda e: e.nop(), inc=S["f_gt"])
    for st in range(16):
        E.do("vector", lambda e, st=st: e.reduce_max(
            T["g_m1"][:, :], PS2[:, st * 8:(st + 1) * 8], axis=AX.X),
            wait=([(S["f_gt"], 5)] if st == 0 else [(S["f_gate"], st)]))
        E.drain("vector")
        E.do("vector", lambda e: e.tensor_scalar_mul(
            T["g_nm1"][:, :], T["g_m1"][:, :], -1.0))
        E.drain("vector")
        E.do("vector", lambda e, st=st: e.tensor_single_scalar(
            T["g_msk"][:, :], PS2[:, st * 8:(st + 1) * 8], T["g_m1"][:, :],
            ALU.is_equal))
        E.drain("vector")
        E.do("vector", lambda e, st=st: e.scalar_tensor_tensor(
            T["g_lm"][:, :], T["g_msk"][:, :], -1e30,
            PS2[:, st * 8:(st + 1) * 8], ALU.mult, ALU.add))
        E.drain("vector")
        E.do("vector", lambda e: e.reduce_max(
            T["g_m2"][:, :], T["g_lm"][:, :], axis=AX.X))
        E.drain("vector")
        E.do("vector", lambda e: e.tensor_scalar(
            T["g_d0"][:, :], T["g_m2"][:, :], T["g_nm1"][:, :], None, ALU.add),
            inc=S["f_wn"])
        E.do("scalar", lambda e, st=st: e.activation(
            T["g_e2"][:, :], T["g_d0"][:, :], AF.Exp),
            wait=[(S["f_wn"], 2 * st + 1)])
        E.drain("scalar")
        E.do("scalar", lambda e: e.add(T["g_dn"][:, :], T["g_e2"][:, :], 1.0))
        E.drain("scalar")
        E.do("scalar", lambda e, st=st: e.activation(
            T["g_e8"][:, :], PS2[:, st * 8:(st + 1) * 8], AF.Exp,
            bias=T["g_nm1"][:, :]),
            inc=S["f_gate"])
        E.do("vector", lambda e, st=st: e.reciprocal(
            T["g_rdn"][:, :], T["g_dn"][:, :]),
            wait=[(S["f_gate"], st + 1)])
        E.drain("vector")
        E.do("vector", lambda e, st=st: e.tensor_single_scalar(
            T["g_mge"][:, :], PS2[:, st * 8:(st + 1) * 8], T["g_m2"][:, :],
            ALU.is_ge))
        E.drain("vector")
        E.do("vector", lambda e: e.tensor_tensor(
            T["g_t8"][:, :], T["g_e8"][:, :], T["g_mge"][:, :], ALU.mult))
        E.drain("vector")
        E.do("vector", lambda e, st=st: e.tensor_scalar(
            T["wN"][:, st * 8:(st + 1) * 8], T["g_t8"][:, :], T["g_rdn"][:, :],
            None, ALU.mult))
        E.drain("vector")
        E.do("vector", lambda e, st=st: e.tensor_copy(
            T["wNb"][:, st * 8:(st + 1) * 8], T["wN"][:, st * 8:(st + 1) * 8]),
            inc=S["f_wn"])
    E.do("vector", lambda e: e.memset(T["wT"][:, :], 0.0),
         wait=[(S["f_sq"], 4)])
    E.drain("vector")
    for st in range(16):
        waits = [(S["f_wn"], 2 * st + 2), (S["f_e2v"], st)]
        if st == 0:
            waits += [(S["f_wn"], 32), (S["f_gate"], 16)]
        mm_di(E, lambda e, st=st: e.matmul(
            PS2[0:8, 0:128], T["wNb"][:, st * 8:(st + 1) * 8], T["ident"][:, :],
            start=True, stop=True),
            wait=waits, inc=S["f_e2t"])
        E.do("vector", lambda e, st=st: e.tensor_copy(
            T["wT"][0:8, st * 128:(st + 1) * 128], PS2[0:8, 0:128]),
            wait=[(S["f_e2t"], st + 1)], inc=S["f_e2v"])


def emit_f3_abel(E, S, T, P):
    dcp_order = [4, 5, 6, 7, 0, 1, 2, 3]
    groups = [(dcp, c) for dcp in dcp_order for c in range(4)]
    for gidx, (dcp, c) in enumerate(groups):
        par = gidx % 2
        if gidx == 16:
            emit_slot_transposes(E, S, T, P)
        for dc in range(8):
            waits = []
            if dc == 0:
                waits = [(S["f_as"], gidx - 1)]
                if gidx == 0:
                    waits += [(S["f_abi"], 16), (S["f_xnt"], 16)]
        
            E.do("tensor", lambda e, dcp=dcp, c=c, dc=dc, par=par: e.matmul(
                P["PEO"][par][:, :],
                T["aw_sb"][:, dc * 1024 + dcp * 128: dc * 1024 + (dcp + 1) * 128],
                T["xnT"][:, dc * 2048 + c * 512: dc * 2048 + (c + 1) * 512],
                start=(dc == 0), stop=(dc == 7)),
                wait=waits, inc=(S["f_at"] if dc == 7 else None))
        gw = [(S["f_at"], gidx + 1)]
        if gidx == 8:
            gw.append((S["f_xnv"], 18))
        if gidx == 16:
            gw.append((S["f_tiv"], 5))
        E.do("scalar", lambda e, dcp=dcp, c=c, par=par, gidx=gidx: e.activation(
            T["abelT"][:, dcp * 2048 + c * 512: dcp * 2048 + (c + 1) * 512],
            P["PEO"][par][:, :], AF.Gelu, bias=T["ab_b"][:, dcp:dcp + 1]),
            wait=gw, inc=S["f_as"])
    for c in range(4):
        for dcp in range(8):
            waits = []
            if dcp == 0:
                waits = [(S["f_glav"], c)]
                if c == 0:
                    waits += [(S["f_as"], 32), (S["f_cstg"], 16 * N_CSTG)]
            fn = lambda e, c=c, dcp=dcp: e.matmul(
                P["PS1"][0:1, :], T["gc1_sb"][:, dcp:dcp + 1],
                T["abelT"][:, dcp * 2048 + c * 512: dcp * 2048 + (c + 1) * 512],
                start=(dcp == 0), stop=(dcp == 7))
            if dcp == 7:
                mm_di(E, fn, wait=waits, inc=S["f_glat"])
            else:
                E.do("tensor", fn, wait=waits)
        E.do("vector", lambda e, c=c: e.tensor_scalar(
            T["gla"][0:1, c * 512:(c + 1) * 512], P["PS1"][0:1, :],
            T["gcb_sb"][0:1, :], None, ALU.add),
            wait=[(S["f_glat"], c + 1)], inc=S["f_glav"])
    E.do("sync", lambda e: e.dma_start(T["d_abel"].ap(), T["abelT"][:, :]),
         wait=[(S["f_glav"], 4)], inc=S["f_abo"], inc_amt=16)


def emit_f4_moe_sparse(E, S, T, P):
    from concourse.bass import IndirectOffsetOnAxis
    gel_count, hmm_count, eomm_count = {}, {}, {}
    n_gel = n_hmm = n_eomm = 0
    for idx in range(16):
        for hc in range(8):
            for sc in range(2):
                n_hmm += 1
                hmm_count[(idx, hc, sc)] = n_hmm
                n_gel += 1
                gel_count[(idx, hc, sc)] = n_gel
        for k in range(NCH):
            for dq in range(2):
                n_eomm += 1
                eomm_count[(idx, k, dq)] = n_eomm
    h_groups = [(idx, hc, sc) for idx in range(16) for hc in range(8)
                for sc in range(2)]
    h_gnum = {g: n for n, g in enumerate(h_groups)}
    eo_groups = [(idx, k, dq) for idx in range(16) for k in range(NCH)
                 for dq in range(2)]
    eo_gnum = {g: n for n, g in enumerate(eo_groups)}

    E.do("gpsimd", lambda e: e.dma_start(T["siota"][:, :], T["d_c_siota"].ap()),
         wait=[(S["f_at"], 32)], inc=S["f_abi"], inc_amt=16)
    _inv_compares(E, S, T, 0)
    _inv_mms(E, S, T, P, 0)
    _inv_finish(E, S, T, P, 0)
    _gathers(E, S, T, 0)

    for e_ in range(8):
        if e_ + 1 <= 7:
            _inv_compares(E, S, T, e_ + 1)
            _inv_mms(E, S, T, P, e_ + 1)
            _inv_finish(E, S, T, P, e_ + 1)
            _gathers(E, S, T, e_ + 1)
        _xposes(E, S, T, P, e_)
        for half in range(2):
            idx = e_ * 2 + half
            buf = idx % 2
            if idx + 1 < 16:
                w = []
                if idx >= 1:
                    w = [(S["f_eomm"], eomm_count[(idx - 1, NCH - 1, 1)])]
                _ew_load(E, S, T, idx + 1, wait=w)
            for hc in range(8):
                for sc in range(2):
                    g = (idx, hc, sc)
                    gn = h_gnum[g]
                    nw = 512 if sc == 0 else 256
                    for dc in range(8):
                        waits = []
                        if dc == 0:
                            pg = h_groups[gn - 2] if gn >= 2 else None
                            if pg:
                                waits.append((S["f_gel"], gel_count[pg]))
                            if hc == 0 and sc == 0:
                                waits.append((S["f_ew%d" % buf],
                                              32 * (idx // 2 + 1)))
                                waits.append((S["f_xev"], NCH * (e_ + 1)))
                        E.do("tensor", lambda e, buf=buf, hc=hc, sc=sc, dc=dc,
                             nw=nw, par=gn % 2: e.matmul(
                            P["PH"][par][:, 0:nw],
                            T["ew1h"][buf][:, dc * 1024 + hc * 128:
                                           dc * 1024 + (hc + 1) * 128],
                            T["xncT"][:, dc * CAP + sc * 512:
                                      dc * CAP + sc * 512 + nw],
                            start=(dc == 0), stop=(dc == 7)),
                            wait=waits, inc=(S["f_hmm"] if dc == 7 else None))
                    gw = [(S["f_hmm"], hmm_count[g])]
                    if idx == 0 and hc == 0 and sc == 0:
                        gw.append((S["f_abo"], 16))
                    pgn = gn - 16
                    if pgn >= 0:
                        pidx = h_groups[pgn][0]
                        gw.append((S["f_eomm"], eomm_count[(pidx, NCH - 1, 1)]))
                    E.do("scalar", lambda e, hc=hc, sc=sc, idx=idx, nw=nw,
                         par=gn % 2: e.activation(
                        T["hcT"][:, hc * CAP + sc * 512: hc * CAP + sc * 512 + nw],
                        P["PH"][par][:, 0:nw], AF.Gelu,
                        bias=T["eb1_sb"][:, idx * 8 + hc: idx * 8 + hc + 1]),
                        wait=gw, inc=S["f_gel"])
            for k in range(NCH):
                for dq in range(2):
                    g = (idx, k, dq)
                    gn = eo_gnum[g]
                    for hc in range(8):
                        waits = [(S["f_gel"],
                                  gel_count[(idx, hc, 0 if k < 4 else 1)])]
                        if hc == 0:
                            pg = eo_groups[gn - 2] if gn >= 2 else None
                            if pg:
                                waits.append((S["f_eov"],
                                              eomm_count[pg]))
                        E.do("tensor", lambda e, buf=buf, k=k, dq=dq, hc=hc,
                             par=gn % 2: e.matmul(
                            P["PEO"][par][:, :],
                            T["hcT"][:, hc * CAP + k * 128:
                                     hc * CAP + k * 128 + 128],
                            T["ew2h"][buf][:, hc * 1024 + dq * 512:
                                           hc * 1024 + dq * 512 + 512],
                            start=(hc == 0), stop=(hc == 7)),
                            wait=waits, inc=(S["f_eomm"] if hc == 7 else None))
                    ew = [(S["f_eomm"], eomm_count[g]),
                          (S["f_sct"], 16 * NCH * idx),
                          (S["f_wgc"], NCH * e_ + k + 1)]
                    E.do("vector", lambda e, e_=e_, k=k, dq=dq, par=gn % 2:
                         e.tensor_scalar(
                        T["eoc"][:, k * 1024 + dq * 512: k * 1024 + dq * 512 + 512],
                        P["PEO"][par][:, :],
                        T["wgc"][:, k * 8 + e_: k * 8 + e_ + 1],
                        None, ALU.mult),
                        wait=ew, inc=S["f_eov"])
            for k in range(NCH):
                waits = [(S["f_eov"], eomm_count[(idx, k, 1)]),
                         (S["f_sct"], 16 * (NCH * idx + k))]
                E.do("gpsimd", lambda e, e_=e_, k=k: e.indirect_dma_start(
                    T["d_moe"].ap(),
                    IndirectOffsetOnAxis(ap=T["idxU"][:, k * 8 + e_: k * 8 + e_ + 1],
                                         axis=0),
                    T["eoc"][:, k * 1024:(k + 1) * 1024], None,
                    compute_op=ALU.add),
                    wait=waits, inc=S["f_sct"], inc_amt=16)
    for q in range(4):
        w = [(S["f_mld"], 16 * q)]
        if q == 0:
            w = [(S["f_sct"], 16 * NCH * 16)]
        E.do("sync", lambda e, q=q: e.dma_start(
            T["macc"][:, q * 4096:(q + 1) * 4096].rearrange(
                "p (st c) -> p st c", st=4),
            T["d_moe"].ap()[q * 512:(q + 1) * 512, :].rearrange(
                "(st p) c -> p st c", p=128)),
            wait=w, inc=S["f_mld"], inc_amt=16)
    e2t_base = E.cnt(S["f_e2t"])
    for stt in range(16):
        for dq in range(2):
            gn = stt * 2 + dq
            waits = [(S["f_e2v"], 16 + gn - 1)]
            if gn == 0:
                waits += [(S["f_eov"], E.cnt(S["f_eov"])), (S["f_e2v"], 16),
                          (S["f_cstg"], 16 * N_CSTG)]
            mm_di(E, lambda e, stt=stt, dq=dq, par=gn % 2: e.matmul(
                P["PEO"][par][:, :], T["wT"][0:8, stt * 128:(stt + 1) * 128],
                T["eb2_sb"][0:8, dq * 512:(dq + 1) * 512],
                start=True, stop=True),
                wait=waits, inc=S["f_e2t"])
            E.do("vector", lambda e, stt=stt, dq=dq, par=gn % 2: e.tensor_tensor(
                T["macc"][:, stt * 1024 + dq * 512: stt * 1024 + dq * 512 + 512],
                T["macc"][:, stt * 1024 + dq * 512: stt * 1024 + dq * 512 + 512],
                P["PEO"][par][:, :], ALU.add),
                wait=[(S["f_e2t"], e2t_base + gn + 1),
                      (S["f_mld"], 16 * (stt // 4 + 1))],
                inc=S["f_e2v"])


def emit_f5_rot_mix_out(E, S, T, P):
    hmm_total = E.cnt(S["f_hmm"])
    eomm_total = E.cnt(S["f_eomm"])
    e2v_total = E.cnt(S["f_e2v"])

    def rload(r, dstslice, sem, wait):
        E.do("gpsimd", lambda e: e.dma_start(
            dstslice, T["d_Rall"].ap()[r]),
            wait=wait, inc=sem, inc_amt=16)

    R0s = T["xnT"][:, 0:8192]
    R1s = T["xnT"][:, 8192:16384]
    ccw = (T["sem_cc"], 1)
    wgc_total = E.cnt(S["f_wgc"])
    rload(0, R0s, S["f_r0"], [(S["f_hmm"], hmm_total), (S["f_wgc"], wgc_total),
                              ccw])
    rload(1, R1s, S["f_r1"], [(S["f_hmm"], hmm_total), (S["f_wgc"], wgc_total),
                              ccw])
    for st in range(16):
        par = st % 2
        for dc in range(8):
            waits = []
            if dc == 0:
                waits = [(S["f_trfv"], st - 1),
                         (S["f_e2v"], 16 + 2 * (st + 1))]
            E.do("tensor", lambda e, st=st, dc=dc, par=par: e.transpose(
                P["PTf"][par][:, dc * 128:(dc + 1) * 128],
                T["macc"][:, st * 1024 + dc * 128: st * 1024 + (dc + 1) * 128],
                T["ident"][:, :]),
                wait=waits, inc=(S["f_trf"] if dc == 7 else None))
        E.do("vector", lambda e, st=st, par=par: e.tensor_copy(
            T["moeT"][:, :].rearrange("p (dc s) -> p dc s", dc=8)[
                :, :, st * 128:(st + 1) * 128],
            P["PTf"][par][:, :].rearrange("p (dc c) -> p dc c", dc=8)),
            wait=[(S["f_trf"], st + 1)], inc=S["f_trfv"])
    rsrc = [T["moeT"], T["rbufA"], T["rbufB"]]
    rdst = [T["rbufA"], T["rbufB"], T["rbufA"]]
    rsem = [S["f_r0"], S["f_r1"], S["f_r2"]]
    gidx = 0
    for r in range(3):
        if r == 1:
            rload(2, R0s, S["f_r2"], [(S["f_rot"], 32)])
        rR = R0s if r in (0, 2) else R1s
        for dcO in range(8):
            for c in range(4):
                par = gidx % 2
                for dc in range(8):
                    waits = []
                    if dc == 0:
                        waits = [(S["f_rotv"], gidx - 1), (rsem[r], 16)]
                        if r == 0 and dcO == 0 and c == 0:
                            waits.append((S["f_trfv"], 16))
                    E.do("tensor", lambda e, rR=rR, r=r, dcO=dcO, c=c, dc=dc,
                         par=par: e.matmul(
                        P["PEO"][par][:, :],
                        rR[:, dc * 1024 + dcO * 128: dc * 1024 + (dcO + 1) * 128],
                        rsrc[r][:, dc * 2048 + c * 512: dc * 2048 + (c + 1) * 512],
                        start=(dc == 0), stop=(dc == 7)),
                        wait=waits, inc=(S["f_rot"] if dc == 7 else None))
                E.do("vector", lambda e, r=r, dcO=dcO, c=c, par=par, gidx=gidx:
                     e.tensor_copy(
                    rdst[r][:, dcO * 2048 + c * 512: dcO * 2048 + (c + 1) * 512],
                    P["PEO"][par][:, :]),
                    wait=[(S["f_rot"], gidx + 1)], inc=S["f_rotv"])
                gidx += 1
    E.do("sync", lambda e: e.dma_start(
        T["abelT2"][:, :], T["d_abel"].ap()),
        wait=[(S["f_eomm"], eomm_total)], inc=S["f_ab2"], inc_amt=16)
    for c in range(4):
        for dcp in range(8):
            waits = []
            if dcp == 0:
                waits = [(S["f_glmv"], c)]
                if c == 0:
                    waits += [(S["f_rotv"], 96), (S["f_cstg"], 16 * N_CSTG)]
            fn = lambda e, c=c, dcp=dcp: e.matmul(
                P["PS1"][0:1, :], T["gc2_sb"][:, dcp:dcp + 1],
                T["rbufA"][:, dcp * 2048 + c * 512: dcp * 2048 + (c + 1) * 512],
                start=(dcp == 0), stop=(dcp == 7))
            if dcp == 7:
                mm_di(E, fn, wait=waits, inc=S["f_glmt"])
            else:
                E.do("tensor", fn, wait=waits)
        E.do("vector", lambda e, c=c: e.tensor_copy(
            T["glm"][0:1, c * 512:(c + 1) * 512], P["PS1"][0:1, :]),
            wait=[(S["f_glmt"], c + 1)], inc=S["f_glmv"])
    E.do("vector", lambda e: e.tensor_tensor(
        T["glm"][0:1, :], T["glm"][0:1, :], T["gla"][0:1, :], ALU.add),
        wait=[(S["f_glmv"], 4), (S["f_glav"], 4)], inc=S["f_gbv"])
    E.do("scalar", lambda e: e.activation(
        T["g_row"][0:1, :], T["glm"][0:1, :], AF.Sigmoid),
        wait=[(S["f_gbv"], 1)], inc=S["f_gate"])
    for c in range(4):
        waits = [(S["f_gate"], 17), (S["f_gbv"], c + 1)]
        if c == 0:
            waits.append((S["f_e2v"], e2v_total))
        mm_di(E, lambda e, c=c: e.matmul(
            P["PS2"][:, :], T["ones_bf"][0:1, :],
            T["g_row"][0:1, c * 512:(c + 1) * 512], start=True, stop=True),
            wait=waits, inc=S["f_gbt"])
        E.do("vector", lambda e, c=c: e.tensor_copy(
            T["gbc"][:, c * 512:(c + 1) * 512], P["PS2"][:, :]),
            wait=[(S["f_gbt"], c + 1)], inc=S["f_gbv"])
    for dc in range(8):
        E.do("vector", lambda e, dc=dc: e.tensor_tensor(
            T["mixtmp"][:, :], T["rbufA"][:, dc * 2048:(dc + 1) * 2048],
            T["abelT2"][:, dc * 2048:(dc + 1) * 2048], ALU.subtract),
            wait=([(S["f_gbv"], 5), (S["f_ab2"], 16)] if dc == 0 else []))
        E.drain("vector")
        E.do("vector", lambda e, dc=dc: e.tensor_tensor(
            T["mixtmp"][:, :], T["mixtmp"][:, :], T["gbc"][:, :], ALU.mult))
        E.drain("vector")
        E.do("vector", lambda e, dc=dc: e.tensor_tensor(
            T["rbufA"][:, dc * 2048:(dc + 1) * 2048],
            T["abelT2"][:, dc * 2048:(dc + 1) * 2048], T["mixtmp"][:, :],
            ALU.add), inc=S["f_mix"])
        E.drain("vector")
    x_t = T["x2s"]
    for st in range(8):
        xw = [(S["f_xin0"], 16 * st)]
        if st == 0:
            xw = [(S["f_rot"], 96), (S["f_at"], 32)]
        E.do("sync", lambda e, st=st: e.dma_start(
            x_t[st][:, :], T["d_x2"].ap()[st * 128:(st + 1) * 128, :]),
            wait=xw, inc=S["f_xin0"], inc_amt=16)
    for st in range(16):
        par = st % 2
        for dc in range(8):
            waits = []
            if dc == 0:
                waits = [(S["f_ov"], st - 1)]
                if st < 2:
                    waits.append((S["f_mix"], 8))
            E.do("tensor", lambda e, st=st, dc=dc, par=par: e.transpose(
                P["PTf"][par][:, dc * 128:(dc + 1) * 128],
                T["rbufA"][:, dc * 2048 + st * 128: dc * 2048 + st * 128 + 128],
                T["ident"][:, :]),
                wait=waits, inc=(S["f_ot"] if dc == 7 else None))
        E.do("vector", lambda e, st=st, par=par: e.tensor_tensor(
            T["outb"][par][:, :], x_t[st % 8][:, :],
            T["y2_sb"][:, st * 1024:(st + 1) * 1024], ALU.add),
            wait=[(S["f_ot"], st + 1), (S["f_xin0"], 16 * (st + 1)),
                  (S["f_out%d" % par], 16 * (st // 2))])
        E.drain("vector")
        E.do("vector", lambda e, st=st, par=par: e.tensor_tensor(
            T["outb"][par][:, :], T["outb"][par][:, :], P["PTf"][par][:, :],
            ALU.add), inc=S["f_ov"])
        E.do("sync", lambda e, st=st, par=par: e.dma_start(
            T["d_out"].ap()[st * 128:(st + 1) * 128, :], T["outb"][par][:, :]),
            wait=[(S["f_ov"], st + 1)], inc=S["f_out%d" % par], inc_amt=16)
        if st + 8 < 16:
            E.do("sync", lambda e, st=st: e.dma_start(
                x_t[st % 8][:, :],
                T["d_x2"].ap()[(st + 8) * 128:(st + 9) * 128, :]),
                wait=[(S["f_xin0"], 16 * (st + 8))], inc=S["f_xin0"],
                inc_amt=16)
    E.wait("sync", S["f_out0"], 16 * 8)
    E.wait("sync", S["f_out1"], 16 * 8)


def make_ffn_tensors(nc, es, T):
    def sb(name, shape, dt):
        return es.enter_context(nc.sbuf_tensor(name, shape, dt))

    def at(name, shape, dt, base, off=0):
        return nc.alloc_sbuf_tensor_at(
            name, shape, dt, offset=nc.lookup_mloc(base).addr + off)
    T["y2_sb"] = sb("y2_sb", [128, 16 * 1024], BF16)
    T["xnT"] = sb("xnT", [128, 8 * 2048], BF16)
    T["macc"] = sb("macc", [128, 16 * 1024], BF16)
    T["ew1h"] = [sb("ew1h%d" % k, [128, 8 * 1024], BF16) for k in range(2)]
    T["ew2h"] = [sb("ew2h%d" % k, [128, 8 * 1024], BF16) for k in range(2)]
    T["x2_tile"] = [sb("x2t%d" % k, [128, 1024], F32) for k in range(2)]
    T["outb"] = [sb("outb%d" % k, [128, 1024], F32) for k in range(2)]
    T["gbc"] = sb("gbc", [128, 2048], BF16)
    T["gamma_bc"] = sb("gamma_bc", [128, 1024], BF16)
    T["beta_bc"] = sb("beta_bc", [128, 1024], BF16)
    T["idxU"] = sb("idxU", [128, NCH * 8], mybir.dt.uint32)
    T["abelT"] = at("abelT", [128, 8 * 2048], BF16, T["macc"])
    T["moeT"] = at("moeT", [128, 8 * 2048], BF16, T["ew1h"][0])
    T["rbufB"] = T["moeT"]
    T["rbufA"] = at("rbufA", [128, 8 * 2048], BF16, T["macc"])
    T["abelT2"] = at("abelT2", [128, 8 * 2048], BF16, T["ew2h"][0])
    T["aw_sb"] = at("aw_sb", [128, 8 * 1024], BF16, T["x2_tile"][0])
    T["f_sqs"] = at("f_sqs", [128, 1024], F32, T["gbc"])
    T["f_sqs2"] = at("f_sqs2", [128, 1024], F32, T["ew1h"][1])
    T["x2bf"] = at("x2bf", [128, 16 * 1024], BF16, T["macc"])
    T["hcT"] = at("hcT", [128, 8 * 1024], BF16, T["macc"])
    T["eoc"] = at("eoc", [128, 8 * 1024], BF16, T["macc"], off=16384)
    T["glog"] = at("glog", [8, 2048], F32, T["macc"])
    T["ib2b"] = at("ib2b", [8, 2048], F32, T["macc"])
    T["ib1"] = at("ib1", [8, 2048], F32, T["macc"], off=8192)
    T["ib3"] = at("ib3", [8, 2048], F32, T["macc"], off=24576)
    T["gla"] = at("gla", [1, 2048], BF16, T["gamma_bc"])
    for n in ("fs_m16", "fs_n16", "fs_q16", "fs_v16", "fs_d16", "fs_r16"):
        T[n] = sb(n, [128, 16], F32)
    T["eps_t"] = sb("f_eps_t", [128, 1], F32)
    T["gam"] = sb("gam", [128, 8], F32)
    T["bet"] = sb("bet", [128, 8], F32)
    T["ab_b"] = sb("ab_b", [128, 8], F32)
    T["eb1_sb"] = sb("eb1_sb", [128, 128], F32)
    T["eb2_sb"] = sb("eb2_sb", [8, 1024], BF16)
    T["gw_sb"] = sb("gw_sb", [128, 64], BF16)
    T["gb_sb"] = sb("gb_sb", [8, 1], F32)
    T["gc1_sb"] = sb("gc1_sb", [128, 8], BF16)
    T["gc2_sb"] = sb("gc2_sb", [128, 8], BF16)
    T["gcb_sb"] = sb("gcb_sb", [1, 1], F32)
    T["ident"] = sb("f_ident", [128, 128], BF16)
    T["id8"] = sb("id8", [8, 8], F32)
    T["ones_bf"] = sb("ones_bf", [1, 128], BF16)
    T["ones_f"] = sb("ones_f", [1, 128], F32)
    T["wN"] = sb("wN", [128, 16 * 8], F32)
    T["wNb"] = sb("wNb", [128, 16 * 8], BF16)
    for n in ("g_m1", "g_nm1", "g_m2", "g_d0", "g_e2", "g_dn", "g_rdn"):
        T[n] = sb(n, [128, 1], F32)
    for n in ("g_msk", "g_lm", "g_e8", "g_mge", "g_t8"):
        T[n] = sb(n, [128, 8], F32)
    T["gath"] = at("gath", [128, NCH * 1032], BF16, T["xnT"])
    T["xncT"] = at("xncT", [128, 8 * CAP], BF16, T["xnT"], off=12416)
    T["MT"] = [at("mtb%d" % b, [128, CAP], BF16, T["xnT"], off=24704 + 1536 * b)
               for b in range(2)]
    T["xnn"] = [sb("xnn%d" % k, [128, 1024], BF16) for k in range(2)]
    T["zt"] = sb("zt", [128, 1032], BF16)
    T["mixtmp"] = at("mixtmp", [128, 2048], BF16, T["x2_tile"][0])
    T["glm"] = at("glm", [1, 2048], BF16, T["outb"][0])
    T["g_row"] = at("g_row", [1, 2048], BF16, T["outb"][1])
    T["gamrow"] = sb("gamrow", [1, 1024], F32)
    T["betrow"] = sb("betrow", [1, 1024], F32)
    T["wT"] = at("wT", [16, 2048], BF16, T["gbc"])
    T["slotsT"] = sb("slotsT", [128, 128], F32)
    T["vrow"] = at("vrow", [2, CAP], F32, T["x2_tile"][1], off=512)
    T["siota"] = at("siota", [128, CAP], F32, T["outb"][1])
    T["pse"] = sb("pse", [128, 2], F32)
    T["iocc"] = sb("iocc", [128, 1], F32)
    T["itmp"] = sb("itmp", [128, 1], F32)
    T["wgc"] = sb("wgc", [128, NCH * 8], F32)
    T["c_hilo_sb"] = sb("c_hilo_sb", [128, 32], BF16)
    T["x2s"] = [at("x2s%d" % k, [128, 1024], F32, T["ew1h"][0], off=4096 * k)
                for k in range(8)]
    T["c_spad_sb"] = sb("c_spad_sb", [128, 2], F32)
    return T


def make_ffn_psum(nc, es):
    P = {}
    P["PTf"] = [es.enter_context(nc.psum_tensor("PTf%d" % k, [128, 1024], BF16))
                for k in range(2)]
    P["PEO"] = [es.enter_context(nc.psum_tensor("PEOf%d" % k, [128, 512], F32))
                for k in range(2)]
    P["PH"] = [es.enter_context(nc.psum_tensor("PHf%d" % k, [128, 512], F32))
               for k in range(2)]
    P["PS1"] = es.enter_context(nc.psum_tensor("PS1f", [128, 512], F32))
    P["PS2"] = es.enter_context(nc.psum_tensor("PS2f", [128, 512], F32))
    return P



FACT = [1.0]
for _k in range(1, 13):
    FACT.append(FACT[-1] * _k)
C = [1.0 / f for f in FACT]

EXPM_SEM_NAMES = ["b_ld", "b_mt", "b_ev", "b_bl", "b_ro", "b_cc"]


def expm_sems(nc, es):
    return {n: Sem(nc, es, n) for n in EXPM_SEM_NAMES}


def emit_expm(E, S, T, P):
    E.do("sync", lambda e: e.dma_start(T["Irow"][:, :], T["c_irow"].ap()),
         inc=S["b_ro"], inc_amt=16)
    _emit_gen(E, S, T, P, 0)
    E.wait("sync", S["b_ro"], 16 + 16)


def emit_expm_allgather(E, S, T):
    E.do("gpsimd", lambda e: e.collective_compute(
        "AllGather", ALU.bypass, [list(range(8))],
        [T["d_Rpart"].ap().opt()], [T["d_Rall"].ap().opt()]),
        wait=[(S["b_ro"], 32)], inc=S["b_cc"])


def _build(E, S, T, dst, terms, gate_waits=(), inc=True, accumulate=False):
    E.drain("vector")
    first = not accumulate
    for k, (coef, mat) in enumerate(terms):
        w = list(gate_waits) if k == 0 else []
        src = T["Irow"] if mat is None else T[mat]
        if first:
            E.do("vector", lambda e, d=dst, s=src, c=coef: e.tensor_scalar_mul(
                d[:, :], s[:, :], c), wait=w)
        else:
            E.do("vector", lambda e, d=dst, s=src, c=coef: e.scalar_tensor_tensor(
                d[:, :], s[:, :], c, d[:, :], ALU.mult, ALU.add))
        E.drain("vector")
        first = False
    if inc:
        E.do("vector", lambda e: e.engine_nop(), inc=S["b_bl"])


def _product(E, S, T, P, g, pi, lhs_name, rhs_name, dst_name, add_name=None,
             lhs_waits=(), rhs_ev=None, out_dma=None):
    lhs, rhs, dst = T[lhs_name], T[rhs_name], T[dst_name]
    base_g = g * 80
    for rm in range(8):
        for q in range(2):
            gi = pi * 16 + rm * 2 + q
            par = gi % 2
            for dc in range(8):
                waits = []
                if dc == 0:
                    waits = [(S["b_ev"], base_g + gi - 1)]
                    if rm == 0 and q == 0:
                        waits += list(lhs_waits)
                        if rhs_ev is not None:
                            waits.append((S["b_ev"], rhs_ev))
                elif pi == 0 and rm == 0 and q == 0:
                    waits = [(S["b_bl"], 2 * (dc + 1))]
                E.do("tensor", lambda e, rm=rm, q=q, dc=dc, par=par: e.matmul(
                    P["PB"][par][:, :],
                    lhs[:, dc * 1024 + rm * 128: dc * 1024 + (rm + 1) * 128],
                    rhs[:, dc * 1024 + q * 512: dc * 1024 + q * 512 + 512],
                    start=(dc == 0), stop=(dc == 7)),
                    wait=waits, inc=(S["b_mt"] if dc == 7 else None))
            dsl = dst[:, rm * 1024 + q * 512: rm * 1024 + q * 512 + 512]
            evw = [(S["b_mt"], base_g + gi + 1)]
            if pi == 0 and rm == 0 and q == 0:
                evw.append((S["b_ro"], 16 + 16 * g))
            if add_name is None:
                E.do("vector", lambda e, d=dsl, par=par: e.tensor_copy(
                    d, P["PB"][par][:, :]), wait=evw, inc=S["b_ev"])
            else:
                asl = T[add_name][:, rm * 1024 + q * 512: rm * 1024 + q * 512 + 512]
                E.do("vector", lambda e, d=dsl, a=asl, par=par: e.tensor_tensor(
                    d, P["PB"][par][:, :], a, ALU.add), wait=evw, inc=S["b_ev"])
    if out_dma is not None:
        E.do("sync", lambda e: e.dma_start(T[out_dma].ap(), dst[:, :]),
             wait=[(S["b_ev"], base_g + (pi + 1) * 16)], inc=S["b_ro"], inc_amt=16)


def _emit_gen(E, S, T, P, g):
    lw = []
    if g > 0:
        lw = [(S["b_mt"], g * 80), (S["b_bl"], g * 3)]
    for dc in range(8):
        w = list(lw) if dc == 0 else [(S["b_ld"], 16 * dc)]
        E.do("sync", lambda e, dc=dc: e.dma_start(
            T["Gf32"][:, dc * 1024:(dc + 1) * 1024],
            T["d_mygen"].ap()[dc * 128:(dc + 1) * 128, :]),
            wait=w, inc=S["b_ld"], inc_amt=16)
        E.do("vector", lambda e, dc=dc: e.tensor_copy(
            T["Gb"][:, dc * 1024:(dc + 1) * 1024],
            T["Gf32"][:, dc * 1024:(dc + 1) * 1024]),
            wait=[(S["b_ld"], 16 * (dc + 1))], inc=S["b_bl"])
        E.do("vector", lambda e, dc=dc: e.tensor_scalar_mul(
            T["GTb"][:, dc * 1024:(dc + 1) * 1024],
            T["Gf32"][:, dc * 1024:(dc + 1) * 1024], -1.0),
            inc=S["b_bl"])
    ldw = [(S["b_ld"], 16 * 8)]
    base_ev = g * 80
    _product(E, S, T, P, g, 0, "GTb", "Gb", "G2b", lhs_waits=[(S["b_bl"], 2)])
    _product(E, S, T, P, g, 1, "GTb", "G2b", "G3b", rhs_ev=base_ev + 16)
    bw = [(S["b_ro"], 16), (S["b_bl"], 16)]
    if g > 0:
        bw.append((S["b_mt"], g * 80 - 16))
    _build(E, S, T, T["bufA"],
           [(C[8], None), (-C[9], "Gb"), (C[10], "G2b"), (-C[11], "G3b")],
           gate_waits=bw, inc=False)
    _build(E, S, T, T["bufB"],
           [(C[4], None), (-C[5], "Gb"), (C[6], "G2b"), (-C[7], "G3b")])
    _product(E, S, T, P, g, 2, "G2b", "G2b", "G4b", rhs_ev=base_ev + 16)
    _build(E, S, T, T["bufA"], [(C[12], "G4b")], accumulate=True)
    _product(E, S, T, P, g, 3, "G4b", "bufA", "bufD", add_name="bufB",
             lhs_waits=[(S["b_bl"], g * 3 + 2 + 16)], rhs_ev=base_ev + 48)
    _build(E, S, T, T["bufB"],
           [(C[0], None), (C[1], "Gb"), (C[2], "G2b"), (C[3], "G3b")])
    _product(E, S, T, P, g, 4, "bufD", "G4b", "Rout", add_name="bufB",
             lhs_waits=[(S["b_bl"], g * 3 + 3 + 16), (S["b_ev"], base_ev + 64)],
             out_dma="d_Rpart")


def make_expm_tensors(nc, es, T):
    def sb(name, shape, dt):
        return es.enter_context(nc.sbuf_tensor(name, shape, dt))
    for n in ("Gb", "GTb", "G2b", "G3b", "G4b", "bufA", "bufB", "bufD", "Irow"):
        T[n] = sb(n, [128, 8 * 1024], BF16)
    T["Rout"] = nc.alloc_sbuf_tensor_at(
        "Rout", [128, 8 * 1024], BF16, offset=nc.lookup_mloc(T["G2b"]).addr)
    T["Gf32"] = nc.alloc_sbuf_tensor_at(
        "Gf32", [128, 8 * 1024], F32, offset=nc.lookup_mloc(T["G4b"]).addr)
    return T


def make_expm_psum(nc, es):
    return {"PB": [es.enter_context(nc.psum_tensor("PBe%d" % k, [128, 512], F32))
                   for k in range(2)]}



def build_program():
    nc = bass.Bass("TRN2", target_bir_lowering=False)
    T = {}
    T["d_x"] = nc.dram_tensor("x", [2048, 1024], F32, kind="ExternalInput")
    for w in ("Wq", "Wk", "Wv", "Wo"):
        T["d_" + w] = nc.dram_tensor(w, [1024, 1024], F32, kind="ExternalInput")
    T["d_bo"] = nc.dram_tensor("bo", [1024], F32, kind="ExternalInput")
    T["d_ln_gamma_t"] = nc.dram_tensor("ln_gamma_t", [128, 8], F32, kind="ExternalInput")
    T["d_ln_beta_t"] = nc.dram_tensor("ln_beta_t", [128, 8], F32, kind="ExternalInput")
    T["d_ln_gamma"] = nc.dram_tensor("ln_gamma", [1024], F32, kind="ExternalInput")
    T["d_ln_beta"] = nc.dram_tensor("ln_beta", [1024], F32, kind="ExternalInput")
    T["d_abel_w"] = nc.dram_tensor("abel_w", [1024, 1024], F32, kind="ExternalInput")
    T["d_abel_b_t"] = nc.dram_tensor("abel_b_t", [128, 8], F32, kind="ExternalInput")
    T["d_gate_w_t"] = nc.dram_tensor("gate_w_t", [128, 64], F32, kind="ExternalInput")
    T["d_gate_b"] = nc.dram_tensor("moe_gate_b", [8], F32, kind="ExternalInput")
    T["d_ew1"] = nc.dram_tensor("ew1", [8, 1024, 2048], F32, kind="ExternalInput")
    T["d_eb1_t"] = nc.dram_tensor("eb1_t", [128, 128], F32, kind="ExternalInput")
    T["d_ew2"] = nc.dram_tensor("ew2", [8, 2048, 1024], F32, kind="ExternalInput")
    T["d_eb2"] = nc.dram_tensor("eb2", [8, 1024], F32, kind="ExternalInput")
    T["d_gc1_t"] = nc.dram_tensor("gc1_t", [128, 8], F32, kind="ExternalInput")
    T["d_gc2_t"] = nc.dram_tensor("gc2_t", [128, 8], F32, kind="ExternalInput")
    T["d_gc_b"] = nc.dram_tensor("gc_b", [1], F32, kind="ExternalInput")
    T["d_mygen"] = nc.dram_tensor("mygen", [1024, 1024], F32, kind="ExternalInput")
    T["c_alibi"] = nc.dram_tensor("c_alibi", [128, 640], F32, kind="ExternalInput")
    T["c_mask"] = nc.dram_tensor("c_mask", [128, 2048], BF16, kind="ExternalInput")
    T["c_ident"] = nc.dram_tensor("c_ident", [128, 128], BF16, kind="ExternalInput")
    T["c_ones"] = nc.dram_tensor("c_ones", [128], F32, kind="ExternalInput")
    T["c_id8"] = nc.dram_tensor("c_id8", [8, 8], F32, kind="ExternalInput")
    T["c_ones_bf"] = nc.dram_tensor("c_ones_bf", [128], BF16, kind="ExternalInput")
    T["c_irow"] = nc.dram_tensor("c_irow", [128, 8192], BF16, kind="ExternalInput")
    T["d_c_hilo"] = nc.dram_tensor("c_hilo", [128, 32], BF16, kind="ExternalInput")
    T["d_c_spad"] = nc.dram_tensor("c_spad", [128, 2], F32, kind="ExternalInput")
    T["d_c_siota"] = nc.dram_tensor("c_siota", [128, 768], F32, kind="ExternalInput")
    T["c_ones_f"] = T["c_ones"]
    T["d_x2"] = nc.dram_tensor("x2_scr", [2048, 1024], F32, kind="Internal")
    T["d_xn"] = nc.dram_tensor("xn_scr", [2176, 1032], BF16, kind="Internal")
    T["d_moe"] = nc.dram_tensor("moe_scr", [2176, 1024], BF16, kind="Internal")
    T["d_abel"] = nc.dram_tensor("abel_scr", [128, 16384], BF16, kind="Internal")
    T["d_Rpart"] = nc.dram_tensor("Rpart_scr", [128, 8192], BF16, kind="Internal")
    T["d_Rall"] = nc.dram_tensor("Rall_scr", [8, 128, 8192], BF16, kind="Internal")
    T["d_out"] = nc.dram_tensor("out", [2048, 1024], F32, kind="ExternalOutput")

    sem_stack = ExitStack()
    SB = expm_sems(nc, sem_stack)
    SA = attn_sems(nc, sem_stack)
    SF = ffn_sems(nc, sem_stack)
    all_sems = list(SB.values()) + list(SA.values()) + list(SF.values())

    es0 = ExitStack()
    make_expm_tensors(nc, es0, T)
    P0 = make_expm_psum(nc, es0)
    with nc.Block() as block:
        for name in ENGINES:
            def body(eng, name=name):
                E = Emitter(nc, name, eng, all_sems)
                emit_expm(E, SB, T, P0)
            getattr(block, name)(body)
    es0.close()

    T["sem_cc"] = SB["b_cc"]
    es1 = ExitStack()
    make_attn_tensors(nc, es1, T)
    P1 = make_attn_psum(nc, es1)
    with nc.Block() as block:
        for name in ENGINES:
            def body(eng, name=name):
                E = Emitter(nc, name, eng, all_sems)
                emit_attention(E, SA, T, P1,
                               extra=lambda EE: emit_expm_allgather(EE, SB, T))
            getattr(block, name)(body)
    es1.close()

    es2 = ExitStack()
    make_ffn_tensors(nc, es2, T)
    P2 = make_ffn_psum(nc, es2)
    with nc.Block() as block:
        for name in ENGINES:
            def body(eng, name=name):
                E = Emitter(nc, name, eng, all_sems)
                emit_ffn(E, SF, T, P2)
            getattr(block, name)(body)
    es2.close()
    sem_stack.close()
    return nc


def _host_irow():
    import ml_dtypes
    ir = np.zeros((128, 8192), dtype=np.float32)
    p = np.arange(128)
    for dc in range(8):
        ir[p, dc * 1024 + dc * 128 + p] = 1.0
    return np.asarray(ir.astype(ml_dtypes.bfloat16))


def _host_hilo():
    import ml_dtypes
    p = np.arange(128)
    hilo = np.zeros((128, 32), dtype=np.float32)
    for tb in range(16):
        hilo[:, 2 * tb] = tb * 8 + (p >> 4) + 1
        hilo[:, 2 * tb + 1] = p & 15
    return np.asarray(hilo.astype(ml_dtypes.bfloat16))


_PROGRAM_CACHE = {}


def kernel(**inputs):
    import ml_dtypes
    x_full = np.ascontiguousarray(np.asarray(inputs["L0_attn_x"], dtype=np.float32))
    gens = np.ascontiguousarray(np.asarray(inputs["L0_ffn_gens"], dtype=np.float32))
    common = dict(
        Wq=np.ascontiguousarray(inputs["L0_attn_Wq"], dtype=np.float32),
        Wk=np.ascontiguousarray(inputs["L0_attn_Wk"], dtype=np.float32),
        Wv=np.ascontiguousarray(inputs["L0_attn_Wv"], dtype=np.float32),
        Wo=np.ascontiguousarray(inputs["L0_attn_Wo"], dtype=np.float32),
        bo=np.ascontiguousarray(inputs["L0_attn_bo"], dtype=np.float32),
        abel_w=np.ascontiguousarray(inputs["L0_ffn_abel_w"], dtype=np.float32),
        moe_gate_b=np.ascontiguousarray(inputs["L0_ffn_moe_gate_b"], dtype=np.float32),
        ew1=np.ascontiguousarray(inputs["L0_ffn_ew1"], dtype=np.float32),
        ew2=np.ascontiguousarray(inputs["L0_ffn_ew2"], dtype=np.float32),
        eb2=np.ascontiguousarray(inputs["L0_ffn_eb2"], dtype=np.float32),
        gc_b=np.ascontiguousarray(inputs["L0_ffn_gc_b"], dtype=np.float32),

        ln_gamma=np.ascontiguousarray(inputs["L0_ffn_ln_gamma"], dtype=np.float32),
        ln_beta=np.ascontiguousarray(inputs["L0_ffn_ln_beta"], dtype=np.float32),
        ln_gamma_t=np.ascontiguousarray(
            np.asarray(inputs["L0_ffn_ln_gamma"], dtype=np.float32).reshape(8, 128).T),
        ln_beta_t=np.ascontiguousarray(
            np.asarray(inputs["L0_ffn_ln_beta"], dtype=np.float32).reshape(8, 128).T),
        abel_b_t=np.ascontiguousarray(
            np.asarray(inputs["L0_ffn_abel_b"], dtype=np.float32).reshape(8, 128).T),
        gate_w_t=np.ascontiguousarray(
            np.asarray(inputs["L0_ffn_moe_gate_w"], dtype=np.float32)
            .reshape(8, 128, 8).transpose(1, 0, 2).reshape(128, 64)),
        eb1_t=np.ascontiguousarray(
            np.asarray(inputs["L0_ffn_eb1"], dtype=np.float32)
            .reshape(8, 2, 8, 128).transpose(3, 0, 1, 2).reshape(128, 128)),
        gc1_t=np.ascontiguousarray(
            np.asarray(inputs["L0_ffn_gc_w"], dtype=np.float32)[:1024, 0]
            .reshape(8, 128).T),
        gc2_t=np.ascontiguousarray(
            np.asarray(inputs["L0_ffn_gc_w"], dtype=np.float32)[1024:, 0]
            .reshape(8, 128).T),
        c_irow=_host_irow(),
        c_hilo=_host_hilo(),
        c_spad=np.stack([2064.0 + np.arange(128), 2048.0 + np.arange(128)],
                        axis=1).astype(np.float32),
        c_siota=np.broadcast_to(
            np.arange(768, dtype=np.float32), (128, 768)).copy(),
        c_id8=np.eye(8, dtype=np.float32),
        c_ones_bf=np.ones(128, dtype=np.float32).astype(ml_dtypes.bfloat16),
        **host_consts(),
    )
    in_maps = [dict(common, x=np.ascontiguousarray(x_full[b]),
                    mygen=np.ascontiguousarray(gens[b % 3]))
               for b in range(8)]
    if "nc" not in _PROGRAM_CACHE:
        _PROGRAM_CACHE["nc"] = build_program()
    nc = _PROGRAM_CACHE["nc"]
    res = run_bass_kernel_spmd(nc, in_maps, core_ids=list(range(8)))
    return np.stack([np.asarray(res.results[b]["out"]) for b in range(8)], axis=0)


# version 15
# speedup vs baseline: 1.0162x; 1.0134x over previous
import math
from contextlib import ExitStack
import numpy as np
import concourse.bass as bass
import concourse.mybir as mybir
from concourse.bass_utils import run_bass_kernel_spmd


F32 = mybir.dt.float32
BF16 = mybir.dt.bfloat16
AF = mybir.ActivationFunctionType
ALU = mybir.AluOpType
AX = mybir.AxisListType

ENGINES = ("sync", "tensor", "vector", "scalar", "gpsimd")


class Sem:

    def __init__(self, nc, es, name):
        self.h = es.enter_context(nc.semaphore(name))
        self.name = name


class Emitter:
    def __init__(self, nc, me, eng, sems):
        self.nc = nc
        self.me = me
        self.eng = eng
        self.n = {s.name: 0 for s in sems}

    def cnt(self, sem):
        return self.n[sem.name]

    def bump(self, sem, amt):
        self.n[sem.name] += amt
        return self.n[sem.name]

    def do(self, engine, fn, wait=(), inc=None, inc_amt=1):
        if self.me == engine:
            for s, v in wait:
                if v > 0:
                    self.eng.wait_ge(s.h, v)
            ins = fn(self.eng)
            if inc is not None:
                ins.then_inc(inc.h, inc_amt)
        if inc is not None:
            self.bump(inc, inc_amt)

    def wait(self, engine, sem, value):
        if self.me == engine and value > 0:
            self.eng.wait_ge(sem.h, value)

    def drain(self, engine):
        if self.me == engine:
            self.eng.drain()


def ceil_div(a, b):
    return (a + b - 1) // b


def alibi_slopes(n):
    def pow2(n):
        start = 2 ** (-(2 ** -(math.log2(n) - 3)))
        return [start * start ** i for i in range(n)]
    if math.log2(n).is_integer():
        return pow2(n)
    c = 2 ** math.floor(math.log2(n))
    return pow2(c) + alibi_slopes(2 * c)[0::2][: n - c]


def host_consts(H=16, N=2048, HD=64):
    import ml_dtypes
    slopes = np.asarray(alibi_slopes(H), dtype=np.float64)
    ali = np.zeros((128, H * 40), dtype=np.float32)
    p = np.arange(128)
    off = [0, 4, 12, 24]
    for h in range(H):
        for qc in range(4):
            for kt in range(4 * (qc + 1)):
                ali[:, h * 40 + off[qc] + kt] = (
                    slopes[h] * (kt * 128 + p - qc * 512) / 8.0
                ).astype(np.float32)
    msk = np.zeros((128, 4 * 512), dtype=np.float32)
    f = np.arange(512)
    for j in range(4):
        m = (f[None, :] < (p[:, None] + 128 * j)).astype(np.float32) * (-1e30)
        msk[:, j * 512:(j + 1) * 512] = m
    ident = np.eye(128).astype(ml_dtypes.bfloat16)
    ones = np.ones((128,), dtype=np.float32)
    return dict(c_alibi=ali, c_mask=msk.astype(ml_dtypes.bfloat16),
                c_ident=np.asarray(ident), c_ones=ones)



EPS = 1e-5

ATTN_SEM_NAMES = [
    "s_xin0", "s_xin1", "s_v3", "s_ln", "s_v4", "s_tr", "s_ev",
    "s_w", "s_a2t", "s_a2v",
    "s_sc", "s_msk", "s_exp", "s_avm", "s_rcp", "s_bc", "s_bcc", "s_nrm",
    "s_const", "s_wot", "s_x2v", "s_x2o0", "s_x2o1", "s_bob", "s_bobv", "s_eps",
    "s_xbf", "s_sq", "s_ve", "s_sd"]


def attn_sems(nc, es):
    return {n: Sem(nc, es, n) for n in ATTN_SEM_NAMES}


def emit_attention(E, S, T, P, extra=None):
    emit_a0_consts(E, S, T)
    emit_a1_ln_transpose(E, S, T, P)
    if extra is not None:
        extra(E)
    emit_a2_qkv(E, S, T, P)
    emit_a3_scores(E, S, T, P)
    emit_a4_out(E, S, T, P)


def emit_a0_consts(E, S, T):
    for q in range(4):
        E.do("gpsimd", lambda e, q=q: e.dma_start(
            T["xbf"][:, q * 4096:(q + 1) * 4096].rearrange(
                "p (st c) -> p st c", st=4),
            T["d_x"].ap()[q * 512:(q + 1) * 512, :].rearrange(
                "(st p) c -> p st c", p=128)),
            wait=([(S["s_xbf"], 16 * q)] if q > 0 else []),
            inc=S["s_xbf"], inc_amt=16)
    s_const = S["s_const"]
    E.do("sync", lambda e: e.dma_start(T["alibi"][:, :], T["c_alibi"].ap()),
         inc=s_const, inc_amt=16)
    E.do("sync", lambda e: e.dma_start(T["ident"][:, :], T["c_ident"].ap()),
         inc=s_const, inc_amt=16)
    E.do("sync", lambda e: e.dma_start(T["mask"][:, :], T["c_mask"].ap()),
         inc=s_const, inc_amt=16)
    E.do("sync", lambda e: e.dma_start(
        T["ones"][:, :], T["c_ones"].ap().rearrange("(o f) -> o f", o=1)),
        inc=s_const, inc_amt=16)
    E.do("sync", lambda e: e.dma_start(
        T["bo_row"][:, :], T["d_bo"].ap().rearrange("(o f) -> o f", o=1)),
        inc=s_const, inc_amt=16)
    E.do("gpsimd", lambda e: e.dma_start(
        T["W_sb"][:, :].rearrange("p (dc c) -> p dc c", dc=8),
        T["d_Wk"].ap().rearrange("(dc p) c -> p dc c", p=128)),
        inc=S["s_w"], inc_amt=16)


def emit_a1_ln_transpose(E, S, T, P):
    y1 = T["y1_tile"]
    E.do("vector", lambda e: e.memset(T["eps_t"][:, :], EPS), inc=S["s_eps"])
    for q in range(4):
        qs = slice(q * 4, (q + 1) * 4)
        for st in range(q * 4, q * 4 + 4):
            E.do("vector", lambda e, st=st: e.reduce_sum(
                T["st_m16"][:, st:st + 1],
                T["xbf"][:, st * 1024:(st + 1) * 1024], axis=AX.X),
                wait=([(S["s_xbf"], 16 * (q + 1))] if st % 4 == 0 else []))
            E.do("scalar", lambda e, st=st: e.activation(
                T["sq_scratch"][st % 2][:, :],
                T["xbf"][:, st * 1024:(st + 1) * 1024],
                AF.Square, accum_out=T["st_q16"][:, st:st + 1]),
                wait=([(S["s_xbf"], 16 * (q + 1))] if st % 4 == 0 else []),
                inc=(S["s_sq"] if st % 4 == 3 else None))
            E.drain("scalar")
        E.drain("vector")
        E.do("vector", lambda e, qs=qs: e.tensor_scalar_mul(
            T["st_m16"][:, qs], T["st_m16"][:, qs], 1.0 / 1024))
        E.drain("vector")
        E.do("vector", lambda e, qs=qs: e.tensor_scalar_mul(
            T["st_n16"][:, qs], T["st_m16"][:, qs], -1.0))
        E.drain("vector")
        E.do("vector", lambda e, qs=qs: e.tensor_tensor(
            T["st_v16"][:, qs], T["st_n16"][:, qs], T["st_m16"][:, qs],
            ALU.mult))
        E.drain("vector")
        E.do("vector", lambda e, qs=qs: e.scalar_tensor_tensor(
            T["st_v16"][:, qs], T["st_q16"][:, qs], 1.0 / 1024,
            T["st_v16"][:, qs], ALU.mult, ALU.add),
            wait=[(S["s_sq"], q + 1)], inc=S["s_ve"])
        E.do("scalar", lambda e, qs=qs: e.activation(
            T["st_d16"][:, qs], T["st_v16"][:, qs], AF.Sqrt,
            bias=T["eps_t"][:, :]),
            wait=[(S["s_ve"], q + 1), (S["s_eps"], 1)], inc=S["s_sd"])
        E.do("vector", lambda e, qs=qs: e.reciprocal(
            T["st_r16"][:, qs], T["st_d16"][:, qs]),
            wait=[(S["s_sd"], q + 1)])
        E.drain("vector")
        for st in range(q * 4, q * 4 + 4):
            par = st % 2
            E.do("vector", lambda e, st=st, par=par: e.tensor_scalar(
                y1[par][:, :], T["xbf"][:, st * 1024:(st + 1) * 1024],
                T["st_n16"][:, st:st + 1], T["st_r16"][:, st:st + 1],
                ALU.add, ALU.mult),
                wait=[(S["s_tr"], st - 1)], inc=S["s_v4"])
            for dc in range(8):
                E.do("tensor", lambda e, dc=dc, par=par: e.transpose(
                    P["PT"][par][:, dc * 128:(dc + 1) * 128],
                    y1[par][:, dc * 128:(dc + 1) * 128], T["ident"][:, :]),
                    wait=([(S["s_v4"], st + 1), (S["s_ev"], st - 1),
                           (S["s_const"], 80)] if dc == 0 else []),
                    inc=(S["s_tr"] if dc == 7 else None))
            E.do("vector", lambda e, st=st, par=par: e.tensor_copy(
                T["y1T"][:, :].rearrange("p (dc s) -> p dc s", dc=8)[
                    :, :, st * 128:(st + 1) * 128],
                P["PT"][par][:, :].rearrange("p (dc c) -> p dc c", dc=8)),
                wait=[(S["s_tr"], st + 1)], inc=S["s_ev"])


def _qkv_groups():
    gs = []
    for hp in range(8):
        for c in range(4):
            gs.append(("kt", hp, c))
    for st in range(16):
        for half in range(2):
            gs.append(("vn", st, half))
    for hp in range(8):
        for c in range(4):
            gs.append(("qt", hp, c))
    return gs


def emit_a2_qkv(E, S, T, P):
    W, y1T = T["W_sb"], T["y1T"]
    groups = _qkv_groups()
    for gi, g in enumerate(groups):
        kind = g[0]
        par = gi % 2
        psum = P["P01"][par]
        if kind == "vn" and g[1] == 0 and g[2] == 0:
            E.do("gpsimd", lambda e: e.dma_start(
                T["W_sb"][:, :].rearrange("p (dc c) -> p dc c", dc=8),
                T["d_Wv"].ap().rearrange("(dc p) c -> p dc c", p=128)),
                wait=[(S["s_a2t"], 32)], inc=S["s_w"], inc_amt=16)
        if kind == "qt" and g[1] == 0 and g[2] == 0:
            E.do("gpsimd", lambda e: e.dma_start(
                T["W_sb"][:, :].rearrange("p (dc c) -> p dc c", dc=8),
                T["d_Wq"].ap().rearrange("(dc p) c -> p dc c", p=128)),
                wait=[(S["s_a2t"], 64)], inc=S["s_w"], inc_amt=16)
        wload = 16 if kind == "kt" else (32 if kind == "vn" else 48)
        for dc in range(8):
            if kind in ("kt", "qt"):
                _, hp, c = g
                lhs = W[:, dc * 1024 + hp * 128: dc * 1024 + (hp + 1) * 128]
                rhs = y1T[:, dc * 2048 + c * 512: dc * 2048 + (c + 1) * 512]
            else:
                _, st, half = g
                lhs = y1T[:, dc * 2048 + st * 128: dc * 2048 + st * 128 + 128]
                rhs = W[:, dc * 1024 + half * 512: dc * 1024 + half * 512 + 512]
            waits = []
            if dc == 0:
                waits = [(S["s_w"], wload), (S["s_a2v"], gi - 1)]
                if gi == 0:
                    waits.append((S["s_ev"], 16))
            E.do("tensor", lambda e, o=psum, l=lhs, r=rhs, dc=dc: e.matmul(
                o[:, :], l, r, start=(dc == 0), stop=(dc == 7)),
                wait=waits, inc=(S["s_a2t"] if dc == 7 else None))
        if kind in ("kt", "qt"):
            _, hp, c = g
            dst_t = T["KT"] if kind == "kt" else T["QT"]
            dst = dst_t[:, hp * 2048 + c * 512: hp * 2048 + (c + 1) * 512]
            E.do("vector", lambda e, d=dst, o=psum: e.tensor_copy(d, o[:, :]),
                 wait=[(S["s_a2t"], gi + 1)], inc=S["s_a2v"])
        else:
            _, st, half = g
            base = st * 1040 + half * 520
            dst = T["VN"][:, base: base + 520].rearrange(
                "p (h j) -> p h j", h=8)[:, :, 0:64]
            E.do("vector", lambda e, d=dst, o=psum: e.tensor_copy(
                d, o[:, :].rearrange("p (h j) -> p h j", h=8)),
                wait=[(S["s_a2t"], gi + 1)], inc=S["s_a2v"])
    E.do("vector", lambda e: e.memset(
        T["VN"][:, :].rearrange("p (g j) -> p g j", j=65)[:, :, 64:65], 1.0),
        inc=S["s_a2v"])


def _a3_iters():
    return [(h, qc) for h in range(16) for qc in range(4)]


def emit_a3_scores(E, S, T, P):
    iters = _a3_iters()
    KT, QT, VN = T["KT"], T["QT"], T["VN"]
    tiles = []
    sc_count, msk_count, exp_count, avm_count = {}, {}, {}, {}
    n_sc = n_msk = n_exp = n_avm = 0
    for i, (h, qc) in enumerate(iters):
        for j in range(4 * (qc + 1)):
            t = (i, j)
            tiles.append(t)
            n_sc += 1
            sc_count[t] = n_sc
            if j >= 4 * qc:
                n_msk += 1
            msk_count[t] = n_msk
            n_exp += 1
            exp_count[t] = n_exp
            n_avm += 1
            avm_count[t] = n_avm
    tile_idx = {t: n for n, t in enumerate(tiles)}

    def prev_tile(t, back):
        n = tile_idx[t] - back
        return tiles[n] if n >= 0 else None

    def attnv(i, h, qc, j, K):
        t = (i, j)
        waits = [(S["s_exp"], exp_count[t])]
        if j == 0:
            waits.append((S["s_nrm"], i - 1))
        E.do("tensor", lambda e, t=t, j=j, i=i, h=h: e.matmul(
            P["PC"][i % 2][:, :],
            VN[:, j * 1040 + h * 65: j * 1040 + (h + 1) * 65],
            T["expb"][tile_idx[t] % 3][:, :],
            start=(j == 0), stop=(j == K - 1)),
            wait=waits, inc=S["s_avm"])

    for i, (h, qc) in enumerate(iters):
        po = 64 * (h % 2)
        hp = h // 2
        K = 4 * (qc + 1)
        for j in range(K):
            t = (i, j)
            pt = prev_tile(t, 3)
            waits = [(S["s_exp"], exp_count[pt] if pt else 0)]
            if i == 0 and j == 0:
                waits.append((S["s_a2v"], 97))
                waits.append((S["s_const"], 80))
            lhs = KT[po:po + 64, hp * 2048 + j * 128: hp * 2048 + (j + 1) * 128]
            rhs = QT[po:po + 64, hp * 2048 + qc * 512: hp * 2048 + (qc + 1) * 512]
            E.do("tensor", lambda e, l=lhs, r=rhs, t=t: e.matmul(
                P["P01"][tile_idx[t] % 3][:, :], l, r, start=True, stop=True),
                wait=waits, inc=S["s_sc"])
            if j == 1 and i >= 1:
                E.do("tensor", lambda e, i=i: e.matmul(
                    P["PB"][0][:, :], T["ones"][0:1, 0:64],
                    T["rd"][(i - 1) % 2][:, :], start=True, stop=True),
                    wait=[(S["s_rcp"], i), (S["s_bcc"], i - 1)], inc=S["s_bc"])
            if j >= 2:
                attnv(i, h, qc, j - 2, K)
        attnv(i, h, qc, K - 2, K)
        attnv(i, h, qc, K - 1, K)

        if i >= 1:
            _normalize(E, S, T, P, i - 1)
        for j in range(4 * qc, K):
            jj = j - 4 * qc
            E.do("vector", lambda e, j=j, jj=jj, i=i, t=(i, j): e.tensor_tensor(
                P["P01"][tile_idx[t] % 3][:, :], P["P01"][tile_idx[t] % 3][:, :],
                T["mask"][:, jj * 512:(jj + 1) * 512], ALU.add),
                wait=[(S["s_sc"], sc_count[(i, j)])], inc=S["s_msk"])
        E.do("vector", lambda e, i=i: e.reciprocal(
            T["rd"][i % 2][:, :], P["PC"][i % 2][64:65, :]),
            wait=[(S["s_avm"], avm_count[(i, K - 1)]), (S["s_bc"], i - 1)],
            inc=S["s_rcp"])

        if i >= 1:
            E.do("scalar", lambda e, i=i: e.activation(
                T["bc_sb"][(i - 1) % 2][:, :], P["PB"][0][:, :],
                AF.Identity),
                wait=[(S["s_bc"], i), (S["s_nrm"], i - 2)], inc=S["s_bcc"])
        for j in range(K):
            t = (i, j)
            idx = h * 40 + [0, 4, 12, 24][qc] + j
            diag = j >= 4 * qc
            w = [(S["s_msk"], msk_count[t])] if diag else [(S["s_sc"], sc_count[t])]
            pt = prev_tile(t, 3)
            if pt:
                w.append((S["s_avm"], avm_count[pt]))
            E.do("scalar", lambda e, j=j, idx=idx, t=t: e.activation(
                T["expb"][tile_idx[t] % 3][:, :], P["P01"][tile_idx[t] % 3][:, :],
                AF.Exp, bias=T["alibi"][:, idx:idx + 1], scale=0.125),
                wait=w, inc=S["s_exp"])
    i_last = len(iters) - 1
    E.do("tensor", lambda e: e.matmul(
        P["PB"][0][:, :], T["ones"][0:1, 0:64],
        T["rd"][i_last % 2][:, :], start=True, stop=True),
        wait=[(S["s_rcp"], i_last + 1), (S["s_bcc"], i_last)], inc=S["s_bc"])
    E.do("scalar", lambda e: e.activation(
        T["bc_sb"][i_last % 2][:, :], P["PB"][0][:, :], AF.Identity),
        wait=[(S["s_bc"], i_last + 1), (S["s_nrm"], i_last - 1)], inc=S["s_bcc"])
    _normalize(E, S, T, P, i_last)


def _normalize(E, S, T, P, i):
    h, qc = _a3_iters()[i]
    po = 64 * (h % 2)
    hp = h // 2
    dst = T["ctxT"][po:po + 64, hp * 2048 + qc * 512: hp * 2048 + (qc + 1) * 512]
    E.do("vector", lambda e, d=dst, i=i: e.tensor_tensor(
        d, P["PC"][i % 2][0:64, :], T["bc_sb"][i % 2][:, :], ALU.mult),
        wait=[(S["s_bcc"], i + 1)], inc=S["s_nrm"])


def emit_a4_out(E, S, T, P):
    n_iters = len(_a3_iters())
    exp_total = E.cnt(S["s_exp"])
    msk_total = E.cnt(S["s_msk"])
    E.do("gpsimd", lambda e: e.dma_start(
        T["W_sb"][:, :].rearrange("p (dc c) -> p dc c", dc=8),
        T["d_Wo"].ap().rearrange("(dc p) c -> p dc c", p=128)),
        wait=[(S["s_a2t"], 96)], inc=S["s_w"], inc_amt=16)
    for dq in range(2):
        E.do("tensor", lambda e, dq=dq: e.matmul(
            P["P01"][dq][:, :], T["ones"][0:1, 0:128],
            T["bo_row"][0:1, dq * 512:(dq + 1) * 512], start=True, stop=True),
            wait=([(S["s_exp"], exp_total), (S["s_msk"], msk_total),
                   (S["s_const"], 80)] if dq == 0 else []),
            inc=S["s_bob"])
    E.do("vector", lambda e: e.tensor_copy(
        T["bo_bc"][:, 0:512], P["P01"][0][:, :]), wait=[(S["s_bob"], 2)])
    E.do("vector", lambda e: e.tensor_copy(
        T["bo_bc"][:, 512:1024], P["P01"][1][:, :]), inc=S["s_bobv"])

    x_t, x2b = T["x_tile"], T["x2_buf"]
    for st in range(16):
        par = st % 2
        xin_waits = [(S["s_x2v"], 2 * (st - 1))]
        if st < 2:
            xin_waits.append((S["s_nrm"], n_iters))
        E.do("sync", lambda e, st=st, par=par: e.dma_start(
            x_t[par][:, :], T["d_x"].ap()[st * 128:(st + 1) * 128, :]),
            wait=xin_waits, inc=S["s_xin%d" % par], inc_amt=16)
        for dq in range(2):
            gidx = st * 2 + dq
            for dc in range(8):
                waits = []
                if dc == 0:
                    waits.append((S["s_x2v"], gidx - 1))
                    if gidx == 0:
                        waits.extend([(S["s_nrm"], n_iters), (S["s_w"], 64),
                                      (S["s_bobv"], 1)])
                E.do("tensor", lambda e, st=st, dq=dq, dc=dc, pp=gidx % 2: e.matmul(
                    P["P01"][pp][:, :],
                    T["ctxT"][:, dc * 2048 + st * 128: dc * 2048 + st * 128 + 128],
                    T["W_sb"][:, dc * 1024 + dq * 512: dc * 1024 + dq * 512 + 512],
                    start=(dc == 0), stop=(dc == 7)),
                    wait=waits, inc=(S["s_wot"] if dc == 7 else None))
            E.do("vector", lambda e, dq=dq, pp=gidx % 2, xp=par: e.tensor_tensor(
                x2b[xp][:, dq * 512:(dq + 1) * 512], P["P01"][pp][:, :],
                x_t[xp][:, dq * 512:(dq + 1) * 512], ALU.add),
                wait=[(S["s_wot"], gidx + 1),
                      (S["s_xin%d" % par], 16 * (st // 2 + 1)),
                      (S["s_x2o%d" % par], 16 * (st // 2))])
            E.drain("vector")
            E.do("vector", lambda e, dq=dq, xp=par: e.tensor_tensor(
                x2b[xp][:, dq * 512:(dq + 1) * 512],
                x2b[xp][:, dq * 512:(dq + 1) * 512],
                T["bo_bc"][:, dq * 512:(dq + 1) * 512], ALU.add),
                inc=S["s_x2v"])
        E.do("sync", lambda e, st=st, par=par: e.dma_start(
            T["d_x2"].ap()[st * 128:(st + 1) * 128, :], x2b[par][:, :]),
            wait=[(S["s_x2v"], 2 * (st + 1))], inc=S["s_x2o%d" % par], inc_amt=16)
    E.wait("sync", S["s_x2o0"], 16 * 8)
    E.wait("sync", S["s_x2o1"], 16 * 8)


def make_attn_tensors(nc, es, T):
    def sb(name, shape, dt):
        return es.enter_context(nc.sbuf_tensor(name, shape, dt))
    T["KT"] = sb("KT", [128, 8 * 2048], BF16)
    T["VN"] = sb("VN", [128, 16 * 1040], BF16)
    T["QT"] = sb("QT", [128, 8 * 2048], BF16)
    T["ctxT"] = sb("ctxT", [128, 8 * 2048], BF16)
    T["y1T"] = sb("y1T", [128, 8 * 2048], BF16)
    T["W_sb"] = sb("W_sb", [128, 8 * 1024], BF16)
    T["alibi"] = sb("alibi", [128, 640], F32)
    T["mask"] = sb("mask", [128, 2048], BF16)
    T["ident"] = sb("ident", [128, 128], BF16)
    T["ones"] = sb("ones", [1, 128], F32)
    T["bo_row"] = sb("bo_row", [1, 1024], F32)
    T["bo_bc"] = sb("bo_bc", [128, 1024], F32)
    T["x_tile"] = [sb("x_tile0", [128, 1024], F32), sb("x_tile1", [128, 1024], F32)]
    T["xbf"] = nc.alloc_sbuf_tensor_at(
        "xbf", [128, 16 * 1024], BF16, offset=nc.lookup_mloc(T["ctxT"]).addr)
    T["sq_scratch"] = [
        nc.alloc_sbuf_tensor_at(
            "sq_scratch0", [128, 1024], F32, offset=nc.lookup_mloc(T["QT"]).addr),
        nc.alloc_sbuf_tensor_at(
            "sq_scratch1", [128, 1024], F32,
            offset=nc.lookup_mloc(T["QT"]).addr + 4096),
    ]
    addr = nc.lookup_mloc(T["y1T"]).addr
    T["x2_buf"] = [
        nc.alloc_sbuf_tensor_at("x2b0", [128, 1024], F32, offset=addr),
        nc.alloc_sbuf_tensor_at("x2b1", [128, 1024], F32, offset=addr + 4096),
    ]
    for n in ("st_m16", "st_n16", "st_q16", "st_v16", "st_d16", "st_r16"):
        T[n] = sb(n, [128, 16], F32)
    T["eps_t"] = sb("eps_t", [128, 1], F32)
    kaddr = nc.lookup_mloc(T["KT"]).addr
    T["y1_tile"] = [
        nc.alloc_sbuf_tensor_at("y1_0", [128, 1024], BF16, offset=kaddr),
        nc.alloc_sbuf_tensor_at("y1_1", [128, 1024], BF16, offset=kaddr + 2048),
    ]
    T["expb"] = [sb("expb%d" % k, [128, 512], BF16) for k in range(3)]
    xa0 = nc.lookup_mloc(T["x_tile"][0]).addr
    xa1 = nc.lookup_mloc(T["x_tile"][1]).addr
    T["rd"] = [nc.alloc_sbuf_tensor_at("rd0", [1, 512], F32, offset=xa0),
               nc.alloc_sbuf_tensor_at("rd1", [1, 512], F32, offset=xa0 + 2048)]
    T["bc_sb"] = [nc.alloc_sbuf_tensor_at("bc0", [64, 512], F32, offset=xa1),
                  nc.alloc_sbuf_tensor_at("bc1", [64, 512], F32, offset=xa1 + 2048)]
    return T


def make_attn_psum(nc, es):
    P = {}
    P["PT"] = [es.enter_context(nc.psum_tensor("PT%d" % k, [128, 1024], BF16))
               for k in range(2)]
    P["P01"] = [es.enter_context(nc.psum_tensor("P01_%d" % k, [128, 512], F32))
                for k in range(3)]
    P["PC"] = [es.enter_context(nc.psum_tensor("PC%d" % k, [65, 512], F32))
               for k in range(2)]
    P["PB"] = [es.enter_context(nc.psum_tensor("PB0", [64, 512], F32))]
    return P



EPS = 1e-5

FFN_SEM_NAMES = [
    "f_xin0", "f_xin1", "f_v4", "f_tr", "f_xnt",
    "f_cst", "f_gt", "f_gv", "f_wn",
    "f_at", "f_as", "f_glat", "f_glav", "f_abo",
    "f_ew0", "f_ew1", "f_hmm", "f_gel", "f_eomm", "f_eov",
    "f_e2t", "f_e2v",
    "f_r0", "f_r1", "f_r2", "f_trf", "f_trfv", "f_rot", "f_rotv",
    "f_glmt", "f_glmv", "f_gate", "f_gbt", "f_gbv", "f_mix",
    "f_ot", "f_ov", "f_out0", "f_out1", "f_abi", "f_cstg", "f_ab2",
    "f_xbf", "f_sq", "f_ve", "f_sd",
    "f_xnw0", "f_xnw1", "f_zm", "f_gth", "f_xev", "f_xtr",
    "f_tiv", "f_sct", "f_mld", "f_gbb", "f_xnv",
    "f_tiv2", "f_cmp", "f_inv", "f_ivv", "f_ti3",
    "f_ixe", "f_wgc"]


def ffn_sems(nc, es):
    return {n: Sem(nc, es, n) for n in FFN_SEM_NAMES}


def mm_di(E, fn, wait=(), inc=None):
    E.do("tensor", fn, wait=wait)
    E.drain("tensor")
    E.do("tensor", lambda e: e.nop(), inc=inc)


def emit_ffn(E, S, T, P):
    emit_f0_consts(E, S, T)
    emit_f1_ln2_xnt(E, S, T, P)
    emit_gbc_build(E, S, T, P)
    emit_f2_gating(E, S, T, P)
    emit_xn_spill(E, S, T)
    emit_slot_chain(E, S, T)
    emit_f3_abel(E, S, T, P)
    emit_f4_moe_sparse(E, S, T, P)
    emit_f5_rot_mix_out(E, S, T, P)

CAP = 768
NCH = CAP // 128


def emit_gbc_build(E, S, T, P):
    for i, (row, dst) in enumerate([("gamrow", "gamma_bc"), ("betrow", "beta_bc")]):
        for c in range(2):
            mm_di(E, lambda e, row=row, c=c: e.matmul(
                P["PS2"][:, :], T["ones_f"][0:1, :],
                T[row][0:1, c * 512:(c + 1) * 512], start=True, stop=True),
                wait=[(S["f_cst"], 16 * N_CST), (S["f_gbb"], 2 * i + c)],
                inc=S["f_tiv"])
            E.do("vector", lambda e, dst=dst, c=c, i=i: e.tensor_copy(
                T[dst][:, c * 512:(c + 1) * 512], P["PS2"][:, :]),
                wait=[(S["f_tiv"], 2 * i + c + 1)], inc=S["f_gbb"])


def emit_slot_chain(E, S, T):
    B1, B3 = T["ib1"], T["ib3"]
    E.do("vector", lambda e: e.tensor_single_scalar(
        B1[:, :], T["wT"][0:8, :], 0.0, ALU.is_gt),
        wait=[(S["f_wn"], 32), (S["f_e2v"], 16)])
    E.drain("vector")
    E.do("vector", lambda e: e.tensor_tensor_scan(
        B3[:, :], B1[:, :], B1[:, :], 0.0, ALU.add, ALU.bypass))
    E.drain("vector")
    E.do("vector", lambda e: e.tensor_tensor(
        T["ib2b"][:, :], B1[:, :], B3[:, :], ALU.mult))
    E.drain("vector")
    E.do("vector", lambda e: e.tensor_scalar_add(
        T["ib2b"][:, :], T["ib2b"][:, :], -1.0), inc=S["f_xnv"])


def emit_slot_transposes(E, S, T, P):
    for st in range(16):
        w = []
        if st == 0:
            w = [(S["f_xnv"], 18), (S["f_e2v"], 16)]
        E.do("tensor", lambda e, st=st: e.matmul(
            P["PS2"][:, st * 8:(st + 1) * 8],
            T["ib2b"][0:8, st * 128:(st + 1) * 128], T["id8"][0:8, :],
            start=True, stop=True), wait=w)
    E.drain("tensor")
    E.do("tensor", lambda e: e.nop(), inc=S["f_tiv"])
    E.do("vector", lambda e: e.tensor_copy(
        T["slotsT"][:, :], P["PS2"][:, 0:128]),
        wait=[(S["f_tiv"], 5)], inc=S["f_tiv2"])


def _inv_compares(E, S, T, e_):
    for tb in range(16):
        gi = e_ * 16 + tb
        w = []
        if gi >= 2:
            w.append((S["f_inv"], 2 * (gi - 1)))
        if gi == 0:
            w += [(S["f_tiv2"], 1), (S["f_abi"], 32), (S["f_gt"], 4)]
        E.do("vector", lambda e, e_=e_, tb=tb, gi=gi: e.tensor_scalar(
            T["MT"][gi % 2][:, :], T["siota"][:, :],
            T["slotsT"][:, tb * 8 + e_: tb * 8 + e_ + 1], None, ALU.is_equal),
            wait=w, inc=S["f_cmp"])


def _inv_mms(E, S, T, P, e_):
    for tb in range(16):
        gi = e_ * 16 + tb
        for sl in range(2):
            w = []
            if sl == 0:
                w.append((S["f_cmp"], gi + 1))
            if tb == 0:
                w.append((S["f_ivv"], 2 * e_))
                if e_ == 0 and sl == 0:
                    w.append((S["f_glav"], 4))
            if tb == 15:
                mm_di(E, lambda e, gi=gi, tb=tb, sl=sl: e.matmul(
                    (P["PS1"][0:2, 0:512] if sl == 0 else P["PS2"][0:2, 0:256]),
                    T["c_hilo_sb"][:, tb * 2:tb * 2 + 2],
                    T["MT"][gi % 2][:, sl * 512: 512 + sl * 256],
                    start=(tb == 0), stop=(tb == 15)),
                    wait=w, inc=S["f_inv"])
            else:
                E.do("tensor", lambda e, gi=gi, tb=tb, sl=sl: e.matmul(
                    (P["PS1"][0:2, 0:512] if sl == 0 else P["PS2"][0:2, 0:256]),
                    T["c_hilo_sb"][:, tb * 2:tb * 2 + 2],
                    T["MT"][gi % 2][:, sl * 512: 512 + sl * 256],
                    start=(tb == 0), stop=(tb == 15)),
                    wait=w, inc=S["f_inv"])


def _inv_finish(E, S, T, P, e_):
    E.do("vector", lambda e: e.tensor_copy(
        T["vrow"][0:2, 0:512], P["PS1"][0:2, 0:512]),
        wait=[(S["f_inv"], 32 * (e_ + 1)), (S["f_ti3"], NCH * e_)],
        inc=S["f_ivv"])
    E.do("vector", lambda e: e.tensor_copy(
        T["vrow"][0:2, 512:768], P["PS2"][0:2, 0:256]), inc=S["f_ivv"])
    for ci in range(NCH):
        n = e_ * NCH + ci
        c0 = 300 + 2 * (ci % 2)
        w = []
        if ci == 0:
            w.append((S["f_ivv"], 2 * (e_ + 1)))
        if n >= 2:
            w.append((S["f_ixe"], 2 * n - 2))
        mm_di(E, lambda e, ci=ci, c0=c0: e.matmul(
            P["PS2"][:, c0:c0 + 2], T["vrow"][0:2, ci * 128:(ci + 1) * 128],
            T["id8"][0:2, 0:2], start=True, stop=True),
            wait=w, inc=S["f_ti3"])
        E.do("vector", lambda e, c0=c0: e.tensor_copy(
            T["pse"][:, :], P["PS2"][:, c0:c0 + 2]),
            wait=[(S["f_ti3"], n + 1)], inc=S["f_ixe"])
        E.drain("vector")
        E.do("vector", lambda e: e.tensor_single_scalar(
            T["iocc"][:, :], T["pse"][:, 0:1], 0.5, ALU.is_gt))
        E.do("vector", lambda e: e.scalar_tensor_tensor(
            T["itmp"][:, :], T["pse"][:, 0:1], 16.0, T["pse"][:, 1:2],
            ALU.mult, ALU.add))
        E.drain("vector")
        E.do("vector", lambda e: e.tensor_tensor(
            T["itmp"][:, :], T["itmp"][:, :], T["c_spad_sb"][:, 0:1],
            ALU.subtract))
        E.drain("vector")
        E.do("vector", lambda e: e.tensor_tensor(
            T["itmp"][:, :], T["itmp"][:, :], T["iocc"][:, :], ALU.mult))
        E.drain("vector")
        E.do("vector", lambda e: e.tensor_tensor(
            T["itmp"][:, :], T["itmp"][:, :], T["c_spad_sb"][:, 1:2], ALU.add))
        E.drain("vector")
        E.do("vector", lambda e, ci=ci, e_=e_: e.tensor_copy(
            T["idxU"][:, ci * 8 + e_: ci * 8 + e_ + 1], T["itmp"][:, :]),
            inc=S["f_ixe"])


def _gathers(E, S, T, e_):
    from concourse.bass import IndirectOffsetOnAxis
    for g in range(NCH):
        w = [(S["f_ixe"], 2 * (NCH * e_ + g + 1))]
        if g == 0:
            w += [(S["f_xtr"], NCH * e_), (S["f_wgc"], NCH * e_)]
            if e_ == 0:
                w += [(S["f_xnw0"], 16 * 8), (S["f_xnw1"], 16 * 8),
                      (S["f_zm"], 16 * 34), (S["f_at"], 32)]
        E.do("gpsimd", lambda e, e_=e_, g=g: e.indirect_dma_start(
            T["gath"][:, g * 1032:(g + 1) * 1032], None,
            T["d_xn"].ap(),
            IndirectOffsetOnAxis(ap=T["idxU"][:, g * 8 + e_: g * 8 + e_ + 1],
                                 axis=0)),
            wait=w, inc=S["f_gth"], inc_amt=16)


def _xposes(E, S, T, P, e_):
    for g in range(NCH):
        par = g % 2
        for dc in range(8):
            w = []
            if dc == 0:
                w = [(S["f_gth"], 16 * NCH * (e_ + 1)),
                     (S["f_xev"], NCH * e_ + g - 1)]
                if e_ == 0 and g == 0:
                    w.append((S["f_xnt"], 16))
            E.do("tensor", lambda e, g=g, dc=dc, par=par: e.transpose(
                P["PTf"][par][:, dc * 128:(dc + 1) * 128],
                T["gath"][:, g * 1032 + dc * 128: g * 1032 + (dc + 1) * 128],
                T["ident"][:, :]),
                wait=w, inc=(S["f_xtr"] if dc == 7 else None))
        E.do("vector", lambda e, g=g, par=par: e.tensor_copy(
            T["xncT"][:, :].rearrange("p (dc c) -> p dc c", dc=8)[
                :, :, g * 128:(g + 1) * 128],
            P["PTf"][par][:, :].rearrange("p (dc c) -> p dc c", dc=8)),
            wait=[(S["f_xtr"], NCH * e_ + g + 1)], inc=S["f_xev"])
    for g in range(NCH):
        E.do("vector", lambda e, g=g, e_=e_: e.tensor_copy(
            T["wgc"][:, g * 8 + e_: g * 8 + e_ + 1],
            T["gath"][:, g * 1032 + 1024 + e_: g * 1032 + 1024 + e_ + 1]),
            wait=([(S["f_gth"], 16 * NCH * (e_ + 1))] if g == 0 else []),
            inc=S["f_wgc"])


def emit_xn_spill(E, S, T):
    E.do("vector", lambda e: e.memset(T["zt"][:, :], 0.0), inc=S["f_xnv"])
    for z in range(17):
        E.do("sync", lambda e, z=z: e.dma_start(
            T["d_moe"].ap()[z * 128:(z + 1) * 128, :], T["zt"][:, 0:1024]),
            wait=([(S["f_xnv"], 1)] if z == 0 else []), inc=S["f_zm"], inc_amt=16)
    E.do("sync", lambda e: e.dma_start(
        T["d_xn"].ap()[2048:2176, :], T["zt"][:, :]), inc=S["f_zm"], inc_amt=16)
    for st in range(16):
        E.do("sync", lambda e, st=st: e.dma_start(
            T["d_xn"].ap()[st * 128:(st + 1) * 128, 1024:1032],
            T["wNb"][:, st * 8:(st + 1) * 8]),
            wait=[(S["f_wn"], 2 * st + 2)], inc=S["f_zm"], inc_amt=16)
    for st0 in range(0, 16, 2):
        for q in range(2):
            st = st0 + q
            E.do("vector", lambda e, st=st, q=q: e.tensor_tensor(
                T["xnn"][q][:, :], T["y2_sb"][:, st * 1024:(st + 1) * 1024],
                T["gamma_bc"][:, :], ALU.mult),
                wait=[(S["f_v4"], st + 1), (S["f_gbb"], 4),
                      (S["f_xnw%d" % q], 16 * (st // 2))])
        E.drain("vector")
        for q in range(2):
            st = st0 + q
            E.do("vector", lambda e, st=st, q=q: e.tensor_tensor(
                T["xnn"][q][:, :], T["xnn"][q][:, :], T["beta_bc"][:, :],
                ALU.add), inc=S["f_xnv"])
        E.drain("vector")
        for q in range(2):
            st = st0 + q
            E.do("sync", lambda e, st=st, q=q: e.dma_start(
                T["d_xn"].ap()[st * 128:(st + 1) * 128, 0:1024], T["xnn"][q][:, :]),
                wait=[(S["f_xnv"], st + 2)], inc=S["f_xnw%d" % q], inc_amt=16)



N_CST = 14
N_CSTG = 4


def emit_f0_consts(E, S, T):
    for q in range(4):
        E.do("gpsimd", lambda e, q=q: e.dma_start(
            T["x2bf"][:, q * 4096:(q + 1) * 4096].rearrange(
                "p (st c) -> p st c", st=4),
            T["d_x2"].ap()[q * 512:(q + 1) * 512, :].rearrange(
                "(st p) c -> p st c", p=128)),
            wait=([(S["f_xbf"], 16 * q)] if q > 0 else []),
            inc=S["f_xbf"], inc_amt=16)
    s = S["f_cst"]
    E.do("sync", lambda e: e.dma_start(T["ident"][:, :], T["c_ident"].ap()),
         inc=s, inc_amt=16)
    E.do("sync", lambda e: e.dma_start(T["id8"][:, :], T["c_id8"].ap()),
         inc=s, inc_amt=16)
    E.do("sync", lambda e: e.dma_start(T["c_hilo_sb"][:, :], T["d_c_hilo"].ap()),
         inc=s, inc_amt=16)
    E.do("sync", lambda e: e.dma_start(T["c_spad_sb"][:, :], T["d_c_spad"].ap()),
         inc=s, inc_amt=16)
    E.do("sync", lambda e: e.dma_start(T["ones_bf"][:, :], T["c_ones_bf"].ap(
        ).rearrange("(o f) -> o f", o=1)), inc=s, inc_amt=16)
    E.do("sync", lambda e: e.dma_start(T["ones_f"][:, :], T["c_ones_f"].ap(
        ).rearrange("(o f) -> o f", o=1)), inc=s, inc_amt=16)
    E.do("sync", lambda e: e.dma_start(
        T["gamrow"][:, :], T["d_ln_gamma"].ap().rearrange("(o f) -> o f", o=1)),
        inc=s, inc_amt=16)
    E.do("sync", lambda e: e.dma_start(
        T["betrow"][:, :], T["d_ln_beta"].ap().rearrange("(o f) -> o f", o=1)),
        inc=s, inc_amt=16)
    E.do("sync", lambda e: e.dma_start(
        T["gam"][:, :], T["d_ln_gamma_t"].ap()), inc=s, inc_amt=16)
    E.do("sync", lambda e: e.dma_start(
        T["bet"][:, :], T["d_ln_beta_t"].ap()), inc=s, inc_amt=16)
    E.do("sync", lambda e: e.dma_start(
        T["ab_b"][:, :], T["d_abel_b_t"].ap()), inc=s, inc_amt=16)
    E.do("sync", lambda e: e.dma_start(
        T["eb1_sb"][:, :], T["d_eb1_t"].ap()), inc=s, inc_amt=16)
    E.do("sync", lambda e: e.dma_start(
        T["gb_sb"][:, :], T["d_gate_b"].ap().rearrange("(p o) -> p o", o=1)),
        inc=s, inc_amt=16)
    E.do("sync", lambda e: e.dma_start(
        T["gcb_sb"][:, :], T["d_gc_b"].ap().rearrange("(p o) -> p o", o=1)),
        inc=s, inc_amt=16)
    sg = S["f_cstg"]
    E.do("gpsimd", lambda e: e.dma_start(
        T["eb2_sb"][:, :], T["d_eb2"].ap()), inc=sg, inc_amt=16)
    E.do("gpsimd", lambda e: e.dma_start(
        T["gw_sb"][:, :], T["d_gate_w_t"].ap()), inc=sg, inc_amt=16)
    E.do("gpsimd", lambda e: e.dma_start(
        T["gc1_sb"][:, :], T["d_gc1_t"].ap()), inc=sg, inc_amt=16)
    E.do("gpsimd", lambda e: e.dma_start(
        T["gc2_sb"][:, :], T["d_gc2_t"].ap()), inc=sg, inc_amt=16)
    E.do("gpsimd", lambda e: e.dma_start(
        T["aw_sb"][:, :].rearrange("p (dc c) -> p dc c", dc=8),
        T["d_abel_w"].ap().rearrange("(dc p) c -> p dc c", p=128)),
        inc=S["f_abi"], inc_amt=16)
    _ew_load(E, S, T, 0)


def _ew_load(E, S, T, idx, wait=()):
    e, half = idx // 2, idx % 2
    buf = idx % 2
    sem = S["f_ew%d" % buf]
    E.do("gpsimd", lambda en: en.dma_start(
        T["ew1h"][buf][:, :].rearrange("p (dc c) -> p dc c", dc=8),
        T["d_ew1"].ap()[e, :, half * 1024:(half + 1) * 1024].rearrange(
            "(dc p) c -> p dc c", p=128)),
        wait=wait, inc=sem, inc_amt=16)
    E.do("gpsimd", lambda en: en.dma_start(
        T["ew2h"][buf][:, :].rearrange("p (hc c) -> p hc c", hc=8),
        T["d_ew2"].ap()[e, half * 1024:(half + 1) * 1024, :].rearrange(
            "(hc p) c -> p hc c", p=128)),
        inc=sem, inc_amt=16)


def emit_f1_ln2_xnt(E, S, T, P):
    E.do("vector", lambda e: e.memset(T["eps_t"][:, :], EPS), inc=S["f_ve"])
    for q in range(4):
        qs = slice(q * 4, (q + 1) * 4)
        for st in range(q * 4, q * 4 + 4):
            E.do("vector", lambda e, st=st: e.reduce_sum(
                T["fs_m16"][:, st:st + 1],
                T["x2bf"][:, st * 1024:(st + 1) * 1024], axis=AX.X),
                wait=([(S["f_xbf"], 16 * (q + 1))] if st % 4 == 0 else []))
            E.do("scalar", lambda e, st=st: e.activation(
                T["f_sqs" if st % 2 == 0 else "f_sqs2"][:, :],
                T["x2bf"][:, st * 1024:(st + 1) * 1024],
                AF.Square, accum_out=T["fs_q16"][:, st:st + 1]),
                wait=([(S["f_xbf"], 16 * (q + 1))] if st % 4 == 0 else []),
                inc=(S["f_sq"] if st % 4 == 3 else None))
            E.drain("scalar")
        E.drain("vector")
        E.do("vector", lambda e, qs=qs: e.tensor_scalar_mul(
            T["fs_m16"][:, qs], T["fs_m16"][:, qs], 1.0 / 1024))
        E.drain("vector")
        E.do("vector", lambda e, qs=qs: e.tensor_scalar_mul(
            T["fs_n16"][:, qs], T["fs_m16"][:, qs], -1.0))
        E.drain("vector")
        E.do("vector", lambda e, qs=qs: e.tensor_tensor(
            T["fs_v16"][:, qs], T["fs_n16"][:, qs], T["fs_m16"][:, qs],
            ALU.mult))
        E.drain("vector")
        E.do("vector", lambda e, qs=qs: e.scalar_tensor_tensor(
            T["fs_v16"][:, qs], T["fs_q16"][:, qs], 1.0 / 1024,
            T["fs_v16"][:, qs], ALU.mult, ALU.add),
            wait=[(S["f_sq"], q + 1)], inc=S["f_ve"])
        E.do("scalar", lambda e, qs=qs: e.activation(
            T["fs_d16"][:, qs], T["fs_v16"][:, qs], AF.Sqrt,
            bias=T["eps_t"][:, :]),
            wait=[(S["f_ve"], q + 2)], inc=S["f_sd"])
        E.do("vector", lambda e, qs=qs: e.reciprocal(
            T["fs_r16"][:, qs], T["fs_d16"][:, qs]),
            wait=[(S["f_sd"], q + 1)])
        E.drain("vector")
        for st in range(q * 4, q * 4 + 4):
            E.do("vector", lambda e, st=st: e.tensor_scalar(
                T["y2_sb"][:, st * 1024:(st + 1) * 1024],
                T["x2bf"][:, st * 1024:(st + 1) * 1024],
                T["fs_n16"][:, st:st + 1], T["fs_r16"][:, st:st + 1],
                ALU.add, ALU.mult), inc=S["f_v4"])
    for st in range(16):
        par = st % 2
        for dc in range(8):
            E.do("tensor", lambda e, dc=dc, st=st, par=par: e.transpose(
                P["PTf"][par][:, dc * 128:(dc + 1) * 128],
                T["y2_sb"][:, st * 1024 + dc * 128: st * 1024 + (dc + 1) * 128],
                T["ident"][:, :]),
                wait=([(S["f_v4"], st + 1), (S["f_xnt"], st - 1),
                       (S["f_cst"], 16 * N_CST)] if dc == 0 else []),
                inc=(S["f_tr"] if dc == 7 else None))
        for dc in range(8):
            E.do("scalar", lambda e, dc=dc, st=st, par=par: e.activation(
                T["xnT"][:, dc * 2048 + st * 128: dc * 2048 + st * 128 + 128],
                P["PTf"][par][:, dc * 128:(dc + 1) * 128], AF.Identity,
                bias=T["bet"][:, dc:dc + 1], scale=T["gam"][:, dc:dc + 1]),
                wait=([(S["f_tr"], st + 1)] if dc == 0 else []),
                inc=(S["f_xnt"] if dc == 7 else None))


def emit_f2_gating(E, S, T, P):
    PS1, PS2 = P["PS1"], P["PS2"]
    for c in range(4):
        for dc in range(8):
            fn = lambda e, c=c, dc=dc: e.matmul(
                PS1[0:8, :], T["gw_sb"][:, dc * 8:(dc + 1) * 8],
                T["xnT"][:, dc * 2048 + c * 512: dc * 2048 + (c + 1) * 512],
                start=(dc == 0), stop=(dc == 7))
            gwt = ([(S["f_xnt"], 16), (S["f_gv"], c),
                    (S["f_cstg"], 16 * N_CSTG)] if dc == 0 else [])
            if dc == 7:
                mm_di(E, fn, wait=gwt, inc=S["f_gt"])
            else:
                E.do("tensor", fn, wait=gwt)
        E.do("vector", lambda e, c=c: e.tensor_scalar(
            T["glog"][:, c * 512:(c + 1) * 512], PS1[0:8, :],
            T["gb_sb"][0:8, :], None, ALU.add),
            wait=[(S["f_gt"], c + 1)], inc=S["f_gv"])
    for st in range(16):
        E.do("tensor", lambda e, st=st: e.matmul(
            PS2[:, st * 8:(st + 1) * 8],
            T["glog"][0:8, st * 128:(st + 1) * 128],
            T["id8"][0:8, :], start=True, stop=True),
            wait=([(S["f_gv"], 4), (S["f_gbb"], 4)] if st == 0 else []))
    E.drain("tensor")
    E.do("tensor", lambda e: e.nop(), inc=S["f_gt"])
    for st in range(16):
        E.do("vector", lambda e, st=st: e.reduce_max(
            T["g_m1"][:, :], PS2[:, st * 8:(st + 1) * 8], axis=AX.X),
            wait=([(S["f_gt"], 5)] if st == 0 else [(S["f_gate"], st)]))
        E.drain("vector")
        E.do("vector", lambda e, st=st: e.tensor_single_scalar(
            T["g_msk"][:, :], PS2[:, st * 8:(st + 1) * 8], T["g_m1"][:, :],
            ALU.is_equal))
        E.drain("vector")
        E.do("vector", lambda e, st=st: e.scalar_tensor_tensor(
            T["g_lm"][:, :], T["g_msk"][:, :], -1e30,
            PS2[:, st * 8:(st + 1) * 8], ALU.mult, ALU.add))
        E.drain("vector")
        E.do("vector", lambda e: e.reduce_max(
            T["g_m2"][:, :], T["g_lm"][:, :], axis=AX.X))
        E.drain("vector")
        E.do("vector", lambda e: e.scalar_tensor_tensor(
            T["g_nm1"][:, :], T["g_m1"][:, :], -1.0, T["g_m2"][:, :],
            ALU.mult, ALU.subtract), inc=S["f_wn"])
        E.do("vector", lambda e, st=st: e.tensor_single_scalar(
            T["g_mge"][:, :], PS2[:, st * 8:(st + 1) * 8], T["g_m2"][:, :],
            ALU.is_ge))
        E.drain("vector")
        E.do("scalar", lambda e, st=st: e.activation(
            T["g_e8"][:, :], PS2[:, st * 8:(st + 1) * 8], AF.Sigmoid,
            bias=T["g_nm1"][:, :], scale=2.0),
            wait=[(S["f_wn"], 2 * st + 1)], inc=S["f_gate"])
        E.do("vector", lambda e, st=st: e.tensor_tensor(
            T["wN"][:, st * 8:(st + 1) * 8], T["g_e8"][:, :], T["g_mge"][:, :],
            ALU.mult),
            wait=[(S["f_gate"], st + 1)])
        E.drain("vector")
        E.do("vector", lambda e, st=st: e.tensor_copy(
            T["wNb"][:, st * 8:(st + 1) * 8], T["wN"][:, st * 8:(st + 1) * 8]),
            inc=S["f_wn"])
    E.do("vector", lambda e: e.memset(T["wT"][:, :], 0.0),
         wait=[(S["f_sq"], 4)])
    E.drain("vector")
    for st in range(16):
        waits = [(S["f_wn"], 2 * st + 2), (S["f_e2v"], st)]
        if st == 0:
            waits += [(S["f_wn"], 32), (S["f_gate"], 16)]
        mm_di(E, lambda e, st=st: e.matmul(
            PS2[0:8, 0:128], T["wNb"][:, st * 8:(st + 1) * 8], T["ident"][:, :],
            start=True, stop=True),
            wait=waits, inc=S["f_e2t"])
        E.do("vector", lambda e, st=st: e.tensor_copy(
            T["wT"][0:8, st * 128:(st + 1) * 128], PS2[0:8, 0:128]),
            wait=[(S["f_e2t"], st + 1)], inc=S["f_e2v"])


def emit_f3_abel(E, S, T, P):
    dcp_order = [4, 5, 6, 7, 0, 1, 2, 3]
    groups = [(dcp, c) for dcp in dcp_order for c in range(4)]
    for gidx, (dcp, c) in enumerate(groups):
        par = gidx % 2
        if gidx == 16:
            emit_slot_transposes(E, S, T, P)
        for dc in range(8):
            waits = []
            if dc == 0:
                waits = [(S["f_as"], gidx - 1)]
                if gidx == 0:
                    waits += [(S["f_abi"], 16), (S["f_xnt"], 16)]
        
            E.do("tensor", lambda e, dcp=dcp, c=c, dc=dc, par=par: e.matmul(
                P["PEO"][par][:, :],
                T["aw_sb"][:, dc * 1024 + dcp * 128: dc * 1024 + (dcp + 1) * 128],
                T["xnT"][:, dc * 2048 + c * 512: dc * 2048 + (c + 1) * 512],
                start=(dc == 0), stop=(dc == 7)),
                wait=waits, inc=(S["f_at"] if dc == 7 else None))
        gw = [(S["f_at"], gidx + 1)]
        if gidx == 8:
            gw.append((S["f_xnv"], 18))
        if gidx == 16:
            gw.append((S["f_tiv"], 5))
        E.do("scalar", lambda e, dcp=dcp, c=c, par=par, gidx=gidx: e.activation(
            T["abelT"][:, dcp * 2048 + c * 512: dcp * 2048 + (c + 1) * 512],
            P["PEO"][par][:, :], AF.Gelu, bias=T["ab_b"][:, dcp:dcp + 1]),
            wait=gw, inc=S["f_as"])
    for c in range(4):
        for dcp in range(8):
            waits = []
            if dcp == 0:
                waits = [(S["f_glav"], c)]
                if c == 0:
                    waits += [(S["f_as"], 32), (S["f_cstg"], 16 * N_CSTG)]
            fn = lambda e, c=c, dcp=dcp: e.matmul(
                P["PS1"][0:1, :], T["gc1_sb"][:, dcp:dcp + 1],
                T["abelT"][:, dcp * 2048 + c * 512: dcp * 2048 + (c + 1) * 512],
                start=(dcp == 0), stop=(dcp == 7))
            if dcp == 7:
                mm_di(E, fn, wait=waits, inc=S["f_glat"])
            else:
                E.do("tensor", fn, wait=waits)
        E.do("vector", lambda e, c=c: e.tensor_scalar(
            T["gla"][0:1, c * 512:(c + 1) * 512], P["PS1"][0:1, :],
            T["gcb_sb"][0:1, :], None, ALU.add),
            wait=[(S["f_glat"], c + 1)], inc=S["f_glav"])
    E.do("sync", lambda e: e.dma_start(T["d_abel"].ap(), T["abelT"][:, :]),
         wait=[(S["f_glav"], 4)], inc=S["f_abo"], inc_amt=16)


def emit_f4_moe_sparse(E, S, T, P):
    from concourse.bass import IndirectOffsetOnAxis
    gel_count, hmm_count, eomm_count = {}, {}, {}
    n_gel = n_hmm = n_eomm = 0
    for idx in range(16):
        for hc in range(8):
            for sc in range(2):
                n_hmm += 1
                hmm_count[(idx, hc, sc)] = n_hmm
                n_gel += 1
                gel_count[(idx, hc, sc)] = n_gel
        for k in range(NCH):
            for dq in range(2):
                n_eomm += 1
                eomm_count[(idx, k, dq)] = n_eomm
    h_groups = [(idx, hc, sc) for idx in range(16) for hc in range(8)
                for sc in range(2)]
    h_gnum = {g: n for n, g in enumerate(h_groups)}
    eo_groups = [(idx, k, dq) for idx in range(16) for k in range(NCH)
                 for dq in range(2)]
    eo_gnum = {g: n for n, g in enumerate(eo_groups)}

    E.do("gpsimd", lambda e: e.dma_start(T["siota"][:, :], T["d_c_siota"].ap()),
         wait=[(S["f_at"], 32)], inc=S["f_abi"], inc_amt=16)
    _inv_compares(E, S, T, 0)
    _inv_mms(E, S, T, P, 0)
    _inv_finish(E, S, T, P, 0)
    _gathers(E, S, T, 0)

    for e_ in range(8):
        if e_ + 1 <= 7:
            _inv_compares(E, S, T, e_ + 1)
            _inv_mms(E, S, T, P, e_ + 1)
            _inv_finish(E, S, T, P, e_ + 1)
            _gathers(E, S, T, e_ + 1)
        _xposes(E, S, T, P, e_)
        for half in range(2):
            idx = e_ * 2 + half
            buf = idx % 2
            if idx + 1 < 16:
                w = []
                if idx >= 1:
                    w = [(S["f_eomm"], eomm_count[(idx - 1, NCH - 1, 1)])]
                _ew_load(E, S, T, idx + 1, wait=w)
            for hc in range(8):
                for sc in range(2):
                    g = (idx, hc, sc)
                    gn = h_gnum[g]
                    nw = 512 if sc == 0 else 256
                    for dc in range(8):
                        waits = []
                        if dc == 0:
                            pg = h_groups[gn - 2] if gn >= 2 else None
                            if pg:
                                waits.append((S["f_gel"], gel_count[pg]))
                            if hc == 0 and sc == 0:
                                waits.append((S["f_ew%d" % buf],
                                              32 * (idx // 2 + 1)))
                                waits.append((S["f_xev"], NCH * (e_ + 1)))
                        E.do("tensor", lambda e, buf=buf, hc=hc, sc=sc, dc=dc,
                             nw=nw, par=gn % 2: e.matmul(
                            P["PH"][par][:, 0:nw],
                            T["ew1h"][buf][:, dc * 1024 + hc * 128:
                                           dc * 1024 + (hc + 1) * 128],
                            T["xncT"][:, dc * CAP + sc * 512:
                                      dc * CAP + sc * 512 + nw],
                            start=(dc == 0), stop=(dc == 7)),
                            wait=waits, inc=(S["f_hmm"] if dc == 7 else None))
                    gw = [(S["f_hmm"], hmm_count[g])]
                    if idx == 0 and hc == 0 and sc == 0:
                        gw.append((S["f_abo"], 16))
                    pgn = gn - 16
                    if pgn >= 0:
                        pidx = h_groups[pgn][0]
                        gw.append((S["f_eomm"], eomm_count[(pidx, NCH - 1, 1)]))
                    E.do("scalar", lambda e, hc=hc, sc=sc, idx=idx, nw=nw,
                         par=gn % 2: e.activation(
                        T["hcT"][:, hc * CAP + sc * 512: hc * CAP + sc * 512 + nw],
                        P["PH"][par][:, 0:nw], AF.Gelu,
                        bias=T["eb1_sb"][:, idx * 8 + hc: idx * 8 + hc + 1]),
                        wait=gw, inc=S["f_gel"])
            for k in range(NCH):
                for dq in range(2):
                    g = (idx, k, dq)
                    gn = eo_gnum[g]
                    for hc in range(8):
                        waits = [(S["f_gel"],
                                  gel_count[(idx, hc, 0 if k < 4 else 1)])]
                        if hc == 0:
                            pg = eo_groups[gn - 2] if gn >= 2 else None
                            if pg:
                                waits.append((S["f_eov"],
                                              eomm_count[pg]))
                        E.do("tensor", lambda e, buf=buf, k=k, dq=dq, hc=hc,
                             par=gn % 2: e.matmul(
                            P["PEO"][par][:, :],
                            T["hcT"][:, hc * CAP + k * 128:
                                     hc * CAP + k * 128 + 128],
                            T["ew2h"][buf][:, hc * 1024 + dq * 512:
                                           hc * 1024 + dq * 512 + 512],
                            start=(hc == 0), stop=(hc == 7)),
                            wait=waits, inc=(S["f_eomm"] if hc == 7 else None))
                    ew = [(S["f_eomm"], eomm_count[g]),
                          (S["f_sct"], 16 * NCH * idx),
                          (S["f_wgc"], NCH * e_ + k + 1)]
                    E.do("vector", lambda e, e_=e_, k=k, dq=dq, par=gn % 2:
                         e.tensor_scalar(
                        T["eoc"][:, k * 1024 + dq * 512: k * 1024 + dq * 512 + 512],
                        P["PEO"][par][:, :],
                        T["wgc"][:, k * 8 + e_: k * 8 + e_ + 1],
                        None, ALU.mult),
                        wait=ew, inc=S["f_eov"])
            for k in range(NCH):
                waits = [(S["f_eov"], eomm_count[(idx, k, 1)]),
                         (S["f_sct"], 16 * (NCH * idx + k))]
                E.do("gpsimd", lambda e, e_=e_, k=k: e.indirect_dma_start(
                    T["d_moe"].ap(),
                    IndirectOffsetOnAxis(ap=T["idxU"][:, k * 8 + e_: k * 8 + e_ + 1],
                                         axis=0),
                    T["eoc"][:, k * 1024:(k + 1) * 1024], None,
                    compute_op=ALU.add),
                    wait=waits, inc=S["f_sct"], inc_amt=16)
    for q in range(4):
        w = [(S["f_mld"], 16 * q)]
        if q == 0:
            w = [(S["f_sct"], 16 * NCH * 16)]
        E.do("sync", lambda e, q=q: e.dma_start(
            T["macc"][:, q * 4096:(q + 1) * 4096].rearrange(
                "p (st c) -> p st c", st=4),
            T["d_moe"].ap()[q * 512:(q + 1) * 512, :].rearrange(
                "(st p) c -> p st c", p=128)),
            wait=w, inc=S["f_mld"], inc_amt=16)
    e2t_base = E.cnt(S["f_e2t"])
    for stt in range(16):
        for dq in range(2):
            gn = stt * 2 + dq
            waits = [(S["f_e2v"], 16 + gn - 1)]
            if gn == 0:
                waits += [(S["f_eov"], E.cnt(S["f_eov"])), (S["f_e2v"], 16),
                          (S["f_cstg"], 16 * N_CSTG)]
            mm_di(E, lambda e, stt=stt, dq=dq, par=gn % 2: e.matmul(
                P["PEO"][par][:, :], T["wT"][0:8, stt * 128:(stt + 1) * 128],
                T["eb2_sb"][0:8, dq * 512:(dq + 1) * 512],
                start=True, stop=True),
                wait=waits, inc=S["f_e2t"])
            E.do("vector", lambda e, stt=stt, dq=dq, par=gn % 2: e.tensor_tensor(
                T["macc"][:, stt * 1024 + dq * 512: stt * 1024 + dq * 512 + 512],
                T["macc"][:, stt * 1024 + dq * 512: stt * 1024 + dq * 512 + 512],
                P["PEO"][par][:, :], ALU.add),
                wait=[(S["f_e2t"], e2t_base + gn + 1),
                      (S["f_mld"], 16 * (stt // 4 + 1))],
                inc=S["f_e2v"])


def emit_f5_rot_mix_out(E, S, T, P):
    hmm_total = E.cnt(S["f_hmm"])
    eomm_total = E.cnt(S["f_eomm"])
    e2v_total = E.cnt(S["f_e2v"])

    def rload(r, dstslice, sem, wait):
        E.do("gpsimd", lambda e: e.dma_start(
            dstslice, T["d_Rall"].ap()[r]),
            wait=wait, inc=sem, inc_amt=16)

    R0s = T["xnT"][:, 0:8192]
    R1s = T["xnT"][:, 8192:16384]
    ccw = (T["sem_cc"], 1)
    wgc_total = E.cnt(S["f_wgc"])
    rload(0, R0s, S["f_r0"], [(S["f_hmm"], hmm_total), (S["f_wgc"], wgc_total),
                              ccw])
    rload(1, R1s, S["f_r1"], [(S["f_hmm"], hmm_total), (S["f_wgc"], wgc_total),
                              ccw])
    for st in range(16):
        par = st % 2
        for dc in range(8):
            waits = []
            if dc == 0:
                waits = [(S["f_trfv"], st - 1),
                         (S["f_e2v"], 16 + 2 * (st + 1))]
            E.do("tensor", lambda e, st=st, dc=dc, par=par: e.transpose(
                P["PTf"][par][:, dc * 128:(dc + 1) * 128],
                T["macc"][:, st * 1024 + dc * 128: st * 1024 + (dc + 1) * 128],
                T["ident"][:, :]),
                wait=waits, inc=(S["f_trf"] if dc == 7 else None))
        E.do("vector", lambda e, st=st, par=par: e.tensor_copy(
            T["moeT"][:, :].rearrange("p (dc s) -> p dc s", dc=8)[
                :, :, st * 128:(st + 1) * 128],
            P["PTf"][par][:, :].rearrange("p (dc c) -> p dc c", dc=8)),
            wait=[(S["f_trf"], st + 1)], inc=S["f_trfv"])
    rsrc = [T["moeT"], T["rbufA"], T["rbufB"]]
    rdst = [T["rbufA"], T["rbufB"], T["rbufA"]]
    rsem = [S["f_r0"], S["f_r1"], S["f_r2"]]
    gidx = 0
    for r in range(3):
        if r == 1:
            rload(2, R0s, S["f_r2"], [(S["f_rot"], 32)])
        rR = R0s if r in (0, 2) else R1s
        for dcO in range(8):
            for c in range(4):
                par = gidx % 2
                for dc in range(8):
                    waits = []
                    if dc == 0:
                        waits = [(S["f_rotv"], gidx - 1), (rsem[r], 16)]
                        if r == 0 and dcO == 0 and c == 0:
                            waits.append((S["f_trfv"], 16))
                    E.do("tensor", lambda e, rR=rR, r=r, dcO=dcO, c=c, dc=dc,
                         par=par: e.matmul(
                        P["PEO"][par][:, :],
                        rR[:, dc * 1024 + dcO * 128: dc * 1024 + (dcO + 1) * 128],
                        rsrc[r][:, dc * 2048 + c * 512: dc * 2048 + (c + 1) * 512],
                        start=(dc == 0), stop=(dc == 7)),
                        wait=waits, inc=(S["f_rot"] if dc == 7 else None))
                E.do("vector", lambda e, r=r, dcO=dcO, c=c, par=par, gidx=gidx:
                     e.tensor_copy(
                    rdst[r][:, dcO * 2048 + c * 512: dcO * 2048 + (c + 1) * 512],
                    P["PEO"][par][:, :]),
                    wait=[(S["f_rot"], gidx + 1)], inc=S["f_rotv"])
                gidx += 1
    E.do("sync", lambda e: e.dma_start(
        T["abelT2"][:, :], T["d_abel"].ap()),
        wait=[(S["f_eomm"], eomm_total)], inc=S["f_ab2"], inc_amt=16)
    for c in range(4):
        for dcp in range(8):
            waits = []
            if dcp == 0:
                waits = [(S["f_glmv"], c)]
                if c == 0:
                    waits += [(S["f_rotv"], 96), (S["f_cstg"], 16 * N_CSTG)]
            fn = lambda e, c=c, dcp=dcp: e.matmul(
                P["PS1"][0:1, :], T["gc2_sb"][:, dcp:dcp + 1],
                T["rbufA"][:, dcp * 2048 + c * 512: dcp * 2048 + (c + 1) * 512],
                start=(dcp == 0), stop=(dcp == 7))
            if dcp == 7:
                mm_di(E, fn, wait=waits, inc=S["f_glmt"])
            else:
                E.do("tensor", fn, wait=waits)
        E.do("vector", lambda e, c=c: e.tensor_copy(
            T["glm"][0:1, c * 512:(c + 1) * 512], P["PS1"][0:1, :]),
            wait=[(S["f_glmt"], c + 1)], inc=S["f_glmv"])
    E.do("vector", lambda e: e.tensor_tensor(
        T["glm"][0:1, :], T["glm"][0:1, :], T["gla"][0:1, :], ALU.add),
        wait=[(S["f_glmv"], 4), (S["f_glav"], 4)], inc=S["f_gbv"])
    E.do("scalar", lambda e: e.activation(
        T["g_row"][0:1, :], T["glm"][0:1, :], AF.Sigmoid),
        wait=[(S["f_gbv"], 1)], inc=S["f_gate"])
    for c in range(4):
        waits = [(S["f_gate"], 17), (S["f_gbv"], c + 1)]
        if c == 0:
            waits.append((S["f_e2v"], e2v_total))
        mm_di(E, lambda e, c=c: e.matmul(
            P["PS2"][:, :], T["ones_bf"][0:1, :],
            T["g_row"][0:1, c * 512:(c + 1) * 512], start=True, stop=True),
            wait=waits, inc=S["f_gbt"])
        E.do("vector", lambda e, c=c: e.tensor_copy(
            T["gbc"][:, c * 512:(c + 1) * 512], P["PS2"][:, :]),
            wait=[(S["f_gbt"], c + 1)], inc=S["f_gbv"])
    mtb = [T["mixtmp"], T["mixtmp2"]]
    for dc0 in range(0, 8, 2):
        for j in range(2):
            dc = dc0 + j
            E.do("vector", lambda e, dc=dc, j=j: e.tensor_tensor(
                mtb[j][:, :], T["rbufA"][:, dc * 2048:(dc + 1) * 2048],
                T["abelT2"][:, dc * 2048:(dc + 1) * 2048], ALU.subtract),
                wait=([(S["f_gbv"], 5), (S["f_ab2"], 16)] if dc == 0 else []))
        E.drain("vector")
        for j in range(2):
            E.do("vector", lambda e, j=j: e.tensor_tensor(
                mtb[j][:, :], mtb[j][:, :], T["gbc"][:, :], ALU.mult))
        E.drain("vector")
        for j in range(2):
            dc = dc0 + j
            E.do("vector", lambda e, dc=dc, j=j: e.tensor_tensor(
                T["rbufA"][:, dc * 2048:(dc + 1) * 2048],
                T["abelT2"][:, dc * 2048:(dc + 1) * 2048], mtb[j][:, :],
                ALU.add), inc=S["f_mix"])
        E.drain("vector")
    x_t = T["x2s"]
    for st in range(8):
        xw = [(S["f_xin0"], 16 * st)]
        if st == 0:
            xw = [(S["f_rot"], 96), (S["f_at"], 32)]
        E.do("sync", lambda e, st=st: e.dma_start(
            x_t[st][:, :], T["d_x2"].ap()[st * 128:(st + 1) * 128, :]),
            wait=xw, inc=S["f_xin0"], inc_amt=16)
    for st in range(16):
        par = st % 2
        for dc in range(8):
            waits = []
            if dc == 0:
                waits = [(S["f_ov"], st - 1)]
                if st < 2:
                    waits.append((S["f_mix"], 8))
            E.do("tensor", lambda e, st=st, dc=dc, par=par: e.transpose(
                P["PTf"][par][:, dc * 128:(dc + 1) * 128],
                T["rbufA"][:, dc * 2048 + st * 128: dc * 2048 + st * 128 + 128],
                T["ident"][:, :]),
                wait=waits, inc=(S["f_ot"] if dc == 7 else None))
        E.do("vector", lambda e, st=st, par=par: e.tensor_tensor(
            T["outb"][par][:, :], x_t[st % 8][:, :],
            T["y2_sb"][:, st * 1024:(st + 1) * 1024], ALU.add),
            wait=[(S["f_ot"], st + 1), (S["f_xin0"], 16 * (st + 1)),
                  (S["f_out%d" % par], 16 * (st // 2))])
        E.drain("vector")
        E.do("vector", lambda e, st=st, par=par: e.tensor_tensor(
            T["outb"][par][:, :], T["outb"][par][:, :], P["PTf"][par][:, :],
            ALU.add), inc=S["f_ov"])
        E.do("sync", lambda e, st=st, par=par: e.dma_start(
            T["d_out"].ap()[st * 128:(st + 1) * 128, :], T["outb"][par][:, :]),
            wait=[(S["f_ov"], st + 1)], inc=S["f_out%d" % par], inc_amt=16)
        if st + 8 < 16:
            E.do("sync", lambda e, st=st: e.dma_start(
                x_t[st % 8][:, :],
                T["d_x2"].ap()[(st + 8) * 128:(st + 9) * 128, :]),
                wait=[(S["f_xin0"], 16 * (st + 8))], inc=S["f_xin0"],
                inc_amt=16)
    E.wait("sync", S["f_out0"], 16 * 8)
    E.wait("sync", S["f_out1"], 16 * 8)


def make_ffn_tensors(nc, es, T):
    def sb(name, shape, dt):
        return es.enter_context(nc.sbuf_tensor(name, shape, dt))

    def at(name, shape, dt, base, off=0):
        return nc.alloc_sbuf_tensor_at(
            name, shape, dt, offset=nc.lookup_mloc(base).addr + off)
    T["y2_sb"] = sb("y2_sb", [128, 16 * 1024], BF16)
    T["xnT"] = sb("xnT", [128, 8 * 2048], BF16)
    T["macc"] = sb("macc", [128, 16 * 1024], BF16)
    T["ew1h"] = [sb("ew1h%d" % k, [128, 8 * 1024], BF16) for k in range(2)]
    T["ew2h"] = [sb("ew2h%d" % k, [128, 8 * 1024], BF16) for k in range(2)]
    T["x2_tile"] = [sb("x2t%d" % k, [128, 1024], F32) for k in range(2)]
    T["outb"] = [sb("outb%d" % k, [128, 1024], F32) for k in range(2)]
    T["gbc"] = sb("gbc", [128, 2048], BF16)
    T["gamma_bc"] = sb("gamma_bc", [128, 1024], BF16)
    T["beta_bc"] = sb("beta_bc", [128, 1024], BF16)
    T["idxU"] = sb("idxU", [128, NCH * 8], mybir.dt.uint32)
    T["abelT"] = at("abelT", [128, 8 * 2048], BF16, T["macc"])
    T["moeT"] = at("moeT", [128, 8 * 2048], BF16, T["ew1h"][0])
    T["rbufB"] = T["moeT"]
    T["rbufA"] = at("rbufA", [128, 8 * 2048], BF16, T["macc"])
    T["abelT2"] = at("abelT2", [128, 8 * 2048], BF16, T["ew2h"][0])
    T["aw_sb"] = at("aw_sb", [128, 8 * 1024], BF16, T["x2_tile"][0])
    T["f_sqs"] = at("f_sqs", [128, 1024], F32, T["gbc"])
    T["f_sqs2"] = at("f_sqs2", [128, 1024], F32, T["ew1h"][1])
    T["x2bf"] = at("x2bf", [128, 16 * 1024], BF16, T["macc"])
    T["hcT"] = at("hcT", [128, 8 * 1024], BF16, T["macc"])
    T["eoc"] = at("eoc", [128, 8 * 1024], BF16, T["macc"], off=16384)
    T["glog"] = at("glog", [8, 2048], F32, T["macc"])
    T["ib2b"] = at("ib2b", [8, 2048], F32, T["macc"])
    T["ib1"] = at("ib1", [8, 2048], F32, T["macc"], off=8192)
    T["ib3"] = at("ib3", [8, 2048], F32, T["macc"], off=24576)
    T["gla"] = at("gla", [1, 2048], BF16, T["gamma_bc"])
    for n in ("fs_m16", "fs_n16", "fs_q16", "fs_v16", "fs_d16", "fs_r16"):
        T[n] = sb(n, [128, 16], F32)
    T["eps_t"] = sb("f_eps_t", [128, 1], F32)
    T["gam"] = sb("gam", [128, 8], F32)
    T["bet"] = sb("bet", [128, 8], F32)
    T["ab_b"] = sb("ab_b", [128, 8], F32)
    T["eb1_sb"] = sb("eb1_sb", [128, 128], F32)
    T["eb2_sb"] = sb("eb2_sb", [8, 1024], BF16)
    T["gw_sb"] = sb("gw_sb", [128, 64], BF16)
    T["gb_sb"] = sb("gb_sb", [8, 1], F32)
    T["gc1_sb"] = sb("gc1_sb", [128, 8], BF16)
    T["gc2_sb"] = sb("gc2_sb", [128, 8], BF16)
    T["gcb_sb"] = sb("gcb_sb", [1, 1], F32)
    T["ident"] = sb("f_ident", [128, 128], BF16)
    T["id8"] = sb("id8", [8, 8], F32)
    T["ones_bf"] = sb("ones_bf", [1, 128], BF16)
    T["ones_f"] = sb("ones_f", [1, 128], F32)
    T["wN"] = sb("wN", [128, 16 * 8], F32)
    T["wNb"] = sb("wNb", [128, 16 * 8], BF16)
    for n in ("g_m1", "g_nm1", "g_m2", "g_d0", "g_e2", "g_dn", "g_rdn"):
        T[n] = sb(n, [128, 1], F32)
    for n in ("g_msk", "g_lm", "g_e8", "g_mge", "g_t8"):
        T[n] = sb(n, [128, 8], F32)
    T["gath"] = at("gath", [128, NCH * 1032], BF16, T["xnT"])
    T["xncT"] = at("xncT", [128, 8 * CAP], BF16, T["xnT"], off=12416)
    T["MT"] = [at("mtb%d" % b, [128, CAP], BF16, T["xnT"], off=24704 + 1536 * b)
               for b in range(2)]
    T["xnn"] = [sb("xnn%d" % k, [128, 1024], BF16) for k in range(2)]
    T["zt"] = sb("zt", [128, 1032], BF16)
    T["mixtmp"] = at("mixtmp", [128, 2048], BF16, T["x2_tile"][0])
    T["mixtmp2"] = at("mixtmp2", [128, 2048], BF16, T["x2_tile"][1])
    T["glm"] = at("glm", [1, 2048], BF16, T["outb"][0])
    T["g_row"] = at("g_row", [1, 2048], BF16, T["outb"][1])
    T["gamrow"] = sb("gamrow", [1, 1024], F32)
    T["betrow"] = sb("betrow", [1, 1024], F32)
    T["wT"] = at("wT", [16, 2048], BF16, T["gbc"])
    T["slotsT"] = sb("slotsT", [128, 128], F32)
    T["vrow"] = at("vrow", [2, CAP], F32, T["x2_tile"][1], off=512)
    T["siota"] = at("siota", [128, CAP], F32, T["outb"][1])
    T["pse"] = sb("pse", [128, 2], F32)
    T["iocc"] = sb("iocc", [128, 1], F32)
    T["itmp"] = sb("itmp", [128, 1], F32)
    T["wgc"] = sb("wgc", [128, NCH * 8], F32)
    T["c_hilo_sb"] = sb("c_hilo_sb", [128, 32], BF16)
    T["x2s"] = [at("x2s%d" % k, [128, 1024], F32, T["ew1h"][0], off=4096 * k)
                for k in range(8)]
    T["c_spad_sb"] = sb("c_spad_sb", [128, 2], F32)
    return T


def make_ffn_psum(nc, es):
    P = {}
    P["PTf"] = [es.enter_context(nc.psum_tensor("PTf%d" % k, [128, 1024], BF16))
                for k in range(2)]
    P["PEO"] = [es.enter_context(nc.psum_tensor("PEOf%d" % k, [128, 512], F32))
                for k in range(2)]
    P["PH"] = [es.enter_context(nc.psum_tensor("PHf%d" % k, [128, 512], F32))
               for k in range(2)]
    P["PS1"] = es.enter_context(nc.psum_tensor("PS1f", [128, 512], F32))
    P["PS2"] = es.enter_context(nc.psum_tensor("PS2f", [128, 512], F32))
    return P



FACT = [1.0]
for _k in range(1, 13):
    FACT.append(FACT[-1] * _k)
C = [1.0 / f for f in FACT]

EXPM_SEM_NAMES = ["b_ld", "b_mt", "b_ev", "b_bl", "b_ro", "b_cc"]


def expm_sems(nc, es):
    return {n: Sem(nc, es, n) for n in EXPM_SEM_NAMES}


def emit_expm(E, S, T, P):
    E.do("sync", lambda e: e.dma_start(T["Irow"][:, :], T["c_irow"].ap()),
         inc=S["b_ro"], inc_amt=16)
    _emit_gen(E, S, T, P, 0)
    E.wait("sync", S["b_ro"], 16 + 16)


def emit_expm_allgather(E, S, T):
    E.do("gpsimd", lambda e: e.collective_compute(
        "AllGather", ALU.bypass, [list(range(8))],
        [T["d_Rpart"].ap().opt()], [T["d_Rall"].ap().opt()]),
        wait=[(S["b_ro"], 32)], inc=S["b_cc"])


def _build(E, S, T, dst, terms, gate_waits=(), inc=True, accumulate=False):
    E.drain("vector")
    first = not accumulate
    for k, (coef, mat) in enumerate(terms):
        w = list(gate_waits) if k == 0 else []
        src = T["Irow"] if mat is None else T[mat]
        if first:
            E.do("vector", lambda e, d=dst, s=src, c=coef: e.tensor_scalar_mul(
                d[:, :], s[:, :], c), wait=w)
        else:
            E.do("vector", lambda e, d=dst, s=src, c=coef: e.scalar_tensor_tensor(
                d[:, :], s[:, :], c, d[:, :], ALU.mult, ALU.add))
        E.drain("vector")
        first = False
    if inc:
        E.do("vector", lambda e: e.engine_nop(), inc=S["b_bl"])


def _product(E, S, T, P, g, pi, lhs_name, rhs_name, dst_name, add_name=None,
             lhs_waits=(), rhs_ev=None, out_dma=None):
    lhs, rhs, dst = T[lhs_name], T[rhs_name], T[dst_name]
    base_g = g * 80
    for rm in range(8):
        for q in range(2):
            gi = pi * 16 + rm * 2 + q
            par = gi % 2
            for dc in range(8):
                waits = []
                if dc == 0:
                    waits = [(S["b_ev"], base_g + gi - 1)]
                    if rm == 0 and q == 0:
                        waits += list(lhs_waits)
                        if rhs_ev is not None:
                            waits.append((S["b_ev"], rhs_ev))
                elif pi == 0 and rm == 0 and q == 0:
                    waits = [(S["b_bl"], 2 * (dc + 1))]
                E.do("tensor", lambda e, rm=rm, q=q, dc=dc, par=par: e.matmul(
                    P["PB"][par][:, :],
                    lhs[:, dc * 1024 + rm * 128: dc * 1024 + (rm + 1) * 128],
                    rhs[:, dc * 1024 + q * 512: dc * 1024 + q * 512 + 512],
                    start=(dc == 0), stop=(dc == 7)),
                    wait=waits, inc=(S["b_mt"] if dc == 7 else None))
            dsl = dst[:, rm * 1024 + q * 512: rm * 1024 + q * 512 + 512]
            evw = [(S["b_mt"], base_g + gi + 1)]
            if pi == 0 and rm == 0 and q == 0:
                evw.append((S["b_ro"], 16 + 16 * g))
            if add_name is None:
                E.do("vector", lambda e, d=dsl, par=par: e.tensor_copy(
                    d, P["PB"][par][:, :]), wait=evw, inc=S["b_ev"])
            else:
                asl = T[add_name][:, rm * 1024 + q * 512: rm * 1024 + q * 512 + 512]
                E.do("vector", lambda e, d=dsl, a=asl, par=par: e.tensor_tensor(
                    d, P["PB"][par][:, :], a, ALU.add), wait=evw, inc=S["b_ev"])
    if out_dma is not None:
        E.do("sync", lambda e: e.dma_start(T[out_dma].ap(), dst[:, :]),
             wait=[(S["b_ev"], base_g + (pi + 1) * 16)], inc=S["b_ro"], inc_amt=16)


def _emit_gen(E, S, T, P, g):
    lw = []
    if g > 0:
        lw = [(S["b_mt"], g * 80), (S["b_bl"], g * 3)]
    for dc in range(8):
        w = list(lw) if dc == 0 else [(S["b_ld"], 16 * dc)]
        E.do("sync", lambda e, dc=dc: e.dma_start(
            T["Gf32"][:, dc * 1024:(dc + 1) * 1024],
            T["d_mygen"].ap()[dc * 128:(dc + 1) * 128, :]),
            wait=w, inc=S["b_ld"], inc_amt=16)
        E.do("vector", lambda e, dc=dc: e.tensor_copy(
            T["Gb"][:, dc * 1024:(dc + 1) * 1024],
            T["Gf32"][:, dc * 1024:(dc + 1) * 1024]),
            wait=[(S["b_ld"], 16 * (dc + 1))], inc=S["b_bl"])
        E.do("vector", lambda e, dc=dc: e.tensor_scalar_mul(
            T["GTb"][:, dc * 1024:(dc + 1) * 1024],
            T["Gf32"][:, dc * 1024:(dc + 1) * 1024], -1.0),
            inc=S["b_bl"])
    ldw = [(S["b_ld"], 16 * 8)]
    base_ev = g * 80
    _product(E, S, T, P, g, 0, "GTb", "Gb", "G2b", lhs_waits=[(S["b_bl"], 2)])
    _product(E, S, T, P, g, 1, "GTb", "G2b", "G3b", rhs_ev=base_ev + 16)
    bw = [(S["b_ro"], 16), (S["b_bl"], 16)]
    if g > 0:
        bw.append((S["b_mt"], g * 80 - 16))
    _build(E, S, T, T["bufA"],
           [(C[8], None), (-C[9], "Gb"), (C[10], "G2b"), (-C[11], "G3b")],
           gate_waits=bw, inc=False)
    _build(E, S, T, T["bufB"],
           [(C[4], None), (-C[5], "Gb"), (C[6], "G2b"), (-C[7], "G3b")])
    _product(E, S, T, P, g, 2, "G2b", "G2b", "G4b", rhs_ev=base_ev + 16)
    _build(E, S, T, T["bufA"], [(C[12], "G4b")], accumulate=True)
    _product(E, S, T, P, g, 3, "G4b", "bufA", "bufD", add_name="bufB",
             lhs_waits=[(S["b_bl"], g * 3 + 2 + 16)], rhs_ev=base_ev + 48)
    _build(E, S, T, T["bufB"],
           [(C[0], None), (C[1], "Gb"), (C[2], "G2b"), (C[3], "G3b")])
    _product(E, S, T, P, g, 4, "bufD", "G4b", "Rout", add_name="bufB",
             lhs_waits=[(S["b_bl"], g * 3 + 3 + 16), (S["b_ev"], base_ev + 64)],
             out_dma="d_Rpart")


def make_expm_tensors(nc, es, T):
    def sb(name, shape, dt):
        return es.enter_context(nc.sbuf_tensor(name, shape, dt))
    for n in ("Gb", "GTb", "G2b", "G3b", "G4b", "bufA", "bufB", "bufD", "Irow"):
        T[n] = sb(n, [128, 8 * 1024], BF16)
    T["Rout"] = nc.alloc_sbuf_tensor_at(
        "Rout", [128, 8 * 1024], BF16, offset=nc.lookup_mloc(T["G2b"]).addr)
    T["Gf32"] = nc.alloc_sbuf_tensor_at(
        "Gf32", [128, 8 * 1024], F32, offset=nc.lookup_mloc(T["G4b"]).addr)
    return T


def make_expm_psum(nc, es):
    return {"PB": [es.enter_context(nc.psum_tensor("PBe%d" % k, [128, 512], F32))
                   for k in range(2)]}



def build_program():
    nc = bass.Bass("TRN2", target_bir_lowering=False)
    T = {}
    T["d_x"] = nc.dram_tensor("x", [2048, 1024], F32, kind="ExternalInput")
    for w in ("Wq", "Wk", "Wv", "Wo"):
        T["d_" + w] = nc.dram_tensor(w, [1024, 1024], F32, kind="ExternalInput")
    T["d_bo"] = nc.dram_tensor("bo", [1024], F32, kind="ExternalInput")
    T["d_ln_gamma_t"] = nc.dram_tensor("ln_gamma_t", [128, 8], F32, kind="ExternalInput")
    T["d_ln_beta_t"] = nc.dram_tensor("ln_beta_t", [128, 8], F32, kind="ExternalInput")
    T["d_ln_gamma"] = nc.dram_tensor("ln_gamma", [1024], F32, kind="ExternalInput")
    T["d_ln_beta"] = nc.dram_tensor("ln_beta", [1024], F32, kind="ExternalInput")
    T["d_abel_w"] = nc.dram_tensor("abel_w", [1024, 1024], F32, kind="ExternalInput")
    T["d_abel_b_t"] = nc.dram_tensor("abel_b_t", [128, 8], F32, kind="ExternalInput")
    T["d_gate_w_t"] = nc.dram_tensor("gate_w_t", [128, 64], F32, kind="ExternalInput")
    T["d_gate_b"] = nc.dram_tensor("moe_gate_b", [8], F32, kind="ExternalInput")
    T["d_ew1"] = nc.dram_tensor("ew1", [8, 1024, 2048], F32, kind="ExternalInput")
    T["d_eb1_t"] = nc.dram_tensor("eb1_t", [128, 128], F32, kind="ExternalInput")
    T["d_ew2"] = nc.dram_tensor("ew2", [8, 2048, 1024], F32, kind="ExternalInput")
    T["d_eb2"] = nc.dram_tensor("eb2", [8, 1024], F32, kind="ExternalInput")
    T["d_gc1_t"] = nc.dram_tensor("gc1_t", [128, 8], F32, kind="ExternalInput")
    T["d_gc2_t"] = nc.dram_tensor("gc2_t", [128, 8], F32, kind="ExternalInput")
    T["d_gc_b"] = nc.dram_tensor("gc_b", [1], F32, kind="ExternalInput")
    T["d_mygen"] = nc.dram_tensor("mygen", [1024, 1024], F32, kind="ExternalInput")
    T["c_alibi"] = nc.dram_tensor("c_alibi", [128, 640], F32, kind="ExternalInput")
    T["c_mask"] = nc.dram_tensor("c_mask", [128, 2048], BF16, kind="ExternalInput")
    T["c_ident"] = nc.dram_tensor("c_ident", [128, 128], BF16, kind="ExternalInput")
    T["c_ones"] = nc.dram_tensor("c_ones", [128], F32, kind="ExternalInput")
    T["c_id8"] = nc.dram_tensor("c_id8", [8, 8], F32, kind="ExternalInput")
    T["c_ones_bf"] = nc.dram_tensor("c_ones_bf", [128], BF16, kind="ExternalInput")
    T["c_irow"] = nc.dram_tensor("c_irow", [128, 8192], BF16, kind="ExternalInput")
    T["d_c_hilo"] = nc.dram_tensor("c_hilo", [128, 32], BF16, kind="ExternalInput")
    T["d_c_spad"] = nc.dram_tensor("c_spad", [128, 2], F32, kind="ExternalInput")
    T["d_c_siota"] = nc.dram_tensor("c_siota", [128, 768], F32, kind="ExternalInput")
    T["c_ones_f"] = T["c_ones"]
    T["d_x2"] = nc.dram_tensor("x2_scr", [2048, 1024], F32, kind="Internal")
    T["d_xn"] = nc.dram_tensor("xn_scr", [2176, 1032], BF16, kind="Internal")
    T["d_moe"] = nc.dram_tensor("moe_scr", [2176, 1024], BF16, kind="Internal")
    T["d_abel"] = nc.dram_tensor("abel_scr", [128, 16384], BF16, kind="Internal")
    T["d_Rpart"] = nc.dram_tensor("Rpart_scr", [128, 8192], BF16, kind="Internal")
    T["d_Rall"] = nc.dram_tensor("Rall_scr", [8, 128, 8192], BF16, kind="Internal")
    T["d_out"] = nc.dram_tensor("out", [2048, 1024], F32, kind="ExternalOutput")

    sem_stack = ExitStack()
    SB = expm_sems(nc, sem_stack)
    SA = attn_sems(nc, sem_stack)
    SF = ffn_sems(nc, sem_stack)
    all_sems = list(SB.values()) + list(SA.values()) + list(SF.values())

    es0 = ExitStack()
    make_expm_tensors(nc, es0, T)
    P0 = make_expm_psum(nc, es0)
    with nc.Block() as block:
        for name in ENGINES:
            def body(eng, name=name):
                E = Emitter(nc, name, eng, all_sems)
                emit_expm(E, SB, T, P0)
            getattr(block, name)(body)
    es0.close()

    T["sem_cc"] = SB["b_cc"]
    es1 = ExitStack()
    make_attn_tensors(nc, es1, T)
    P1 = make_attn_psum(nc, es1)
    with nc.Block() as block:
        for name in ENGINES:
            def body(eng, name=name):
                E = Emitter(nc, name, eng, all_sems)
                emit_attention(E, SA, T, P1,
                               extra=lambda EE: emit_expm_allgather(EE, SB, T))
            getattr(block, name)(body)
    es1.close()

    es2 = ExitStack()
    make_ffn_tensors(nc, es2, T)
    P2 = make_ffn_psum(nc, es2)
    with nc.Block() as block:
        for name in ENGINES:
            def body(eng, name=name):
                E = Emitter(nc, name, eng, all_sems)
                emit_ffn(E, SF, T, P2)
            getattr(block, name)(body)
    es2.close()
    sem_stack.close()
    return nc


def _host_irow():
    import ml_dtypes
    ir = np.zeros((128, 8192), dtype=np.float32)
    p = np.arange(128)
    for dc in range(8):
        ir[p, dc * 1024 + dc * 128 + p] = 1.0
    return np.asarray(ir.astype(ml_dtypes.bfloat16))


def _host_hilo():
    import ml_dtypes
    p = np.arange(128)
    hilo = np.zeros((128, 32), dtype=np.float32)
    for tb in range(16):
        hilo[:, 2 * tb] = tb * 8 + (p >> 4) + 1
        hilo[:, 2 * tb + 1] = p & 15
    return np.asarray(hilo.astype(ml_dtypes.bfloat16))


_PROGRAM_CACHE = {}


def kernel(**inputs):
    import ml_dtypes
    x_full = np.ascontiguousarray(np.asarray(inputs["L0_attn_x"], dtype=np.float32))
    gens = np.ascontiguousarray(np.asarray(inputs["L0_ffn_gens"], dtype=np.float32))
    common = dict(
        Wq=np.ascontiguousarray(inputs["L0_attn_Wq"], dtype=np.float32),
        Wk=np.ascontiguousarray(inputs["L0_attn_Wk"], dtype=np.float32),
        Wv=np.ascontiguousarray(inputs["L0_attn_Wv"], dtype=np.float32),
        Wo=np.ascontiguousarray(inputs["L0_attn_Wo"], dtype=np.float32),
        bo=np.ascontiguousarray(inputs["L0_attn_bo"], dtype=np.float32),
        abel_w=np.ascontiguousarray(inputs["L0_ffn_abel_w"], dtype=np.float32),
        moe_gate_b=np.ascontiguousarray(inputs["L0_ffn_moe_gate_b"], dtype=np.float32),
        ew1=np.ascontiguousarray(inputs["L0_ffn_ew1"], dtype=np.float32),
        ew2=np.ascontiguousarray(inputs["L0_ffn_ew2"], dtype=np.float32),
        eb2=np.ascontiguousarray(inputs["L0_ffn_eb2"], dtype=np.float32),
        gc_b=np.ascontiguousarray(inputs["L0_ffn_gc_b"], dtype=np.float32),

        ln_gamma=np.ascontiguousarray(inputs["L0_ffn_ln_gamma"], dtype=np.float32),
        ln_beta=np.ascontiguousarray(inputs["L0_ffn_ln_beta"], dtype=np.float32),
        ln_gamma_t=np.ascontiguousarray(
            np.asarray(inputs["L0_ffn_ln_gamma"], dtype=np.float32).reshape(8, 128).T),
        ln_beta_t=np.ascontiguousarray(
            np.asarray(inputs["L0_ffn_ln_beta"], dtype=np.float32).reshape(8, 128).T),
        abel_b_t=np.ascontiguousarray(
            np.asarray(inputs["L0_ffn_abel_b"], dtype=np.float32).reshape(8, 128).T),
        gate_w_t=np.ascontiguousarray(
            np.asarray(inputs["L0_ffn_moe_gate_w"], dtype=np.float32)
            .reshape(8, 128, 8).transpose(1, 0, 2).reshape(128, 64)),
        eb1_t=np.ascontiguousarray(
            np.asarray(inputs["L0_ffn_eb1"], dtype=np.float32)
            .reshape(8, 2, 8, 128).transpose(3, 0, 1, 2).reshape(128, 128)),
        gc1_t=np.ascontiguousarray(
            np.asarray(inputs["L0_ffn_gc_w"], dtype=np.float32)[:1024, 0]
            .reshape(8, 128).T),
        gc2_t=np.ascontiguousarray(
            np.asarray(inputs["L0_ffn_gc_w"], dtype=np.float32)[1024:, 0]
            .reshape(8, 128).T),
        c_irow=_host_irow(),
        c_hilo=_host_hilo(),
        c_spad=np.stack([2064.0 + np.arange(128), 2048.0 + np.arange(128)],
                        axis=1).astype(np.float32),
        c_siota=np.broadcast_to(
            np.arange(768, dtype=np.float32), (128, 768)).copy(),
        c_id8=np.eye(8, dtype=np.float32),
        c_ones_bf=np.ones(128, dtype=np.float32).astype(ml_dtypes.bfloat16),
        **host_consts(),
    )
    in_maps = [dict(common, x=np.ascontiguousarray(x_full[b]),
                    mygen=np.ascontiguousarray(gens[b % 3]))
               for b in range(8)]
    if "nc" not in _PROGRAM_CACHE:
        _PROGRAM_CACHE["nc"] = build_program()
    nc = _PROGRAM_CACHE["nc"]
    res = run_bass_kernel_spmd(nc, in_maps, core_ids=list(range(8)))
    return np.stack([np.asarray(res.results[b]["out"]) for b in range(8)], axis=0)
